# version 01
# baseline (speedup 1.0000x reference)
import numpy as np
import ml_dtypes
import concourse.bass as bass
import concourse.mybir as mybir
from concourse.bass_utils import run_bass_kernel_spmd

F32 = mybir.dt.float32
BF16 = mybir.dt.bfloat16
I32 = mybir.dt.int32
AF = mybir.ActivationFunctionType
ALU = mybir.AluOpType
AX = mybir.AxisListType

NB = 4
S = 2048
D = 1024
H = 4096
NT = S // 128
DT = D // 128
HT = H // 128
CH = 256
NCH = S // CH
FC = 128
NFC = S // FC
EPS = 1e-5
BIGW = 2050
DBG = False


class Sem:
    def __init__(self, nc, name):
        self.h = nc.alloc_semaphore(name)
        self.n = 0

    def inc(self, instr, k=1):
        self.n += k
        instr.then_inc(self.h, k)
        return instr

    def dinc(self, instr):
        self.n += 16
        instr.then_inc(self.h, 16)
        return instr

    def wait(self, eng, val=None):
        eng.wait_ge(self.h, self.n if val is None else val)


def build_nc():
    nc = bass.Bass("TRN2", target_bir_lowering=False)

    x_h = nc.dram_tensor("x", [NB, S, D], F32, kind="ExternalInput").ap()
    Wq_h = nc.dram_tensor("Wq", [D, D], F32, kind="ExternalInput").ap()
    Wk_h = nc.dram_tensor("Wk", [D, D], F32, kind="ExternalInput").ap()
    Wv_h = nc.dram_tensor("Wv", [D, D], F32, kind="ExternalInput").ap()
    Wo_h = nc.dram_tensor("Wo", [D, D], F32, kind="ExternalInput").ap()
    bq_h = nc.dram_tensor("bq", [D], F32, kind="ExternalInput").ap()
    bk_h = nc.dram_tensor("bk", [D], F32, kind="ExternalInput").ap()
    bv_h = nc.dram_tensor("bv", [D], F32, kind="ExternalInput").ap()
    bo_h = nc.dram_tensor("bo", [D], F32, kind="ExternalInput").ap()
    w1_h = nc.dram_tensor("w1", [H, D], F32, kind="ExternalInput").ap()
    cw_h = nc.dram_tensor("cw", [H, 3], F32, kind="ExternalInput").ap()
    cb_h = nc.dram_tensor("cb", [H], F32, kind="ExternalInput").ap()
    w2_h = nc.dram_tensor("w2", [D, H], F32, kind="ExternalInput").ap()
    idf_h = nc.dram_tensor("idf", [128, 128], F32, kind="ExternalInput").ap()
    idb_h = nc.dram_tensor("idb", [128, 128], BF16, kind="ExternalInput").ap()

    y_h = nc.dram_tensor("y", [NB, S, D], F32, kind="ExternalOutput").ap()

    w1qT_h = nc.dram_tensor("w1qT_h", [128, DT * H], BF16, kind="Internal").ap()
    w2qT_h = nc.dram_tensor("w2qT_h", [128, HT * D], BF16, kind="Internal").ap()
    xsp_h = nc.dram_tensor("xsp_h", [S, D], F32, kind="Internal").ap()
    xq_h = nc.dram_tensor("xq_h", [S, D], BF16, kind="Internal").ap()

    BASE = 16512
    TOP = 229344

    def esz(dt_):
        return 4 if dt_ in (F32, I32) else 2

    cur = [BASE]

    def place(name, shape, dtype, at=None):
        nbytes = int(np.prod(shape[1:])) * esz(dtype)
        if at is None:
            at = (cur[0] + 31) & ~31
            cur[0] = at + nbytes
        assert at + nbytes <= TOP, f"SBUF overflow at {name}: {at + nbytes} > {TOP}"
        return nc.alloc_sbuf_tensor_at(name, list(shape), dtype, offset=at).ap(), at, at + nbytes

    def P(name, shape, dtype):
        return place(name, shape, dtype)[0]

    idf = P("idf_s", [128, 128], F32)
    idb = P("idb_s", [128, 128], BF16)
    onescol_b = P("onescol_b", [128, 1], BF16)
    onesrow_b = P("onesrow_b", [1, 256], BF16)
    onesrow_f = P("onesrow_f", [1, 128], F32)
    epscol = P("epscol", [128, 1], F32)
    colpk = P("colpk", [128, 24], F32)
    bqcol = P("bqcol", [128, DT], F32)
    bkcol = P("bkcol", [128, DT], F32)
    cwcol = P("cwcol", [128, 3 * HT], F32)
    cbcol = P("cbcol", [128, HT], F32)
    rowf = P("rowf", [1, 256], F32)
    rowb = P("rowb", [1, 256], BF16)
    c1col = P("c1col", [128, NT], F32)
    xrel = P("xrel", [128, D], F32)
    lnbuf, lnbuf_at, _ = place("lnbuf", [128, D], F32)
    q_i32 = place("q_i32", [128, D], I32, at=lnbuf_at)[0]
    sqb = P("sqb", [128, D], F32)
    ckb = P("ckb", [128, D], F32)

    ATT0 = (cur[0] + 63) & ~63
    a_cur = [ATT0]

    def PA(name, shape, dtype):
        at = (a_cur[0] + 31) & ~31
        ap, _, end = place(name, shape, dtype, at=at)
        a_cur[0] = end
        return ap

    qsqrow = PA("qsqrow", [1, S], BF16)
    ksqrow = PA("ksqrow", [1, S], BF16)
    sqb2a = PA("sqb2a", [128, 512], BF16)
    sqb2b = PA("sqb2b", [128, 512], BF16)
    lnta = PA("lnta", [128, D], BF16)
    lntb = PA("lntb", [128, D], BF16)
    rB = PA("rB", [128, CH], F32)
    sba = PA("sba", [128, CH], F32)
    sbb = PA("sbb", [128, CH], F32)
    eba = PA("eba", [128, CH], BF16)
    ebb = PA("ebb", [128, CH], BF16)
    xrel2 = PA("xrel2", [128, D], F32)
    bvrow = PA("bvrow", [1, D], BF16)
    borow = PA("borow", [1, D], BF16)

    f_cur = [ATT0]

    def PF(name, shape, dtype):
        at = (f_cur[0] + 31) & ~31
        ap, _, end = place(name, shape, dtype, at=at)
        f_cur[0] = end
        return ap

    YS = PF("YS", [128, HT * FC], F32)
    hqb = PF("hqb", [128, HT * FC], BF16)
    cva = PF("cva", [128, 128], F32)
    cvb = PF("cvb", [128, 128], F32)
    r2B = PF("r2B", [128, FC], F32)

    PB = (max(a_cur[0], f_cur[0]) + 63) & ~63
    kT = place("kT", [128, DT * S], BF16, at=PB)[0]
    vtok = place("vtok", [128, NT * D], BF16, at=PB + 32768)[0]
    w1qT = place("w1qT", [128, DT * H], BF16, at=PB)[0]
    BIGA = place("BIGA", [128, DT * BIGW], BF16, at=PB + 65536)[0]
    WB = (PB + 65536 + 2 * DT * BIGW + 63) & ~63
    Wbf = place("Wbf", [128, DT * D], BF16, at=WB)[0]
    outn = place("outn", [128, DT * CH], BF16, at=WB + 16384)[0]
    q1c = place("q1c", [128, D], F32, at=WB + 16384)[0]
    wstage = place("wstage", [128, D], F32, at=WB + 16384)[0]
    w2qT = place("w2qT", [128, HT * D], BF16, at=WB)[0]
    endF2 = WB + HT * D * 2
    assert endF2 <= TOP, f"arena overflow endF2={endF2} > {TOP}"
    wst = place("wst", [128, 4 * D], F32, at=PB)[0]
    wst2 = place("wst2", [128, 4 * D], F32, at=PB + 16384)[0]
    tbuf = place("tbuf", [128, 4 * D], BF16, at=PB + 32768)[0]
    stg = place("stg", [128, 4 * D], BF16, at=PB + 40960)[0]
    tern2 = place("tern2", [128, 4 * D], BF16, at=PB + 49152)[0]

    pbk = [nc.alloc_psum_tensor(f"pb{i}", [128, 512], F32).ap() for i in range(8)]
    nc.bump_psum(banks=-2)
    ptb = [nc.alloc_psum_tensor(f"ptb{i}", [128, 1024], BF16).ap() for i in range(2)]

    s_ld = Sem(nc, "s_ld")
    s_sp = Sem(nc, "s_sp")
    s_out = Sem(nc, "s_out")
    s_pe = Sem(nc, "s_pe")
    s_v = Sem(nc, "s_v")
    s_s = Sem(nc, "s_s")
    s_g = Sem(nc, "s_g")

    (C_M1, C_M2, C_T1, C_NT1, C_T2, C_NT2, C_MEAN, C_VAR, C_INV, C_NMI,
     C_AMX, C_R127, C_C2, C_VE, C_Y, C_TA, C_TB, C_SUM, C_M127_2, C_M127_1,
     C_DMY) = range(21)

    def col(i):
        return colpk[:, i:i + 1]

    with nc.Block() as blk:
        def vgap(v, n=1):
            for _ in range(n):
                v.tensor_single_scalar(col(C_DMY), epscol, 1.0, ALU.mult)

        def tap(name, src_ap):
            if not DBG:
                return
            h = nc.dram_tensor(name, list(src_ap.shape), src_ap.dtype, kind="ExternalOutput").ap()
            @blk.gpsimd
            def _(g):
                s_v.wait(g)
                s_s.wait(g)
                s_pe.wait(g)
                s_out.dinc(g.dma_start(h, src_ap))

        @blk.sync
        def _(sync):
            s_ld.dinc(sync.dma_start(idf, idf_h))
            s_ld.dinc(sync.dma_start(idb, idb_h))
            s_ld.dinc(sync.dma_start(bqcol, bq_h.rearrange("(t p) -> p t", p=128), allow_slow_non_contiguous=True))
            s_ld.dinc(sync.dma_start(bkcol, bk_h.rearrange("(t p) -> p t", p=128), allow_slow_non_contiguous=True))
            s_ld.dinc(sync.dma_start(cwcol.rearrange("p (t k) -> p t k", k=3), cw_h.rearrange("(t p) k -> p t k", p=128), allow_slow_non_contiguous=True))
            s_ld.dinc(sync.dma_start(cbcol, cb_h.rearrange("(t p) -> p t", p=128), allow_slow_non_contiguous=True))

        @blk.vector
        def _(v):
            s_ld.wait(v)
            v.memset(onescol_b, 1.0)
            v.memset(onesrow_b, 1.0)
            v.memset(onesrow_f, 1.0)
            s_v.inc(v.memset(epscol, EPS))

        for wi in range(2):
            w_src = w1_h if wi == 0 else w2_h
            nrow = H if wi == 0 else D
            NCHW = 8
            mcol_i = C_M1 if wi == 0 else C_M2
            tcol_i = C_T1 if wi == 0 else C_T2
            ntcol_i = C_NT1 if wi == 0 else C_NT2
            wview = w_src.rearrange("(c q p) d -> c p q d", c=NCHW, p=128)
            for ch in range(NCHW):
                buf = wst if ch % 2 == 0 else wst2
                @blk.sync
                def _(sync):
                    s_v.wait(sync)
                    s_ld.dinc(sync.dma_start(buf.rearrange("p (q d) -> p q d", d=D), wview[ch]))
                @blk.vector
                def _(v, ch=ch, buf=buf):
                    s_ld.wait(v)
                    if ch == 0:
                        s_v.inc(v.reduce_sum(col(C_TA), buf, AX.X, apply_absolute_value=True))
                    else:
                        v.reduce_sum(col(C_TB), buf, AX.X, apply_absolute_value=True)
                        vgap(v)
                        vgap(v)
                        s_v.inc(v.tensor_add(col(C_TA), col(C_TA), col(C_TB)))
            @blk.tensor
            def _(te):
                s_v.wait(te)
                s_pe.inc(te.matmul(pbk[6][0:1, 0:128], col(C_TA), idf, start=True, stop=True,
                                   skip_group_check=True))
            @blk.scalar
            def _(s):
                s_pe.wait(s)
                s_s.inc(s.copy(rowf[0:1, 0:128], pbk[6][0:1, 0:128]))
            @blk.vector
            def _(v):
                s_s.wait(v)
                s_v.inc(v.reduce_sum(rowf[0:1, 128:129], rowf[0:1, 0:128], AX.X))
            @blk.tensor
            def _(te):
                s_v.wait(te)
                s_pe.inc(te.matmul(pbk[6][:, 128:129], onesrow_f, rowf[0:1, 128:129],
                                   start=True, stop=True, skip_group_check=True))
            @blk.vector
            def _(v, nrow=nrow, mcol_i=mcol_i, tcol_i=tcol_i, ntcol_i=ntcol_i, wi=wi):
                s_pe.wait(v)
                v.tensor_single_scalar(col(mcol_i), pbk[6][:, 128:129],
                                       1.0 / (nrow * (H * D // nrow)), ALU.mult)
                vgap(v)
                vgap(v)
                v.tensor_single_scalar(col(mcol_i), col(mcol_i), EPS, ALU.max)
                vgap(v)
                vgap(v)
                v.tensor_single_scalar(col(tcol_i), col(mcol_i), 0.5, ALU.mult)
                vgap(v)
                vgap(v)
                v.tensor_single_scalar(col(ntcol_i), col(tcol_i), -1.0, ALU.mult)
                s_v.inc(v.tensor_single_scalar(col(C_M127_2 if wi == 1 else C_M127_1),
                                                col(mcol_i), 1.0 / 127.0, ALU.mult))
            for ch in range(NCHW):
                buf = wst if ch % 2 == 0 else wst2
                @blk.sync
                def _(sync):
                    s_v.wait(sync)
                    s_ld.dinc(sync.dma_start(buf.rearrange("p (q d) -> p q d", d=D), wview[ch]))
                @blk.vector
                def _(v, ch=ch, buf=buf, tcol_i=tcol_i, ntcol_i=ntcol_i, wi=wi):
                    s_ld.wait(v)
                    if ch > 0 or wi > 0:
                        s_sp.wait(v)
                    v.tensor_scalar(tbuf, buf, col(tcol_i), None, ALU.is_ge)
                    v.tensor_scalar(tern2, buf, col(ntcol_i), None, ALU.is_le)
                    vgap(v)
                    s_v.inc(v.tensor_sub(tbuf, tbuf, tern2))
                for q in range(4):
                    for dt in range(DT):
                        bank = ptb[1][:, (dt % 4) * 128:(dt % 4) * 128 + 128]
                        @blk.tensor
                        def _(te, q=q, dt=dt, bank=bank):
                            s_v.wait(te)
                            s_pe.inc(te.transpose(bank, tbuf[:, q * D + dt * 128: q * D + dt * 128 + 128], idb))
                        @blk.vector
                        def _(v, q=q, dt=dt, bank=bank, wi=wi):
                            s_pe.wait(v)
                            if wi == 0:
                                dst = stg[:, dt * 512 + q * 128: dt * 512 + q * 128 + 128]
                            else:
                                dst = stg[:, (q * DT + dt) * 128: (q * DT + dt) * 128 + 128]
                            s_v.inc(v.tensor_copy(dst, bank))
                @blk.gpsimd
                def _(g, ch=ch, wi=wi):
                    s_v.wait(g)
                    if wi == 0:
                        dst = w1qT_h.rearrange("p (dt u) -> p dt u", dt=DT)[:, :, ch * 512:(ch + 1) * 512]
                        s_sp.dinc(g.dma_start(dst, stg.rearrange("p (dt u) -> p dt u", dt=DT)))
                    else:
                        dst = w2qT_h.rearrange("p (ht dd) -> p ht dd", ht=HT)[:, :, ch * 128:(ch + 1) * 128]
                        s_sp.dinc(g.dma_start(dst, stg.rearrange("p (ht j) -> p ht j", ht=HT)))

        def emit_var(v, src):
            v.reduce_sum(col(C_MEAN), src, AX.X)
            v.reduce_sum(col(C_VAR), sqb, AX.X)
            v.tensor_single_scalar(col(C_MEAN), col(C_MEAN), 1.0 / D, ALU.mult)
            v.tensor_single_scalar(col(C_VAR), col(C_VAR), 1.0 / D, ALU.mult)
            v.tensor_mul(col(C_SUM), col(C_MEAN), col(C_MEAN))
            vgap(v, 2)
            return v.tensor_sub(col(C_VAR), col(C_VAR), col(C_SUM))

        def emit_stats(src):
            @blk.scalar
            def _(s):
                s_v.wait(s)
                s_s.inc(s.activation(sqb, src, AF.Square))
            @blk.vector
            def _(v):
                s_s.wait(v)
                s_v.inc(emit_var(v, src))
            @blk.scalar
            def _(s):
                s_v.wait(s)
                s.activation(col(C_INV), col(C_VAR), AF.Ln, bias=epscol, scale=1.0)
                s_s.inc(s.activation(col(C_INV), col(C_INV), AF.Exp, bias=0.0, scale=-0.5))
            @blk.vector
            def _(v):
                s_s.wait(v)
                s_v.inc(v.scalar_tensor_tensor(col(C_NMI), col(C_MEAN), -1.0, col(C_INV),
                                               ALU.mult, ALU.mult))

        tap("d_m", colpk)

        def batch(b):
            xr = x_h[b].rearrange("(t p) d -> t p d", p=128)

            @blk.sync
            def _(sync):
                s_s.wait(sync)
                s_ld.dinc(sync.dma_start(ckb[0:1, 0:D], bv_h.rearrange("d -> () d")))
                s_ld.dinc(sync.dma_start(sqb[0:1, 0:D], bo_h.rearrange("d -> () d")))
            @blk.vector
            def _(v):
                s_pe.wait(v)
                s_ld.wait(v)
                for dt in range(DT):
                    v.memset(BIGA[:, dt * BIGW: dt * BIGW + 1], 0.0)
                v.tensor_copy(bvrow, ckb[0:1, 0:D])
                v.tensor_copy(borow, sqb[0:1, 0:D])
                s_v.inc(v.memset(BIGA[:, (DT - 1) * BIGW + 2049: (DT - 1) * BIGW + 2050], 0.0))
            @blk.vector
            def _(v):
                for dt in range(DT - 1):
                    v.memset(BIGA[:, dt * BIGW + 2049: dt * BIGW + 2050], 0.0)
                s_v.inc(v.tensor_single_scalar(col(C_DMY), epscol, 1.0, ALU.mult))

            for t in range(NT):
                xb = xrel if t % 2 == 0 else xrel2
                lb = lnta if t % 2 == 0 else lntb
                @blk.sync
                def _(sync, t=t, xb=xb):
                    s_s.wait(sync)
                    s_v.wait(sync)
                    s_ld.dinc(sync.dma_start(xb, xr[t]))
                @blk.scalar
                def _(s, xb=xb):
                    s_ld.wait(s)
                    s_s.inc(s.activation(sqb, xb, AF.Square))
                @blk.vector
                def _(v, xb=xb, t=t):
                    s_s.wait(v)
                    emit_var(v, xb)
                    if DBG and b == 0:
                        vgap(v)
                        v.tensor_copy(c1col[:, t:t + 1], col(C_VAR))
                    s_v.inc(v.tensor_single_scalar(col(C_DMY), epscol, 1.0, ALU.mult))
                @blk.scalar
                def _(s):
                    s_v.wait(s)
                    s.activation(col(C_INV), col(C_VAR), AF.Ln, bias=epscol, scale=1.0)
                    s_s.inc(s.activation(col(C_INV), col(C_INV), AF.Exp, bias=0.0, scale=-0.5))
                @blk.vector
                def _(v):
                    s_s.wait(v)
                    s_v.inc(v.scalar_tensor_tensor(col(C_NMI), col(C_MEAN), -1.0, col(C_INV),
                                                   ALU.mult, ALU.mult))
                @blk.scalar
                def _(s, xb=xb, lb=lb):
                    s_v.wait(s)
                    s_s.inc(s.activation(lb, xb, AF.Identity, bias=col(C_NMI), scale=col(C_INV)))
                for dt in range(DT):
                    bank = ptb[dt % 2][:, 0:128]
                    @blk.tensor
                    def _(te, lb=lb, dt=dt, bank=bank):
                        s_s.wait(te)
                        s_v.wait(te)
                        s_pe.inc(te.transpose(bank, lb[:, dt * 128: dt * 128 + 128], idb))
                    @blk.vector
                    def _(v, t=t, dt=dt, bank=bank):
                        s_pe.wait(v)
                        s_v.inc(v.tensor_copy(
                            BIGA[:, dt * BIGW + 1 + t * 128: dt * BIGW + 1 + t * 128 + 128], bank))

            if b == 0:
                tap("d_biga", BIGA)
                tap("d_var", c1col)

            def load_weight(w_src):
                for dt in range(DT):
                    @blk.sync
                    def _(sync, dt=dt, w_src=w_src):
                        s_v.wait(sync)
                        s_pe.wait(sync)
                        s_ld.dinc(sync.dma_start(wstage, w_src[dt * 128:(dt + 1) * 128, :]))
                    @blk.vector
                    def _(v, dt=dt):
                        s_ld.wait(v)
                        s_v.inc(v.tensor_copy(Wbf[:, dt * D:(dt + 1) * D], wstage))

            load_weight(Wv_h)
            for mt in range(NT):
                for ncv in range(2):
                    bank = pbk[(mt * 2 + ncv) % 4][:, 0:512]
                    @blk.tensor
                    def _(te, mt=mt, ncv=ncv, bank=bank):
                        if mt == 0 and ncv == 0:
                            s_v.wait(te)
                        s_s.wait(te)
                        for dt in range(DT):
                            te.matmul(bank, BIGA[:, dt * BIGW + 1 + mt * 128: dt * BIGW + 1 + mt * 128 + 128],
                                      Wbf[:, dt * D + ncv * 512: dt * D + ncv * 512 + 512],
                                      start=(dt == 0), stop=False, skip_group_check=True)
                        s_pe.inc(te.matmul(bank, onesrow_b[0:1, 0:128], bvrow[:, ncv * 512: ncv * 512 + 512],
                                           start=False, stop=True, skip_group_check=True))
                    @blk.scalar
                    def _(s, mt=mt, ncv=ncv, bank=bank):
                        s_pe.wait(s)
                        s_s.inc(s.copy(vtok[:, mt * D + ncv * 512: mt * D + ncv * 512 + 512], bank))

            load_weight(Wk_h)
            for mt in range(DT):
                for ncv in range(4):
                    bank = pbk[(mt * 4 + ncv) % 4][:, 0:512]
                    @blk.tensor
                    def _(te, mt=mt, ncv=ncv, bank=bank):
                        if mt == 0 and ncv == 0:
                            s_v.wait(te)
                        s_s.wait(te)
                        ii = None
                        for dt in range(DT):
                            ii = te.matmul(bank, Wbf[:, dt * D + mt * 128: dt * D + mt * 128 + 128],
                                           BIGA[:, dt * BIGW + 1 + ncv * 512: dt * BIGW + 1 + ncv * 512 + 512],
                                           start=(dt == 0), stop=(dt == DT - 1), skip_group_check=True)
                        s_pe.inc(ii)
                    @blk.scalar
                    def _(s, mt=mt, ncv=ncv, bank=bank):
                        s_pe.wait(s)
                        s_s.inc(s.activation(kT[:, mt * S + ncv * 512: mt * S + ncv * 512 + 512], bank,
                                             AF.Identity, bias=bkcol[:, mt:mt + 1], scale=1.0))
            for ncv in range(4):
                bank = pbk[4][0:1, 0:512]
                for dt in range(DT):
                    sq = sqb2a if dt % 2 == 0 else sqb2b
                    @blk.scalar
                    def _(s, ncv=ncv, dt=dt, sq=sq):
                        s_pe.wait(s)
                        s_s.inc(s.activation(sq, kT[:, dt * S + ncv * 512: dt * S + ncv * 512 + 512],
                                             AF.Square))
                    @blk.tensor
                    def _(te, dt=dt, sq=sq, bank=bank):
                        s_s.wait(te)
                        s_pe.inc(te.matmul(bank, onescol_b, sq, start=(dt == 0), stop=(dt == DT - 1),
                                           skip_group_check=True))
                @blk.scalar
                def _(s, ncv=ncv, bank=bank):
                    s_pe.wait(s)
                    s_s.inc(s.activation(ksqrow[0:1, ncv * 512:(ncv + 1) * 512], bank, AF.Identity,
                                         bias=0.0, scale=-0.5))

            load_weight(Wq_h)
            for ncv in range(4):
                @blk.tensor
                def _(te, ncv=ncv):
                    if ncv == 0:
                        s_v.wait(te)
                    s_s.wait(te)
                    ii = None
                    for mt in range(DT):
                        for dt in range(DT):
                            ii = te.matmul(pbk[mt][:, 0:512],
                                           Wbf[:, dt * D + mt * 128: dt * D + mt * 128 + 128],
                                           BIGA[:, dt * BIGW + 1 + ncv * 512: dt * BIGW + 1 + ncv * 512 + 512],
                                           start=(dt == 0), stop=(dt == DT - 1), skip_group_check=True)
                    s_pe.inc(ii)
                @blk.scalar
                def _(s, ncv=ncv):
                    s_pe.wait(s)
                    ii = None
                    for mt in range(DT):
                        ii = s.activation(BIGA[:, mt * BIGW + 1 + ncv * 512: mt * BIGW + 1 + ncv * 512 + 512],
                                          pbk[mt][:, 0:512], AF.Identity, bias=bqcol[:, mt:mt + 1], scale=1.0)
                    s_s.inc(ii)
            for ncv in range(4):
                bank = pbk[4][0:1, 0:512]
                for dt in range(DT):
                    sq = sqb2a if dt % 2 == 0 else sqb2b
                    @blk.scalar
                    def _(s, ncv=ncv, dt=dt, sq=sq):
                        s_pe.wait(s)
                        s_s.inc(s.activation(sq,
                                             BIGA[:, dt * BIGW + 1 + ncv * 512: dt * BIGW + 1 + ncv * 512 + 512],
                                             AF.Square))
                    @blk.tensor
                    def _(te, dt=dt, sq=sq, bank=bank):
                        s_s.wait(te)
                        s_pe.inc(te.matmul(bank, onescol_b, sq, start=(dt == 0), stop=(dt == DT - 1),
                                           skip_group_check=True))
                @blk.scalar
                def _(s, ncv=ncv, bank=bank):
                    s_pe.wait(s)
                    s_s.inc(s.activation(qsqrow[0:1, ncv * 512:(ncv + 1) * 512], bank, AF.Identity,
                                         bias=0.0, scale=-0.5))

            load_weight(Wo_h)
            if b == 0:
                tap("d_kt", kT)
                tap("d_vt", vtok)
                tap("d_q", BIGA)
                tap("d_qsq", qsqrow)
                tap("d_ksq", ksqrow)
                tap("d_wbf", Wbf)

            for c in range(NCH):
                opsum = [pbk[2 + i // 2][:, (i % 2) * CH:(i % 2) * CH + CH] for i in range(DT)]
                dn = pbk[6][0:1, 0:CH]
                for jt in range(NT):
                    qk = pbk[jt % 2][:, 0:CH]
                    sb_ = sba if jt % 2 == 0 else sbb
                    eb_ = eba if jt % 2 == 0 else ebb
                    @blk.tensor
                    def _(te, c=c, jt=jt, qk=qk):
                        if c > 0 or jt >= 2:
                            s_s.wait(te)
                        for dt in range(DT):
                            te.matmul(qk, kT[:, dt * S + jt * 128: dt * S + jt * 128 + 128],
                                      BIGA[:, dt * BIGW + 1 + c * CH: dt * BIGW + 1 + c * CH + CH],
                                      start=(dt == 0), stop=False, skip_group_check=True)
                        te.matmul(qk, onesrow_b[0:1, 0:128], qsqrow[0:1, c * CH: c * CH + CH],
                                  start=False, stop=False, skip_group_check=True)
                        s_pe.inc(te.matmul(qk, ksqrow[0:1, jt * 128: jt * 128 + 128],
                                           onesrow_b[0:1, 0:CH],
                                           start=False, stop=True, skip_group_check=True))
                    @blk.scalar
                    def _(s, qk=qk, sb_=sb_, eb_=eb_):
                        s_pe.wait(s)
                        s.activation(sb_, qk, AF.Exp, bias=0.0, scale=2.0)
                        s_s.inc(s.activation(eb_, sb_, AF.Exp, bias=0.0, scale=1.0))
                    @blk.tensor
                    def _(te, jt=jt, eb_=eb_):
                        s_s.wait(te)
                        for i in range(DT):
                            te.matmul(opsum[i], vtok[:, jt * D + i * 128: jt * D + i * 128 + 128], eb_,
                                      start=(jt == 0 and i % 2 == 0), stop=(jt == NT - 1),
                                      skip_group_check=True)
                        s_pe.inc(te.matmul(dn, onescol_b, eb_, start=(jt == 0), stop=(jt == NT - 1),
                                           skip_group_check=True))
                if b == 0 and c == 0:
                    tap("d_sb", sbb)
                    tap("d_eb", ebb)
                @blk.vector
                def _(v):
                    s_pe.wait(v)
                    v.reciprocal(rowf[0:1, 0:CH], dn)
                    vgap(v)
                    vgap(v)
                    s_v.inc(v.tensor_copy(rowb[0:1, 0:CH], rowf[0:1, 0:CH]))
                @blk.tensor
                def _(te):
                    s_v.wait(te)
                    s_pe.inc(te.matmul(pbk[6][:, CH:CH + CH], onesrow_b[0:1, 0:128], rowb[0:1, 0:CH],
                                       start=True, stop=True, skip_group_check=True))
                @blk.vector
                def _(v):
                    s_pe.wait(v)
                    v.tensor_copy(rB, pbk[6][:, CH:CH + CH])
                    vgap(v)
                    vgap(v)
                    ii = None
                    for i in range(DT):
                        ii = v.tensor_mul(outn[:, i * CH:(i + 1) * CH], opsum[i], rB)
                    s_v.inc(ii)
                if b == 0 and c == 0:
                    tap("d_outn", outn)
                    tap("d_rb", rB)
                    tap("d_rowf", rowf)
                for imt in range(2):
                    aop = [pbk[2 + 2 * imt + ncv][:, 0:512] for ncv in range(2)]
                    @blk.tensor
                    def _(te, imt=imt, aop=aop):
                        s_v.wait(te)
                        ii = None
                        for ncv in range(2):
                            for i in range(DT):
                                te.matmul(aop[ncv], outn[:, i * CH + imt * 128: i * CH + imt * 128 + 128],
                                          Wbf[:, i * D + ncv * 512: i * D + ncv * 512 + 512],
                                          start=(i == 0), stop=False, skip_group_check=True)
                            ii = te.matmul(aop[ncv], onesrow_b[0:1, 0:128], borow[:, ncv * 512: ncv * 512 + 512],
                                           start=False, stop=True, skip_group_check=True)
                        s_pe.inc(ii)
                for imt in range(2):
                    aop = [pbk[2 + 2 * imt + ncv][:, 0:512] for ncv in range(2)]
                    @blk.scalar
                    def _(s, aop=aop):
                        s_pe.wait(s)
                        s_v.wait(s)
                        s.copy(ckb[:, 0:512], aop[0])
                        s_s.inc(s.copy(ckb[:, 512:1024], aop[1]))
                    t_glob = c * 2 + imt
                    xb = xrel if t_glob % 2 == 0 else xrel2
                    @blk.sync
                    def _(sync, t_glob=t_glob, xb=xb):
                        s_sp.wait(sync)
                        s_v.wait(sync)
                        s_ld.dinc(sync.dma_start(xb, xr[t_glob]))
                    emit_stats(ckb)
                    @blk.scalar
                    def _(s):
                        s_v.wait(s)
                        s_s.inc(s.activation(lnbuf, ckb, AF.Identity, bias=col(C_NMI), scale=col(C_INV)))
                    @blk.vector
                    def _(v, xb=xb):
                        s_s.wait(v)
                        s_ld.wait(v)
                        s_v.inc(v.tensor_add(xb, xb, lnbuf))
                    if b == 0 and c == 0:
                        tap("d_ao" + str(imt), ckb)
                        tap("d_xnew" + str(imt), xb)
                    @blk.gpsimd
                    def _(g, t_glob=t_glob, xb=xb):
                        s_v.wait(g)
                        s_sp.dinc(g.dma_start(xsp_h.rearrange("(t p) d -> t p d", p=128)[t_glob], xb))
                    emit_stats(xb)
                    @blk.scalar
                    def _(s, xb=xb):
                        s_v.wait(s)
                        s_s.inc(s.activation(lnbuf, xb, AF.Identity, bias=col(C_NMI), scale=col(C_INV)))
                    @blk.vector
                    def _(v, t_glob=t_glob):
                        s_s.wait(v)
                        v.reduce_max(col(C_AMX), lnbuf, AX.X, apply_absolute_value=True)
                        vgap(v)
                        vgap(v)
                        v.tensor_single_scalar(col(C_AMX), col(C_AMX), EPS, ALU.max)
                        vgap(v)
                        vgap(v)
                        v.reciprocal(col(C_C2), col(C_AMX))
                        v.tensor_mul(c1col[:, t_glob:t_glob + 1], col(C_AMX), col(C_M127_1))
                        vgap(v)
                        v.tensor_single_scalar(col(C_R127), col(C_C2), 127.0, ALU.mult)
                        vgap(v, 2)
                        v.tensor_scalar(sqb, lnbuf, col(C_R127), 128.0, ALU.mult, ALU.add)
                        vgap(v)
                        v.tensor_copy(q_i32, sqb)
                        vgap(v)
                        v.tensor_copy(q1c, q_i32)
                        vgap(v)
                        v.scalar_tensor_tensor(ckb, sqb, 0.5, q1c, ALU.add, ALU.subtract)
                        vgap(v)
                        v.tensor_single_scalar(ckb, ckb, 1.0, ALU.is_ge)
                        vgap(v)
                        v.scalar_tensor_tensor(q1c, q1c, -128.0, ckb, ALU.add, ALU.add)
                        vgap(v)
                        s_v.inc(v.tensor_scalar(lnta if t_glob % 2 == 0 else lntb, q1c,
                                                c1col[:, t_glob:t_glob + 1], None, ALU.mult))
                    @blk.gpsimd
                    def _(g, t_glob=t_glob):
                        s_v.wait(g)
                        s_sp.dinc(g.dma_start(xq_h.rearrange("(t p) d -> t p d", p=128)[t_glob],
                                             lnta if t_glob % 2 == 0 else lntb))

            @blk.sync
            def _(sync):
                s_pe.wait(sync)
                s_sp.wait(sync)
                s_s.wait(sync)
                s_ld.dinc(sync.dma_start(w1qT, w1qT_h))
                s_ld.dinc(sync.dma_start(w2qT, w2qT_h))
            for t in range(NT):
                lb = lnta if t % 2 == 0 else lntb
                @blk.sync
                def _(sync, t=t, lb=lb):
                    s_v.wait(sync)
                    s_ld.dinc(sync.dma_start(lb, xq_h.rearrange("(t p) d -> t p d", p=128)[t]))
                for dt in range(DT):
                    bank = ptb[dt % 2][:, 0:128]
                    @blk.tensor
                    def _(te, dt=dt, lb=lb, bank=bank):
                        s_ld.wait(te)
                        s_v.wait(te)
                        s_pe.inc(te.transpose(bank, lb[:, dt * 128: dt * 128 + 128], idb))
                    @blk.vector
                    def _(v, t=t, dt=dt, bank=bank):
                        s_pe.wait(v)
                        s_v.inc(v.tensor_copy(
                            BIGA[:, dt * BIGW + 1 + t * 128: dt * BIGW + 1 + t * 128 + 128], bank))

            if b == 0:
                tap("d_xqt", BIGA)
                tap("d_w1", w1qT)
                tap("d_w2", w2qT)

            for c in range(NFC):
                for ht in range(HT):
                    hb = pbk[ht % 4][:, 0:130]
                    @blk.tensor
                    def _(te, c=c, ht=ht, hb=hb):
                        if c == 0 and ht == 0:
                            s_ld.wait(te)
                            s_v.wait(te)
                        s_v.wait(te)
                        ii = None
                        for dt in range(DT):
                            ii = te.matmul(hb, w1qT[:, dt * H + ht * 128: dt * H + ht * 128 + 128],
                                           BIGA[:, dt * BIGW + c * FC: dt * BIGW + c * FC + 130],
                                           start=(dt == 0), stop=(dt == DT - 1), skip_group_check=True)
                        s_pe.inc(ii)
                    @blk.scalar
                    def _(s, ht=ht, hb=hb):
                        s_pe.wait(s)
                        s_s.inc(s.activation(cva, hb[:, 0:128], AF.Identity, bias=0.0,
                                             scale=cwcol[:, ht * 3: ht * 3 + 1]))
                    @blk.vector
                    def _(v, ht=ht, hb=hb):
                        s_s.wait(v)
                        v.scalar_tensor_tensor(cvb, hb[:, 1:129], cwcol[:, ht * 3 + 1: ht * 3 + 2], cva,
                                               ALU.mult, ALU.add)
                        vgap(v)
                        s_v.inc(v.scalar_tensor_tensor(cva, hb[:, 2:130], cwcol[:, ht * 3 + 2: ht * 3 + 3],
                                                       cvb, ALU.mult, ALU.add))
                    @blk.scalar
                    def _(s, ht=ht):
                        s_v.wait(s)
                        s_s.inc(s.activation(YS[:, ht * FC:(ht + 1) * FC], cva, AF.Silu,
                                             bias=cbcol[:, ht:ht + 1], scale=1.0))
                @blk.vector
                def _(v):
                    s_s.wait(v)
                    s_v.inc(v.reduce_max(cva, YS.rearrange("p (ht i) -> p i ht", ht=HT), AX.X,
                                         apply_absolute_value=True))
                @blk.tensor
                def _(te):
                    s_v.wait(te)
                    s_pe.inc(te.transpose(pbk[6][:, 0:128], cva, idf))
                @blk.vector
                def _(v):
                    s_pe.wait(v)
                    v.reduce_max(col(C_AMX), pbk[6][:, 0:128], AX.X)
                    vgap(v)
                    vgap(v)
                    v.tensor_single_scalar(col(C_AMX), col(C_AMX), EPS, ALU.max)
                    vgap(v)
                    vgap(v)
                    v.reciprocal(col(C_R127), col(C_AMX))
                    v.tensor_mul(col(C_C2), col(C_AMX), col(C_M127_2))
                    vgap(v)
                    s_v.inc(v.tensor_single_scalar(col(C_R127), col(C_R127), 127.0, ALU.mult))
                @blk.tensor
                def _(te):
                    s_v.wait(te)
                    s_pe.inc(te.matmul(pbk[6][0:1, 128:256], col(C_R127), idf, start=True, stop=True,
                                       skip_group_check=True))
                @blk.scalar
                def _(s):
                    s_pe.wait(s)
                    s_s.inc(s.copy(rowf[0:1, 0:FC], pbk[6][0:1, 128:256]))
                @blk.tensor
                def _(te):
                    s_s.wait(te)
                    s_pe.inc(te.matmul(pbk[6][:, 256:384], onesrow_f, rowf[0:1, 0:FC],
                                       start=True, stop=True, skip_group_check=True))
                @blk.vector
                def _(v):
                    s_pe.wait(v)
                    s_v.inc(v.tensor_copy(r2B, pbk[6][:, 256:384]))
                for qq in range(4):
                    sl = slice(qq * 8 * FC, (qq + 1) * 8 * FC)
                    @blk.vector
                    def _(v, sl=sl, c=c, qq=qq):
                        s_v.wait(v)
                        if c > 0 and qq == 0:
                            s_pe.wait(v)
                        ysq = YS[:, sl]
                        vgap(v)
                        v.tensor_tensor(ysq.rearrange("p (a i) -> p a i", a=8),
                                        ysq.rearrange("p (a i) -> p a i", a=8),
                                        r2B.rearrange("p i -> p () i").broadcast_to([128, 8, FC]),
                                        ALU.mult)
                        vgap(v)
                        v.tensor_single_scalar(ysq, ysq, 128.0, ALU.add)
                        vgap(v)
                        v.tensor_copy(q_i32, ysq)
                        vgap(v)
                        v.tensor_copy(sqb, q_i32)
                        vgap(v)
                        v.scalar_tensor_tensor(ysq, ysq, 0.5, sqb, ALU.add, ALU.subtract)
                        vgap(v)
                        v.tensor_single_scalar(ysq, ysq, 1.0, ALU.is_ge)
                        vgap(v)
                        s_v.inc(v.scalar_tensor_tensor(hqb[:, sl], sqb, -128.0, ysq, ALU.add, ALU.add))
                if b == 0 and c == 0:
                    tap("d_ys", YS)
                    tap("d_hq", hqb)
                    tap("d_c1", c1col)
                for ncv in range(2):
                    bank = pbk[4 + ncv][:, 0:512]
                    @blk.tensor
                    def _(te, c=c, ncv=ncv, bank=bank):
                        s_v.wait(te)
                        s_s.wait(te)
                        ii = None
                        for ht in range(HT):
                            ii = te.matmul(bank, hqb[:, ht * FC: ht * FC + 128],
                                           w2qT[:, ht * D + ncv * 512: ht * D + ncv * 512 + 512],
                                           start=(ht == 0), stop=(ht == HT - 1), skip_group_check=True)
                        s_pe.inc(ii)
                    @blk.scalar
                    def _(s, ncv=ncv, bank=bank):
                        s_pe.wait(s)
                        s_s.inc(s.activation(ckb[:, ncv * 512:(ncv + 1) * 512], bank, AF.Identity,
                                             bias=0.0, scale=col(C_C2)))
                @blk.sync
                def _(sync, c=c):
                    s_v.wait(sync)
                    s_out.wait(sync, s_out.n)
                    s_ld.dinc(sync.dma_start(xrel, xsp_h.rearrange("(t p) d -> t p d", p=128)[c]))
                @blk.scalar
                def _(s):
                    s_s.inc(s.activation(sqb, ckb, AF.Square))
                @blk.vector
                def _(v):
                    s_s.wait(v)
                    emit_var(v, ckb)
                    vgap(v)
                    vgap(v)
                    s_v.inc(v.tensor_single_scalar(col(C_VE), col(C_VAR), EPS, ALU.add))
                @blk.gpsimd
                def _(g):
                    s_v.wait(g)
                    g.tensor_single_scalar(col(C_Y), col(C_VE), 2.0, ALU.is_lt)
                    for lo, hi, val in ((2.0, 8.0, 0.5), (8.0, 128.0, 0.125),
                                        (128.0, 2048.0, 0.03125)):
                        g.tensor_single_scalar(col(C_TA), col(C_VE), lo, ALU.is_ge)
                        g.tensor_single_scalar(col(C_TB), col(C_VE), hi, ALU.is_lt)
                        g.tensor_mul(col(C_TA), col(C_TA), col(C_TB))
                        g.tensor_scalar(col(C_TA), col(C_TA), val, None, ALU.mult)
                        g.tensor_add(col(C_Y), col(C_Y), col(C_TA))
                    g.tensor_single_scalar(col(C_TA), col(C_VE), 2048.0, ALU.is_ge)
                    g.tensor_scalar(col(C_TA), col(C_TA), 0.0078125, None, ALU.mult)
                    g.tensor_add(col(C_Y), col(C_Y), col(C_TA))
                    for _i in range(14):
                        g.tensor_mul(col(C_TA), col(C_Y), col(C_Y))
                        g.tensor_mul(col(C_TA), col(C_TA), col(C_VE))
                        g.tensor_scalar(col(C_TA), col(C_TA), -0.5, 1.5, ALU.mult, ALU.add)
                        g.tensor_mul(col(C_Y), col(C_Y), col(C_TA))
                    s_g.inc(g.tensor_mul(col(C_NMI), col(C_MEAN), col(C_Y)))
                @blk.vector
                def _(v):
                    s_g.wait(v)
                    s_v.inc(v.tensor_single_scalar(col(C_NMI), col(C_NMI), -1.0, ALU.mult))
                @blk.scalar
                def _(s):
                    s_v.wait(s)
                    s_out.wait(s, s_out.n)
                    s_s.inc(s.activation(lnbuf, ckb, AF.Identity, bias=col(C_NMI), scale=col(C_Y)))
                @blk.vector
                def _(v):
                    s_s.wait(v)
                    s_ld.wait(v)
                    s_v.inc(v.tensor_add(lnbuf, lnbuf, xrel))
                if b == 0 and c == 0:
                    tap("d_fo", ckb)
                @blk.gpsimd
                def _(g, b=b, c=c):
                    s_v.wait(g)
                    s_out.dinc(g.dma_start(y_h[b].rearrange("(t p) d -> t p d", p=128)[c], lnbuf))

        for b in range(NB):
            batch(b)

        @blk.gpsimd
        def _(g):
            g.wait_ge(s_out.h, s_out.n)

    return nc


def kernel(L0_attn_x, L0_attn_Wq, L0_attn_bq, L0_attn_Wk, L0_attn_bk, L0_attn_Wv,
           L0_attn_bv, L0_attn_Wo, L0_attn_bo, L0_ffn_w1, L0_ffn_conv_w,
           L0_ffn_conv_b, L0_ffn_w2):
    x = np.ascontiguousarray(np.asarray(L0_attn_x, dtype=np.float32))
    assert x.shape == (32, S, D)
    base = dict(
        Wq=np.ascontiguousarray(L0_attn_Wq, dtype=np.float32),
        Wk=np.ascontiguousarray(L0_attn_Wk, dtype=np.float32),
        Wv=np.ascontiguousarray(L0_attn_Wv, dtype=np.float32),
        Wo=np.ascontiguousarray(L0_attn_Wo, dtype=np.float32),
        bq=np.ascontiguousarray(L0_attn_bq, dtype=np.float32),
        bk=np.ascontiguousarray(L0_attn_bk, dtype=np.float32),
        bv=np.ascontiguousarray(L0_attn_bv, dtype=np.float32),
        bo=np.ascontiguousarray(L0_attn_bo, dtype=np.float32),
        w1=np.ascontiguousarray(L0_ffn_w1, dtype=np.float32),
        cw=np.ascontiguousarray(np.asarray(L0_ffn_conv_w, dtype=np.float32).reshape(H, 3)),
        cb=np.ascontiguousarray(L0_ffn_conv_b, dtype=np.float32),
        w2=np.ascontiguousarray(L0_ffn_w2, dtype=np.float32),
        idf=np.eye(128, dtype=np.float32),
        idb=np.eye(128, dtype=ml_dtypes.bfloat16),
    )
    nc = build_nc()
    in_maps = [dict(base, x=np.ascontiguousarray(x[4 * c:4 * c + 4])) for c in range(8)]
    res = run_bass_kernel_spmd(nc, in_maps, core_ids=list(range(8)))
    return np.concatenate([r["y"] for r in res.results], axis=0)


# version 04
# speedup vs baseline: 1.2819x; 1.2819x over previous
import numpy as np
import ml_dtypes
import concourse.bass as bass
import concourse.mybir as mybir
from concourse.bass_utils import run_bass_kernel_spmd

F32 = mybir.dt.float32
BF16 = mybir.dt.bfloat16
I32 = mybir.dt.int32
AF = mybir.ActivationFunctionType
ALU = mybir.AluOpType
AX = mybir.AxisListType

NB = 4
S = 2048
D = 1024
H = 4096
NT = S // 128
DT = D // 128
HT = H // 128
CH = 256
NCH = S // CH
FC = 128
NFC = S // FC
EPS = 1e-5
BIGW = 2050
DBG = False


class Sem:
    def __init__(self, nc, name):
        self.h = nc.alloc_semaphore(name)
        self.n = 0

    def inc(self, instr, k=1):
        self.n += k
        instr.then_inc(self.h, k)
        return instr

    def dinc(self, instr):
        self.n += 16
        instr.then_inc(self.h, 16)
        return instr

    def wait(self, eng, val=None):
        eng.wait_ge(self.h, self.n if val is None else val)

    def mark(self):
        return self.n


def build_nc():
    nc = bass.Bass("TRN2", target_bir_lowering=False)

    x_h = nc.dram_tensor("x", [NB, S, D], F32, kind="ExternalInput").ap()
    Wq_h = nc.dram_tensor("Wq", [D, D], F32, kind="ExternalInput").ap()
    Wk_h = nc.dram_tensor("Wk", [D, D], F32, kind="ExternalInput").ap()
    Wv_h = nc.dram_tensor("Wv", [D, D], F32, kind="ExternalInput").ap()
    Wo_h = nc.dram_tensor("Wo", [D, D], F32, kind="ExternalInput").ap()
    bq_h = nc.dram_tensor("bq", [D], F32, kind="ExternalInput").ap()
    bk_h = nc.dram_tensor("bk", [D], F32, kind="ExternalInput").ap()
    bv_h = nc.dram_tensor("bv", [D], F32, kind="ExternalInput").ap()
    bo_h = nc.dram_tensor("bo", [D], F32, kind="ExternalInput").ap()
    w1_h = nc.dram_tensor("w1", [H, D], F32, kind="ExternalInput").ap()
    cw_h = nc.dram_tensor("cw", [H, 3], F32, kind="ExternalInput").ap()
    cb_h = nc.dram_tensor("cb", [H], F32, kind="ExternalInput").ap()
    w2_h = nc.dram_tensor("w2", [D, H], F32, kind="ExternalInput").ap()
    idf_h = nc.dram_tensor("idf", [128, 128], F32, kind="ExternalInput").ap()
    idb_h = nc.dram_tensor("idb", [128, 128], BF16, kind="ExternalInput").ap()

    y_h = nc.dram_tensor("y", [NB, S, D], F32, kind="ExternalOutput").ap()

    w1qT_h = nc.dram_tensor("w1qT_h", [128, DT * H], BF16, kind="Internal").ap()
    w2qT_h = nc.dram_tensor("w2qT_h", [128, HT * D], BF16, kind="Internal").ap()
    xsp_h = nc.dram_tensor("xsp_h", [S, D], F32, kind="Internal").ap()
    xq_h = nc.dram_tensor("xq_h", [S, D], BF16, kind="Internal").ap()

    BASE = 16512
    TOP = 229344

    def esz(dt_):
        return 4 if dt_ in (F32, I32) else 2

    cur = [BASE]

    def place(name, shape, dtype, at=None):
        nbytes = int(np.prod(shape[1:])) * esz(dtype)
        if at is None:
            at = (cur[0] + 31) & ~31
            cur[0] = at + nbytes
        assert at + nbytes <= TOP, f"SBUF overflow at {name}: {at + nbytes} > {TOP}"
        return nc.alloc_sbuf_tensor_at(name, list(shape), dtype, offset=at).ap(), at, at + nbytes

    def P(name, shape, dtype):
        return place(name, shape, dtype)[0]

    idf = P("idf_s", [128, 128], F32)
    idb = P("idb_s", [128, 128], BF16)
    onescol_b = P("onescol_b", [128, 1], BF16)
    onesrow_b = P("onesrow_b", [1, 256], BF16)
    onesrow_f = P("onesrow_f", [1, 128], F32)
    epscol = P("epscol", [128, 1], F32)
    colpk = P("colpk", [128, 24], F32)
    bqcol = P("bqcol", [128, DT], F32)
    bkcol = P("bkcol", [128, DT], F32)
    cwcol = P("cwcol", [128, 3 * HT], F32)
    cbcol = P("cbcol", [128, HT], F32)
    rowf = P("rowf", [1, 256], F32)
    rowb = P("rowb", [1, 256], BF16)
    c1col = P("c1col", [128, NT], F32)
    mcol16 = P("mcol16", [128, NT], F32)
    vcol16 = P("vcol16", [128, NT], F32)
    icol16 = P("icol16", [128, NT], F32)
    xrel = P("xrel", [128, D], F32)
    lnbuf, lnbuf_at, _ = place("lnbuf", [128, D], F32)
    q_i32 = place("q_i32", [128, D], I32, at=lnbuf_at)[0]
    sqb = P("sqb", [128, D], F32)
    ckb = P("ckb", [128, D], F32)

    ATT0 = (cur[0] + 63) & ~63
    a_cur = [ATT0]

    def PA(name, shape, dtype):
        at = (a_cur[0] + 31) & ~31
        ap, _, end = place(name, shape, dtype, at=at)
        a_cur[0] = end
        return ap

    qsqrow = PA("qsqrow", [1, S], BF16)
    ksqrow = PA("ksqrow", [1, S], BF16)
    sqb2a = PA("sqb2a", [128, 512], BF16)
    sqb2b = PA("sqb2b", [128, 512], BF16)
    lnta = PA("lnta", [128, D], BF16)
    lntb = PA("lntb", [128, D], BF16)
    rB = PA("rB", [128, CH], F32)
    sba = PA("sba", [128, CH], F32)
    sbb = PA("sbb", [128, CH], F32)
    eba = PA("eba", [128, CH], BF16)
    ebb = PA("ebb", [128, CH], BF16)
    xrel2 = PA("xrel2", [128, D], F32)
    bvrow = PA("bvrow", [1, D], BF16)
    borow = PA("borow", [1, D], BF16)

    f_cur = [ATT0]

    def PF(name, shape, dtype):
        at = (f_cur[0] + 31) & ~31
        ap, _, end = place(name, shape, dtype, at=at)
        f_cur[0] = end
        return ap

    YS = PF("YS", [128, HT * FC], F32)
    hqb = PF("hqb", [128, HT * FC], BF16)
    cva = PF("cva", [128, 128], F32)
    cvb = PF("cvb", [128, 128], F32)
    cvc = PF("cvc", [128, 128], F32)
    cvd = PF("cvd", [128, 128], F32)
    cve = PF("cve", [128, 128], F32)
    cvf = PF("cvf", [128, 128], F32)
    r2B = PF("r2B", [128, FC], F32)

    PB = (max(a_cur[0], f_cur[0]) + 63) & ~63
    kT = place("kT", [128, DT * S], BF16, at=PB)[0]
    vtok = place("vtok", [128, NT * D], BF16, at=PB + 32768)[0]
    w1qT = place("w1qT", [128, DT * H], BF16, at=PB)[0]
    BIGA = place("BIGA", [128, DT * BIGW], BF16, at=PB + 65536)[0]
    WB = (PB + 65536 + 2 * DT * BIGW + 63) & ~63
    Wbf = place("Wbf", [128, DT * D], BF16, at=WB)[0]
    outn = place("outn", [128, DT * CH], BF16, at=WB + 16384)[0]
    q1c = place("q1c", [128, D], F32, at=WB + 16384)[0]
    wstage = place("wstage", [128, D], F32, at=WB + 16384)[0]
    w2qT = place("w2qT", [128, HT * D], BF16, at=WB)[0]
    endF2 = WB + HT * D * 2
    assert endF2 <= TOP, f"arena overflow endF2={endF2} > {TOP}"
    wst = place("wst", [128, 4 * D], F32, at=PB)[0]
    wst2 = place("wst2", [128, 4 * D], F32, at=PB + 16384)[0]
    tbuf = place("tbuf", [128, 4 * D], BF16, at=PB + 32768)[0]
    stg = place("stg", [128, 4 * D], BF16, at=PB + 40960)[0]
    tern2 = place("tern2", [128, 4 * D], BF16, at=PB + 49152)[0]

    pbk = [nc.alloc_psum_tensor(f"pb{i}", [128, 512], F32).ap() for i in range(8)]
    nc.bump_psum(banks=-2)
    ptb = [nc.alloc_psum_tensor(f"ptb{i}", [128, 1024], BF16).ap() for i in range(2)]

    s_ld = Sem(nc, "s_ld")
    s_sp = Sem(nc, "s_sp")
    s_out = Sem(nc, "s_out")
    s_pe = Sem(nc, "s_pe")
    s_v = Sem(nc, "s_v")
    s_s = Sem(nc, "s_s")
    s_g = Sem(nc, "s_g")

    (C_M1, C_M2, C_T1, C_NT1, C_T2, C_NT2, C_MEAN, C_VAR, C_INV, C_NMI,
     C_AMX, C_R127, C_C2, C_VE, C_Y, C_TA, C_TB, C_SUM, C_M127_2, C_M127_1,
     C_DMY) = range(21)

    def col(i):
        return colpk[:, i:i + 1]

    TBL_LNEXP = 6
    TBL_SILU = 18

    with nc.Block() as blk:
        PIN_TABLES = False

        def pin_table(s, idx):
            if not PIN_TABLES:
                return
            s.add_instruction(mybir.InstLoadActFuncSet(
                name=nc.get_next_instruction_name(), act_func_set_id=idx, ins=[], outs=[]))

        def emit_newton(g, ve_ap, y_ap, ta=None, tb=None, niter=14):
            ta = col(C_TA) if ta is None else ta
            tb = col(C_TB) if tb is None else tb
            g.tensor_single_scalar(y_ap, ve_ap, 2.0, ALU.is_lt)
            for lo, hi, val in ((2.0, 8.0, 0.5), (8.0, 128.0, 0.125),
                                (128.0, 2048.0, 0.03125)):
                g.tensor_single_scalar(ta, ve_ap, lo, ALU.is_ge)
                g.tensor_single_scalar(tb, ve_ap, hi, ALU.is_lt)
                g.tensor_mul(ta, ta, tb)
                g.tensor_scalar(ta, ta, val, None, ALU.mult)
                g.tensor_add(y_ap, y_ap, ta)
            g.tensor_single_scalar(ta, ve_ap, 2048.0, ALU.is_ge)
            g.tensor_scalar(ta, ta, 0.0078125, None, ALU.mult)
            g.tensor_add(y_ap, y_ap, ta)
            for _i in range(niter):
                g.tensor_mul(ta, y_ap, y_ap)
                g.tensor_mul(ta, ta, ve_ap)
                g.tensor_scalar(ta, ta, -0.5, 1.5, ALU.mult, ALU.add)
                g.tensor_mul(y_ap, y_ap, ta)

        def vgap(v, n=1):
            for _ in range(n):
                v.tensor_single_scalar(col(C_DMY), epscol, 1.0, ALU.mult)

        def tap(name, src_ap):
            if not DBG:
                return
            h = nc.dram_tensor(name, list(src_ap.shape), src_ap.dtype, kind="ExternalOutput").ap()
            @blk.gpsimd
            def _(g):
                s_v.wait(g)
                s_s.wait(g)
                s_pe.wait(g)
                s_out.dinc(g.dma_start(h, src_ap))

        @blk.sync
        def _(sync):
            s_ld.dinc(sync.dma_start(idf, idf_h))
            s_ld.dinc(sync.dma_start(idb, idb_h))
            s_ld.dinc(sync.dma_start(bqcol, bq_h.rearrange("(t p) -> p t", p=128), allow_slow_non_contiguous=True))
            s_ld.dinc(sync.dma_start(bkcol, bk_h.rearrange("(t p) -> p t", p=128), allow_slow_non_contiguous=True))
            s_ld.dinc(sync.dma_start(cwcol.rearrange("p (t k) -> p t k", k=3), cw_h.rearrange("(t p) k -> p t k", p=128), allow_slow_non_contiguous=True))
            s_ld.dinc(sync.dma_start(cbcol, cb_h.rearrange("(t p) -> p t", p=128), allow_slow_non_contiguous=True))

        @blk.vector
        def _(v):
            s_ld.wait(v)
            v.memset(onescol_b, 1.0)
            v.memset(onesrow_b, 1.0)
            v.memset(onesrow_f, 1.0)
            s_v.inc(v.memset(epscol, EPS))

        for wi in range(2):
            w_src = w1_h if wi == 0 else w2_h
            nrow = H if wi == 0 else D
            NCHW = 8
            mcol_i = C_M1 if wi == 0 else C_M2
            tcol_i = C_T1 if wi == 0 else C_T2
            ntcol_i = C_NT1 if wi == 0 else C_NT2
            wview = w_src.rearrange("(c q p) d -> c p q d", c=NCHW, p=128)
            for ch in range(NCHW):
                buf = wst if ch % 2 == 0 else wst2
                @blk.sync
                def _(sync):
                    s_v.wait(sync)
                    s_ld.dinc(sync.dma_start(buf.rearrange("p (q d) -> p q d", d=D), wview[ch]))
                @blk.vector
                def _(v, ch=ch, buf=buf):
                    s_ld.wait(v)
                    if ch == 0:
                        s_v.inc(v.reduce_sum(col(C_TA), buf, AX.X, apply_absolute_value=True))
                    else:
                        v.reduce_sum(col(C_TB), buf, AX.X, apply_absolute_value=True)
                        vgap(v)
                        vgap(v)
                        s_v.inc(v.tensor_add(col(C_TA), col(C_TA), col(C_TB)))
            @blk.tensor
            def _(te):
                s_v.wait(te)
                s_pe.inc(te.matmul(pbk[6][0:1, 0:128], col(C_TA), idf, start=True, stop=True,
                                   skip_group_check=True))
            @blk.scalar
            def _(s, wi=wi):
                if wi == 0:
                    pin_table(s, TBL_LNEXP)
                s_pe.wait(s)
                s_s.inc(s.copy(rowf[0:1, 0:128], pbk[6][0:1, 0:128]))
            @blk.vector
            def _(v):
                s_s.wait(v)
                s_v.inc(v.reduce_sum(rowf[0:1, 128:129], rowf[0:1, 0:128], AX.X))
            @blk.tensor
            def _(te):
                s_v.wait(te)
                s_pe.inc(te.matmul(pbk[6][:, 128:129], onesrow_f, rowf[0:1, 128:129],
                                   start=True, stop=True, skip_group_check=True))
            @blk.vector
            def _(v, nrow=nrow, mcol_i=mcol_i, tcol_i=tcol_i, ntcol_i=ntcol_i, wi=wi):
                s_pe.wait(v)
                v.tensor_single_scalar(col(mcol_i), pbk[6][:, 128:129],
                                       1.0 / (nrow * (H * D // nrow)), ALU.mult)
                vgap(v)
                vgap(v)
                v.tensor_single_scalar(col(mcol_i), col(mcol_i), EPS, ALU.max)
                vgap(v)
                vgap(v)
                v.tensor_single_scalar(col(tcol_i), col(mcol_i), 0.5, ALU.mult)
                vgap(v)
                vgap(v)
                v.tensor_single_scalar(col(ntcol_i), col(tcol_i), -1.0, ALU.mult)
                s_v.inc(v.tensor_single_scalar(col(C_M127_2 if wi == 1 else C_M127_1),
                                                col(mcol_i), 1.0 / 127.0, ALU.mult))
            for ch in range(NCHW):
                buf = wst if ch % 2 == 0 else wst2
                @blk.sync
                def _(sync):
                    s_v.wait(sync)
                    s_ld.dinc(sync.dma_start(buf.rearrange("p (q d) -> p q d", d=D), wview[ch]))
                @blk.vector
                def _(v, ch=ch, buf=buf, tcol_i=tcol_i, ntcol_i=ntcol_i, wi=wi):
                    s_ld.wait(v)
                    if ch > 0 or wi > 0:
                        s_sp.wait(v)
                    v.tensor_scalar(tbuf, buf, col(tcol_i), None, ALU.is_ge)
                    v.tensor_scalar(tern2, buf, col(ntcol_i), None, ALU.is_le)
                    vgap(v)
                    s_v.inc(v.tensor_sub(tbuf, tbuf, tern2))
                for q in range(4):
                    for dt in range(DT):
                        bank = ptb[1][:, (dt % 4) * 128:(dt % 4) * 128 + 128]
                        @blk.tensor
                        def _(te, q=q, dt=dt, bank=bank):
                            s_v.wait(te)
                            s_pe.inc(te.transpose(bank, tbuf[:, q * D + dt * 128: q * D + dt * 128 + 128], idb))
                        @blk.vector
                        def _(v, q=q, dt=dt, bank=bank, wi=wi):
                            s_pe.wait(v)
                            if wi == 0:
                                dst = stg[:, dt * 512 + q * 128: dt * 512 + q * 128 + 128]
                            else:
                                dst = stg[:, (q * DT + dt) * 128: (q * DT + dt) * 128 + 128]
                            s_v.inc(v.tensor_copy(dst, bank))
                @blk.gpsimd
                def _(g, ch=ch, wi=wi):
                    s_v.wait(g)
                    if wi == 0:
                        dst = w1qT_h.rearrange("p (dt u) -> p dt u", dt=DT)[:, :, ch * 512:(ch + 1) * 512]
                        s_sp.dinc(g.dma_start(dst, stg.rearrange("p (dt u) -> p dt u", dt=DT)))
                    else:
                        dst = w2qT_h.rearrange("p (ht dd) -> p ht dd", ht=HT)[:, :, ch * 128:(ch + 1) * 128]
                        s_sp.dinc(g.dma_start(dst, stg.rearrange("p (ht j) -> p ht j", ht=HT)))

        def emit_var(v, src):
            v.reduce_sum(col(C_MEAN), src, AX.X)
            v.reduce_sum(col(C_VAR), sqb, AX.X)
            v.tensor_single_scalar(col(C_MEAN), col(C_MEAN), 1.0 / D, ALU.mult)
            v.tensor_single_scalar(col(C_VAR), col(C_VAR), 1.0 / D, ALU.mult)
            v.tensor_mul(col(C_SUM), col(C_MEAN), col(C_MEAN))
            vgap(v, 2)
            return v.tensor_sub(col(C_VAR), col(C_VAR), col(C_SUM))

        def emit_stats(src):
            @blk.scalar
            def _(s):
                s_v.wait(s)
                s_s.inc(s.activation(sqb, src, AF.Square))
            @blk.vector
            def _(v):
                s_s.wait(v)
                emit_var(v, src)
                vgap(v, 2)
                s_v.inc(v.tensor_single_scalar(col(C_VE), col(C_VAR), EPS, ALU.add))
            @blk.gpsimd
            def _(g):
                s_v.wait(g)
                emit_newton(g, col(C_VE), col(C_Y))
                s_g.inc(g.tensor_mul(col(C_NMI), col(C_MEAN), col(C_Y)))
            @blk.vector
            def _(v):
                s_g.wait(v)
                s_v.inc(v.tensor_single_scalar(col(C_NMI), col(C_NMI), -1.0, ALU.mult))

        tap("d_m", colpk)

        def batch(b):
            xr = x_h[b].rearrange("(t p) d -> t p d", p=128)

            @blk.sync
            def _(sync):
                s_s.wait(sync)
                s_ld.dinc(sync.dma_start(ckb[0:1, 0:D], bv_h.rearrange("d -> () d")))
                s_ld.dinc(sync.dma_start(sqb[0:1, 0:D], bo_h.rearrange("d -> () d")))
            @blk.vector
            def _(v):
                s_pe.wait(v)
                s_ld.wait(v)
                for dt in range(DT):
                    v.memset(BIGA[:, dt * BIGW: dt * BIGW + 1], 0.0)
                v.tensor_copy(bvrow, ckb[0:1, 0:D])
                v.tensor_copy(borow, sqb[0:1, 0:D])
                s_v.inc(v.memset(BIGA[:, (DT - 1) * BIGW + 2049: (DT - 1) * BIGW + 2050], 0.0))
            @blk.vector
            def _(v):
                for dt in range(DT - 1):
                    v.memset(BIGA[:, dt * BIGW + 2049: dt * BIGW + 2050], 0.0)
                s_v.inc(v.tensor_single_scalar(col(C_DMY), epscol, 1.0, ALU.mult))

            for t in range(NT):
                xb = xrel if t % 2 == 0 else xrel2
                @blk.sync
                def _(sync, t=t, xb=xb):
                    s_s.wait(sync)
                    s_v.wait(sync)
                    s_ld.dinc(sync.dma_start(xb, xr[t]))
                @blk.scalar
                def _(s, xb=xb, t=t):
                    s_ld.wait(s)
                    s_s.inc(s.activation(sqb, xb, AF.Square))
                @blk.vector
                def _(v, xb=xb, t=t):
                    s_s.wait(v)
                    v.reduce_sum(mcol16[:, t:t + 1], xb, AX.X)
                    v.reduce_sum(vcol16[:, t:t + 1], sqb, AX.X)
                    s_v.inc(v.tensor_single_scalar(mcol16[:, t:t + 1], mcol16[:, t:t + 1],
                                                   1.0 / D, ALU.mult))
            @blk.vector
            def _(v):
                v.tensor_single_scalar(vcol16, vcol16, 1.0 / D, ALU.mult)
                v.tensor_mul(icol16, mcol16, mcol16)
                vgap(v, 2)
                v.tensor_sub(vcol16, vcol16, icol16)
                vgap(v, 2)
                s_v.inc(v.tensor_single_scalar(vcol16, vcol16, EPS, ALU.add))
            @blk.gpsimd
            def _(g):
                s_v.wait(g)
                emit_newton(g, vcol16, icol16, sqb[:, 0:16], sqb[:, 16:32])
                s_g.inc(g.tensor_mul(mcol16, mcol16, icol16))
            @blk.vector
            def _(v):
                s_g.wait(v)
                s_v.inc(v.tensor_single_scalar(mcol16, mcol16, -1.0, ALU.mult))
            for t in range(NT):
                xb = xrel if t % 2 == 0 else xrel2
                lb = lnta if t % 2 == 0 else lntb
                @blk.sync
                def _(sync, t=t, xb=xb):
                    s_s.wait(sync)
                    s_v.wait(sync)
                    s_ld.dinc(sync.dma_start(xb, xr[t]))
                @blk.scalar
                def _(s, xb=xb, lb=lb, t=t):
                    s_ld.wait(s)
                    s_s.inc(s.activation(lb, xb, AF.Identity, bias=mcol16[:, t:t + 1],
                                         scale=icol16[:, t:t + 1]))
                for dt in range(DT):
                    bank = ptb[dt % 2][:, 0:128]
                    @blk.tensor
                    def _(te, lb=lb, dt=dt, bank=bank):
                        s_s.wait(te)
                        s_v.wait(te)
                        s_pe.inc(te.transpose(bank, lb[:, dt * 128: dt * 128 + 128], idb))
                    @blk.vector
                    def _(v, t=t, dt=dt, bank=bank):
                        s_pe.wait(v)
                        s_v.inc(v.tensor_copy(
                            BIGA[:, dt * BIGW + 1 + t * 128: dt * BIGW + 1 + t * 128 + 128], bank))

            def load_weight(w_src):
                for dt in range(DT):
                    @blk.sync
                    def _(sync, dt=dt, w_src=w_src):
                        s_v.wait(sync)
                        s_pe.wait(sync)
                        s_ld.dinc(sync.dma_start(wstage, w_src[dt * 128:(dt + 1) * 128, :]))
                    @blk.vector
                    def _(v, dt=dt):
                        s_ld.wait(v)
                        s_v.inc(v.tensor_copy(Wbf[:, dt * D:(dt + 1) * D], wstage))

            load_weight(Wv_h)
            for mt in range(NT):
                for ncv in range(2):
                    bank = pbk[(mt * 2 + ncv) % 4][:, 0:512]
                    @blk.tensor
                    def _(te, mt=mt, ncv=ncv, bank=bank):
                        if mt == 0 and ncv == 0:
                            s_v.wait(te)
                        s_s.wait(te)
                        for dt in range(DT):
                            te.matmul(bank, BIGA[:, dt * BIGW + 1 + mt * 128: dt * BIGW + 1 + mt * 128 + 128],
                                      Wbf[:, dt * D + ncv * 512: dt * D + ncv * 512 + 512],
                                      start=(dt == 0), stop=False, skip_group_check=True)
                        s_pe.inc(te.matmul(bank, onesrow_b[0:1, 0:128], bvrow[:, ncv * 512: ncv * 512 + 512],
                                           start=False, stop=True, skip_group_check=True))
                    @blk.scalar
                    def _(s, mt=mt, ncv=ncv, bank=bank):
                        s_pe.wait(s)
                        s_s.inc(s.copy(vtok[:, mt * D + ncv * 512: mt * D + ncv * 512 + 512], bank))

            load_weight(Wk_h)
            for mt in range(DT):
                for ncv in range(4):
                    bank = pbk[(mt * 4 + ncv) % 4][:, 0:512]
                    @blk.tensor
                    def _(te, mt=mt, ncv=ncv, bank=bank):
                        if mt == 0 and ncv == 0:
                            s_v.wait(te)
                        s_s.wait(te)
                        ii = None
                        for dt in range(DT):
                            ii = te.matmul(bank, Wbf[:, dt * D + mt * 128: dt * D + mt * 128 + 128],
                                           BIGA[:, dt * BIGW + 1 + ncv * 512: dt * BIGW + 1 + ncv * 512 + 512],
                                           start=(dt == 0), stop=(dt == DT - 1), skip_group_check=True)
                        s_pe.inc(ii)
                    @blk.scalar
                    def _(s, mt=mt, ncv=ncv, bank=bank):
                        s_pe.wait(s)
                        s_s.inc(s.activation(kT[:, mt * S + ncv * 512: mt * S + ncv * 512 + 512], bank,
                                             AF.Identity, bias=bkcol[:, mt:mt + 1], scale=1.0))
            for ncv in range(4):
                bank = pbk[4][0:1, 0:512]
                for dt in range(DT):
                    sq = sqb2a if dt % 2 == 0 else sqb2b
                    @blk.scalar
                    def _(s, ncv=ncv, dt=dt, sq=sq):
                        s_pe.wait(s)
                        s_s.inc(s.activation(sq, kT[:, dt * S + ncv * 512: dt * S + ncv * 512 + 512],
                                             AF.Square))
                    @blk.tensor
                    def _(te, dt=dt, sq=sq, bank=bank):
                        s_s.wait(te)
                        s_pe.inc(te.matmul(bank, onescol_b, sq, start=(dt == 0), stop=(dt == DT - 1),
                                           skip_group_check=True))
                @blk.scalar
                def _(s, ncv=ncv, bank=bank):
                    s_pe.wait(s)
                    s_s.inc(s.activation(ksqrow[0:1, ncv * 512:(ncv + 1) * 512], bank, AF.Identity,
                                         bias=0.0, scale=-0.5))

            load_weight(Wq_h)
            for ncv in range(4):
                @blk.tensor
                def _(te, ncv=ncv):
                    if ncv == 0:
                        s_v.wait(te)
                    s_s.wait(te)
                    ii = None
                    for mt in range(DT):
                        for dt in range(DT):
                            ii = te.matmul(pbk[mt][:, 0:512],
                                           Wbf[:, dt * D + mt * 128: dt * D + mt * 128 + 128],
                                           BIGA[:, dt * BIGW + 1 + ncv * 512: dt * BIGW + 1 + ncv * 512 + 512],
                                           start=(dt == 0), stop=(dt == DT - 1), skip_group_check=True)
                    s_pe.inc(ii)
                @blk.scalar
                def _(s, ncv=ncv):
                    s_pe.wait(s)
                    ii = None
                    for mt in range(DT):
                        ii = s.activation(BIGA[:, mt * BIGW + 1 + ncv * 512: mt * BIGW + 1 + ncv * 512 + 512],
                                          pbk[mt][:, 0:512], AF.Identity, bias=bqcol[:, mt:mt + 1], scale=1.0)
                    s_s.inc(ii)
            for ncv in range(4):
                bank = pbk[4][0:1, 0:512]
                for dt in range(DT):
                    sq = sqb2a if dt % 2 == 0 else sqb2b
                    @blk.scalar
                    def _(s, ncv=ncv, dt=dt, sq=sq):
                        s_pe.wait(s)
                        s_s.inc(s.activation(sq,
                                             BIGA[:, dt * BIGW + 1 + ncv * 512: dt * BIGW + 1 + ncv * 512 + 512],
                                             AF.Square))
                    @blk.tensor
                    def _(te, dt=dt, sq=sq, bank=bank):
                        s_s.wait(te)
                        s_pe.inc(te.matmul(bank, onescol_b, sq, start=(dt == 0), stop=(dt == DT - 1),
                                           skip_group_check=True))
                @blk.scalar
                def _(s, ncv=ncv, bank=bank):
                    s_pe.wait(s)
                    s_s.inc(s.activation(qsqrow[0:1, ncv * 512:(ncv + 1) * 512], bank, AF.Identity,
                                         bias=0.0, scale=-0.5))

            load_weight(Wo_h)
            if b == 0:
                tap("d_kt", kT)
                tap("d_vt", vtok)
                tap("d_q", BIGA)
                tap("d_qsq", qsqrow)
                tap("d_ksq", ksqrow)
                tap("d_wbf", Wbf)

            m_qk = {}
            m_exp = {}
            m_ev = {}
            m_evac = {}
            m_ckb = {}
            for c in range(NCH):
                opsum = [pbk[2 + i // 2][:, (i % 2) * CH:(i % 2) * CH + CH] for i in range(DT)]
                dn = pbk[6][0:1, 0:CH]
                for step in range(NT + 1):
                    jt = step
                    if jt < NT:
                        qk = pbk[jt % 2][:, 0:CH]
                        sb_ = sba if jt % 2 == 0 else sbb
                        eb_ = eba if jt % 2 == 0 else ebb
                        @blk.tensor
                        def _(te, c=c, jt=jt, qk=qk):
                            prev = (c, jt - 2) if jt >= 2 else (c - 1, NT + jt - 2)
                            if prev in m_exp:
                                s_s.wait(te, m_exp[prev])
                            if jt == 0 and c > 0:
                                s_v.wait(te, m_evac[c - 1])
                                s_s.wait(te, m_ckb[c - 1])
                            for dt in range(DT):
                                te.matmul(qk, kT[:, dt * S + jt * 128: dt * S + jt * 128 + 128],
                                          BIGA[:, dt * BIGW + 1 + c * CH: dt * BIGW + 1 + c * CH + CH],
                                          start=(dt == 0), stop=False, skip_group_check=True)
                            te.matmul(qk, onesrow_b[0:1, 0:128], qsqrow[0:1, c * CH: c * CH + CH],
                                      start=False, stop=False, skip_group_check=True)
                            s_pe.inc(te.matmul(qk, ksqrow[0:1, jt * 128: jt * 128 + 128],
                                               onesrow_b[0:1, 0:CH],
                                               start=False, stop=True, skip_group_check=True))
                            m_qk[(c, jt)] = s_pe.mark()
                        @blk.scalar
                        def _(s, c=c, jt=jt, qk=qk, sb_=sb_, eb_=eb_):
                            s_pe.wait(s, m_qk[(c, jt)])
                            prev = (c, jt - 2) if jt >= 2 else (c - 1, NT + jt - 2)
                            if prev in m_ev:
                                s_pe.wait(s, m_ev[prev])
                            s.activation(sb_, qk, AF.Exp, bias=0.0, scale=2.0)
                            s_s.inc(s.activation(eb_, sb_, AF.Exp, bias=0.0, scale=1.0))
                            m_exp[(c, jt)] = s_s.mark()
                    if step >= 1:
                        jt1 = step - 1
                        eb1 = eba if jt1 % 2 == 0 else ebb
                        @blk.tensor
                        def _(te, c=c, jt1=jt1, eb1=eb1):
                            s_s.wait(te, m_exp[(c, jt1)])
                            for i in range(DT):
                                te.matmul(opsum[i], vtok[:, jt1 * D + i * 128: jt1 * D + i * 128 + 128],
                                          eb1, start=(jt1 == 0 and i % 2 == 0), stop=(jt1 == NT - 1),
                                          skip_group_check=True)
                            s_pe.inc(te.matmul(dn, onescol_b, eb1, start=(jt1 == 0), stop=(jt1 == NT - 1),
                                               skip_group_check=True))
                            m_ev[(c, jt1)] = s_pe.mark()
                if b == 0 and c == 0:
                    tap("d_sb", sbb)
                    tap("d_eb", ebb)
                @blk.vector
                def _(v):
                    s_pe.wait(v)
                    v.reciprocal(rowf[0:1, 0:CH], dn)
                    vgap(v)
                    vgap(v)
                    s_v.inc(v.tensor_copy(rowb[0:1, 0:CH], rowf[0:1, 0:CH]))
                @blk.tensor
                def _(te):
                    s_v.wait(te)
                    s_pe.inc(te.matmul(pbk[6][:, CH:CH + CH], onesrow_b[0:1, 0:128], rowb[0:1, 0:CH],
                                       start=True, stop=True, skip_group_check=True))
                @blk.vector
                def _(v, c=c):
                    s_pe.wait(v)
                    v.tensor_copy(rB, pbk[6][:, CH:CH + CH])
                    vgap(v)
                    vgap(v)
                    ii = None
                    for i in range(DT):
                        ii = v.tensor_mul(outn[:, i * CH:(i + 1) * CH], opsum[i], rB)
                    s_v.inc(ii)
                    m_evac[c] = s_v.mark()
                if b == 0 and c == 0:
                    tap("d_outn", outn)
                    tap("d_rb", rB)
                    tap("d_rowf", rowf)
                for imt in range(2):
                    aop = [pbk[2 + 2 * imt + ncv][:, 0:512] for ncv in range(2)]
                    @blk.tensor
                    def _(te, imt=imt, aop=aop):
                        s_v.wait(te)
                        ii = None
                        for ncv in range(2):
                            for i in range(DT):
                                te.matmul(aop[ncv], outn[:, i * CH + imt * 128: i * CH + imt * 128 + 128],
                                          Wbf[:, i * D + ncv * 512: i * D + ncv * 512 + 512],
                                          start=(i == 0), stop=False, skip_group_check=True)
                            ii = te.matmul(aop[ncv], onesrow_b[0:1, 0:128], borow[:, ncv * 512: ncv * 512 + 512],
                                           start=False, stop=True, skip_group_check=True)
                        s_pe.inc(ii)
                for imt in range(2):
                    aop = [pbk[2 + 2 * imt + ncv][:, 0:512] for ncv in range(2)]
                    @blk.scalar
                    def _(s, c=c, aop=aop):
                        s_pe.wait(s)
                        s_v.wait(s)
                        s.copy(ckb[:, 0:512], aop[0])
                        s_s.inc(s.copy(ckb[:, 512:1024], aop[1]))
                        m_ckb[c] = s_s.mark()
                    t_glob = c * 2 + imt
                    xb = xrel if t_glob % 2 == 0 else xrel2
                    @blk.sync
                    def _(sync, t_glob=t_glob, xb=xb):
                        s_sp.wait(sync)
                        s_v.wait(sync)
                        s_ld.dinc(sync.dma_start(xb, xr[t_glob]))
                    emit_stats(ckb)
                    @blk.scalar
                    def _(s):
                        s_v.wait(s)
                        s_s.inc(s.activation(lnbuf, ckb, AF.Identity, bias=col(C_NMI), scale=col(C_Y)))
                    @blk.vector
                    def _(v, xb=xb):
                        s_s.wait(v)
                        s_ld.wait(v)
                        s_v.inc(v.tensor_add(xb, xb, lnbuf))
                    if b == 0 and c == 0:
                        tap("d_ao" + str(imt), ckb)
                        tap("d_xnew" + str(imt), xb)
                    @blk.gpsimd
                    def _(g, t_glob=t_glob, xb=xb):
                        s_v.wait(g)
                        s_sp.dinc(g.dma_start(xsp_h.rearrange("(t p) d -> t p d", p=128)[t_glob], xb))
                    emit_stats(xb)
                    @blk.scalar
                    def _(s, xb=xb):
                        s_v.wait(s)
                        s_s.inc(s.activation(lnbuf, xb, AF.Identity, bias=col(C_NMI), scale=col(C_Y)))
                    @blk.vector
                    def _(v, t_glob=t_glob):
                        s_s.wait(v)
                        v.reduce_max(col(C_AMX), lnbuf, AX.X, apply_absolute_value=True)
                        vgap(v)
                        vgap(v)
                        v.tensor_single_scalar(col(C_AMX), col(C_AMX), EPS, ALU.max)
                        vgap(v)
                        vgap(v)
                        v.reciprocal(col(C_C2), col(C_AMX))
                        v.tensor_mul(c1col[:, t_glob:t_glob + 1], col(C_AMX), col(C_M127_1))
                        vgap(v)
                        v.tensor_single_scalar(col(C_R127), col(C_C2), 127.0, ALU.mult)
                        vgap(v, 2)
                        v.tensor_scalar(sqb, lnbuf, col(C_R127), 128.0, ALU.mult, ALU.add)
                        vgap(v)
                        v.tensor_copy(q_i32, sqb)
                        vgap(v)
                        v.tensor_copy(q1c, q_i32)
                        vgap(v)
                        v.scalar_tensor_tensor(ckb, sqb, 0.5, q1c, ALU.add, ALU.subtract)
                        vgap(v)
                        v.tensor_single_scalar(ckb, ckb, 1.0, ALU.is_ge)
                        vgap(v)
                        v.scalar_tensor_tensor(q1c, q1c, -128.0, ckb, ALU.add, ALU.add)
                        vgap(v)
                        s_v.inc(v.tensor_scalar(lnta if t_glob % 2 == 0 else lntb, q1c,
                                                c1col[:, t_glob:t_glob + 1], None, ALU.mult))
                    @blk.gpsimd
                    def _(g, t_glob=t_glob):
                        s_v.wait(g)
                        s_sp.dinc(g.dma_start(xq_h.rearrange("(t p) d -> t p d", p=128)[t_glob],
                                             lnta if t_glob % 2 == 0 else lntb))

            @blk.sync
            def _(sync):
                s_pe.wait(sync)
                s_sp.wait(sync)
                s_s.wait(sync)
                s_ld.dinc(sync.dma_start(w1qT, w1qT_h))
                s_ld.dinc(sync.dma_start(w2qT, w2qT_h))
            for t in range(NT):
                lb = lnta if t % 2 == 0 else lntb
                @blk.sync
                def _(sync, t=t, lb=lb):
                    s_v.wait(sync)
                    s_ld.dinc(sync.dma_start(lb, xq_h.rearrange("(t p) d -> t p d", p=128)[t]))
                for dt in range(DT):
                    bank = ptb[dt % 2][:, 0:128]
                    @blk.tensor
                    def _(te, dt=dt, lb=lb, bank=bank):
                        s_ld.wait(te)
                        s_v.wait(te)
                        s_pe.inc(te.transpose(bank, lb[:, dt * 128: dt * 128 + 128], idb))
                    @blk.vector
                    def _(v, t=t, dt=dt, bank=bank):
                        s_pe.wait(v)
                        s_v.inc(v.tensor_copy(
                            BIGA[:, dt * BIGW + 1 + t * 128: dt * BIGW + 1 + t * 128 + 128], bank))

            if b == 0:
                tap("d_xqt", BIGA)
                tap("d_w1", w1qT)
                tap("d_w2", w2qT)

            m_u2 = {}
            m_silu = {}
            m_hmm = {}
            for c in range(NFC):
                for s_i in range(HT + 2):
                    ht = s_i
                    if ht < HT:
                        hb = pbk[ht % 4][:, 0:130]
                        @blk.tensor
                        def _(te, c=c, ht=ht, hb=hb):
                            if c == 0 and ht == 0:
                                s_ld.wait(te)
                                s_v.wait(te)
                                s_s.wait(te)
                            prev = (c, ht - 4) if ht >= 4 else (c - 1, HT + ht - 4)
                            if prev in m_u2:
                                s_v.wait(te, m_u2[prev])
                            ii = None
                            for dt in range(DT):
                                ii = te.matmul(hb, w1qT[:, dt * H + ht * 128: dt * H + ht * 128 + 128],
                                               BIGA[:, dt * BIGW + c * FC: dt * BIGW + c * FC + 130],
                                               start=(dt == 0), stop=(dt == DT - 1), skip_group_check=True)
                            s_pe.inc(ii)
                            m_hmm[(c, ht)] = s_pe.mark()
                    @blk.vector
                    def _(v, c=c, s_i=s_i):
                        ht0, ht1, ht2 = s_i, s_i - 1, s_i - 2
                        did = None
                        if ht0 < HT:
                            s_pe.wait(v, m_hmm[(c, ht0)])
                            hb = pbk[ht0 % 4][:, 0:130]
                            did = v.tensor_scalar(cva if ht0 % 2 == 0 else cvb, hb[:, 0:128],
                                                  cwcol[:, ht0 * 3: ht0 * 3 + 1], None, ALU.mult)
                        if 0 <= ht1 < HT:
                            hb = pbk[ht1 % 4][:, 0:130]
                            did = v.scalar_tensor_tensor(cvc if ht1 % 2 == 0 else cvd, hb[:, 1:129],
                                                         cwcol[:, ht1 * 3 + 1: ht1 * 3 + 2],
                                                         cva if ht1 % 2 == 0 else cvb,
                                                         ALU.mult, ALU.add)
                        if 0 <= ht2 < HT:
                            if (c, ht2 - 2) in m_silu:
                                s_s.wait(v, m_silu[(c, ht2 - 2)])
                            hb = pbk[ht2 % 4][:, 0:130]
                            did = v.scalar_tensor_tensor(cve if ht2 % 2 == 0 else cvf, hb[:, 2:130],
                                                         cwcol[:, ht2 * 3 + 2: ht2 * 3 + 3],
                                                         cvc if ht2 % 2 == 0 else cvd,
                                                         ALU.mult, ALU.add)
                            s_v.inc(did)
                            m_u2[(c, ht2)] = s_v.mark()
                        elif did is not None:
                            s_v.inc(did)
                    if 0 <= s_i - 2 < HT:
                        ht2 = s_i - 2
                        @blk.scalar
                        def _(s, c=c, ht2=ht2):
                            if c == 0 and ht2 == 0 and b == 0:
                                pin_table(s, TBL_SILU)
                            s_v.wait(s, m_u2[(c, ht2)])
                            s_s.inc(s.activation(YS[:, ht2 * FC:(ht2 + 1) * FC],
                                                 cve if ht2 % 2 == 0 else cvf, AF.Silu,
                                                 bias=cbcol[:, ht2:ht2 + 1], scale=1.0))
                            m_silu[(c, ht2)] = s_s.mark()
                @blk.vector
                def _(v):
                    s_s.wait(v)
                    s_v.inc(v.reduce_max(cva, YS.rearrange("p (ht i) -> p i ht", ht=HT), AX.X,
                                         apply_absolute_value=True))
                @blk.tensor
                def _(te):
                    s_v.wait(te)
                    s_pe.inc(te.transpose(pbk[6][:, 0:128], cva, idf))
                @blk.vector
                def _(v):
                    s_pe.wait(v)
                    v.reduce_max(col(C_AMX), pbk[6][:, 0:128], AX.X)
                    vgap(v)
                    vgap(v)
                    v.tensor_single_scalar(col(C_AMX), col(C_AMX), EPS, ALU.max)
                    vgap(v)
                    vgap(v)
                    v.reciprocal(col(C_R127), col(C_AMX))
                    v.tensor_mul(col(C_C2), col(C_AMX), col(C_M127_2))
                    vgap(v)
                    s_v.inc(v.tensor_single_scalar(col(C_R127), col(C_R127), 127.0, ALU.mult))
                @blk.tensor
                def _(te):
                    s_v.wait(te)
                    s_pe.inc(te.matmul(pbk[6][0:1, 128:256], col(C_R127), idf, start=True, stop=True,
                                       skip_group_check=True))
                @blk.scalar
                def _(s):
                    s_pe.wait(s)
                    s_s.inc(s.copy(rowf[0:1, 0:FC], pbk[6][0:1, 128:256]))
                @blk.tensor
                def _(te):
                    s_s.wait(te)
                    s_pe.inc(te.matmul(pbk[6][:, 256:384], onesrow_f, rowf[0:1, 0:FC],
                                       start=True, stop=True, skip_group_check=True))
                @blk.vector
                def _(v):
                    s_pe.wait(v)
                    s_v.inc(v.tensor_copy(r2B, pbk[6][:, 256:384]))
                for qq in range(4):
                    sl = slice(qq * 8 * FC, (qq + 1) * 8 * FC)
                    @blk.vector
                    def _(v, sl=sl, c=c, qq=qq):
                        s_v.wait(v)
                        if c > 0 and qq == 0:
                            s_pe.wait(v)
                        ysq = YS[:, sl]
                        vgap(v)
                        v.tensor_tensor(ysq.rearrange("p (a i) -> p a i", a=8),
                                        ysq.rearrange("p (a i) -> p a i", a=8),
                                        r2B.rearrange("p i -> p () i").broadcast_to([128, 8, FC]),
                                        ALU.mult)
                        vgap(v)
                        v.tensor_single_scalar(ysq, ysq, 128.0, ALU.add)
                        vgap(v)
                        v.tensor_copy(q_i32, ysq)
                        vgap(v)
                        v.tensor_copy(sqb, q_i32)
                        vgap(v)
                        v.scalar_tensor_tensor(ysq, ysq, 0.5, sqb, ALU.add, ALU.subtract)
                        vgap(v)
                        v.tensor_single_scalar(ysq, ysq, 1.0, ALU.is_ge)
                        vgap(v)
                        s_v.inc(v.scalar_tensor_tensor(hqb[:, sl], sqb, -128.0, ysq, ALU.add, ALU.add))
                if b == 0 and c == 0:
                    tap("d_ys", YS)
                    tap("d_hq", hqb)
                    tap("d_c1", c1col)
                for ncv in range(2):
                    bank = pbk[4 + ncv][:, 0:512]
                    @blk.tensor
                    def _(te, c=c, ncv=ncv, bank=bank):
                        s_v.wait(te)
                        s_s.wait(te)
                        ii = None
                        for ht in range(HT):
                            ii = te.matmul(bank, hqb[:, ht * FC: ht * FC + 128],
                                           w2qT[:, ht * D + ncv * 512: ht * D + ncv * 512 + 512],
                                           start=(ht == 0), stop=(ht == HT - 1), skip_group_check=True)
                        s_pe.inc(ii)
                    @blk.scalar
                    def _(s, ncv=ncv, bank=bank):
                        s_pe.wait(s)
                        s_s.inc(s.activation(ckb[:, ncv * 512:(ncv + 1) * 512], bank, AF.Identity,
                                             bias=0.0, scale=col(C_C2)))
                @blk.sync
                def _(sync, c=c):
                    s_v.wait(sync)
                    s_out.wait(sync, s_out.n)
                    s_ld.dinc(sync.dma_start(xrel, xsp_h.rearrange("(t p) d -> t p d", p=128)[c]))
                @blk.scalar
                def _(s):
                    s_s.inc(s.activation(sqb, ckb, AF.Square))
                @blk.vector
                def _(v):
                    s_s.wait(v)
                    emit_var(v, ckb)
                    vgap(v)
                    vgap(v)
                    s_v.inc(v.tensor_single_scalar(col(C_VE), col(C_VAR), EPS, ALU.add))
                @blk.gpsimd
                def _(g):
                    s_v.wait(g)
                    emit_newton(g, col(C_VE), col(C_Y))
                    s_g.inc(g.tensor_mul(col(C_NMI), col(C_MEAN), col(C_Y)))
                @blk.vector
                def _(v):
                    s_g.wait(v)
                    s_v.inc(v.tensor_single_scalar(col(C_NMI), col(C_NMI), -1.0, ALU.mult))
                @blk.scalar
                def _(s):
                    s_v.wait(s)
                    s_out.wait(s, s_out.n)
                    s_s.inc(s.activation(lnbuf, ckb, AF.Identity, bias=col(C_NMI), scale=col(C_Y)))
                @blk.vector
                def _(v):
                    s_s.wait(v)
                    s_ld.wait(v)
                    s_v.inc(v.tensor_add(lnbuf, lnbuf, xrel))
                if b == 0 and c == 0:
                    tap("d_fo", ckb)
                @blk.gpsimd
                def _(g, b=b, c=c):
                    s_v.wait(g)
                    s_out.dinc(g.dma_start(y_h[b].rearrange("(t p) d -> t p d", p=128)[c], lnbuf))

        for b in range(NB):
            batch(b)

        @blk.gpsimd
        def _(g):
            g.wait_ge(s_out.h, s_out.n)

    return nc


def kernel(L0_attn_x, L0_attn_Wq, L0_attn_bq, L0_attn_Wk, L0_attn_bk, L0_attn_Wv,
           L0_attn_bv, L0_attn_Wo, L0_attn_bo, L0_ffn_w1, L0_ffn_conv_w,
           L0_ffn_conv_b, L0_ffn_w2):
    x = np.ascontiguousarray(np.asarray(L0_attn_x, dtype=np.float32))
    assert x.shape == (32, S, D)
    base = dict(
        Wq=np.ascontiguousarray(L0_attn_Wq, dtype=np.float32),
        Wk=np.ascontiguousarray(L0_attn_Wk, dtype=np.float32),
        Wv=np.ascontiguousarray(L0_attn_Wv, dtype=np.float32),
        Wo=np.ascontiguousarray(L0_attn_Wo, dtype=np.float32),
        bq=np.ascontiguousarray(L0_attn_bq, dtype=np.float32),
        bk=np.ascontiguousarray(L0_attn_bk, dtype=np.float32),
        bv=np.ascontiguousarray(L0_attn_bv, dtype=np.float32),
        bo=np.ascontiguousarray(L0_attn_bo, dtype=np.float32),
        w1=np.ascontiguousarray(L0_ffn_w1, dtype=np.float32),
        cw=np.ascontiguousarray(np.asarray(L0_ffn_conv_w, dtype=np.float32).reshape(H, 3)),
        cb=np.ascontiguousarray(L0_ffn_conv_b, dtype=np.float32),
        w2=np.ascontiguousarray(L0_ffn_w2, dtype=np.float32),
        idf=np.eye(128, dtype=np.float32),
        idb=np.eye(128, dtype=ml_dtypes.bfloat16),
    )
    nc = build_nc()
    in_maps = [dict(base, x=np.ascontiguousarray(x[4 * c:4 * c + 4])) for c in range(8)]
    res = run_bass_kernel_spmd(nc, in_maps, core_ids=list(range(8)))
    return np.concatenate([r["y"] for r in res.results], axis=0)


# version 06
# speedup vs baseline: 1.5648x; 1.2207x over previous
import numpy as np
import ml_dtypes
import concourse.bass as bass
import concourse.mybir as mybir
from concourse.bass_utils import run_bass_kernel_spmd

F32 = mybir.dt.float32
BF16 = mybir.dt.bfloat16
I32 = mybir.dt.int32
AF = mybir.ActivationFunctionType
ALU = mybir.AluOpType
AX = mybir.AxisListType

NB = 4
S = 2048
D = 1024
H = 4096
NT = S // 128
DT = D // 128
HT = H // 128
CH = 256
NCH = S // CH
FC = 128
NFC = S // FC
EPS = 1e-5
BIGW = 2050
DBG = False


class Sem:
    def __init__(self, nc, name):
        self.h = nc.alloc_semaphore(name)
        self.n = 0

    def inc(self, instr, k=1):
        self.n += k
        instr.then_inc(self.h, k)
        return instr

    def dinc(self, instr):
        self.n += 16
        instr.then_inc(self.h, 16)
        return instr

    def wait(self, eng, val=None):
        eng.wait_ge(self.h, self.n if val is None else val)

    def mark(self):
        return self.n


def build_nc():
    nc = bass.Bass("TRN2", target_bir_lowering=False)

    x_h = nc.dram_tensor("x", [NB, S, D], F32, kind="ExternalInput").ap()
    Wq_h = nc.dram_tensor("Wq", [D, D], F32, kind="ExternalInput").ap()
    Wk_h = nc.dram_tensor("Wk", [D, D], F32, kind="ExternalInput").ap()
    Wv_h = nc.dram_tensor("Wv", [D, D], F32, kind="ExternalInput").ap()
    Wo_h = nc.dram_tensor("Wo", [D, D], F32, kind="ExternalInput").ap()
    bq_h = nc.dram_tensor("bq", [D], F32, kind="ExternalInput").ap()
    bk_h = nc.dram_tensor("bk", [D], F32, kind="ExternalInput").ap()
    bv_h = nc.dram_tensor("bv", [D], F32, kind="ExternalInput").ap()
    bo_h = nc.dram_tensor("bo", [D], F32, kind="ExternalInput").ap()
    w1_h = nc.dram_tensor("w1", [H, D], F32, kind="ExternalInput").ap()
    cw_h = nc.dram_tensor("cw", [H, 3], F32, kind="ExternalInput").ap()
    cb_h = nc.dram_tensor("cb", [H], F32, kind="ExternalInput").ap()
    w2_h = nc.dram_tensor("w2", [D, H], F32, kind="ExternalInput").ap()
    idf_h = nc.dram_tensor("idf", [128, 128], F32, kind="ExternalInput").ap()
    idb_h = nc.dram_tensor("idb", [128, 128], BF16, kind="ExternalInput").ap()

    y_h = nc.dram_tensor("y", [NB, S, D], F32, kind="ExternalOutput").ap()

    w1qT_h = nc.dram_tensor("w1qT_h", [128, DT * H], BF16, kind="Internal").ap()
    w2qT_h = nc.dram_tensor("w2qT_h", [128, HT * D], BF16, kind="Internal").ap()
    xsp_h = nc.dram_tensor("xsp_h", [S, D], F32, kind="Internal").ap()
    xq_h = nc.dram_tensor("xq_h", [S, D], BF16, kind="Internal").ap()

    BASE = 16512
    TOP = 229344

    def esz(dt_):
        return 4 if dt_ in (F32, I32) else 2

    cur = [BASE]

    def place(name, shape, dtype, at=None):
        nbytes = int(np.prod(shape[1:])) * esz(dtype)
        if at is None:
            at = (cur[0] + 31) & ~31
            cur[0] = at + nbytes
        assert at + nbytes <= TOP, f"SBUF overflow at {name}: {at + nbytes} > {TOP}"
        return nc.alloc_sbuf_tensor_at(name, list(shape), dtype, offset=at).ap(), at, at + nbytes

    def P(name, shape, dtype):
        return place(name, shape, dtype)[0]

    idf = P("idf_s", [128, 128], F32)
    idb = P("idb_s", [128, 128], BF16)
    onescol_b = P("onescol_b", [128, 1], BF16)
    onesrow_b = P("onesrow_b", [1, 256], BF16)
    onesrow_f = P("onesrow_f", [1, 128], F32)
    epscol = P("epscol", [128, 1], F32)
    colpk = P("colpk", [128, 24], F32)
    bqcol = P("bqcol", [128, DT], F32)
    bkcol = P("bkcol", [128, DT], F32)
    cwcol = P("cwcol", [128, 3 * HT], F32)
    cbcol = P("cbcol", [128, HT], F32)
    rowf = P("rowf", [1, 256], F32)
    rowb = P("rowb", [1, 256], BF16)
    c1col = P("c1col", [128, NT], F32)
    mcol16 = P("mcol16", [128, NT], F32)
    vcol16 = P("vcol16", [128, NT], F32)
    icol16 = P("icol16", [128, NT], F32)
    xrel = P("xrel", [128, D], F32)
    lnbuf, lnbuf_at, _ = place("lnbuf", [128, D], F32)
    q_i32 = place("q_i32", [128, D], I32, at=lnbuf_at)[0]
    sqb = P("sqb", [128, D], F32)
    ckb = P("ckb", [128, D], F32)

    ATT0 = (cur[0] + 63) & ~63
    a_cur = [ATT0]

    def PA(name, shape, dtype):
        at = (a_cur[0] + 31) & ~31
        ap, _, end = place(name, shape, dtype, at=at)
        a_cur[0] = end
        return ap

    qsqrow = PA("qsqrow", [1, S], BF16)
    ksqrow = PA("ksqrow", [1, S], BF16)
    sqb2a = PA("sqb2a", [128, 512], BF16)
    sqb2b = PA("sqb2b", [128, 512], BF16)
    lnta = PA("lnta", [128, D], BF16)
    lntb = PA("lntb", [128, D], BF16)
    rB = PA("rB", [128, CH], F32)
    sba = PA("sba", [128, CH], F32)
    sbb = PA("sbb", [128, CH], F32)
    eba = PA("eba", [128, CH], BF16)
    ebb = PA("ebb", [128, CH], BF16)
    xrel2 = PA("xrel2", [128, D], F32)
    bvrow = PA("bvrow", [1, D], BF16)
    borow = PA("borow", [1, D], BF16)

    f_cur = [ATT0]

    def PF(name, shape, dtype):
        at = (f_cur[0] + 31) & ~31
        ap, _, end = place(name, shape, dtype, at=at)
        f_cur[0] = end
        return ap

    YS = PF("YS", [128, HT * FC], F32)
    hqb = PF("hqb", [128, HT * FC], BF16)
    cva = PF("cva", [128, 128], F32)
    cvb = PF("cvb", [128, 128], F32)
    cvc = PF("cvc", [128, 128], F32)
    cvd = PF("cvd", [128, 128], F32)
    cve = PF("cve", [128, 128], F32)
    cvf = PF("cvf", [128, 128], F32)
    r2B = PF("r2B", [128, FC], F32)

    PB = (max(a_cur[0], f_cur[0]) + 63) & ~63
    kT = place("kT", [128, DT * S], BF16, at=PB)[0]
    vtok = place("vtok", [128, NT * D], BF16, at=PB + 32768)[0]
    w1qT = place("w1qT", [128, DT * H], BF16, at=PB)[0]
    BIGA = place("BIGA", [128, DT * BIGW], BF16, at=PB + 65536)[0]
    WB = (PB + 65536 + 2 * DT * BIGW + 63) & ~63
    Wbf = place("Wbf", [128, DT * D], BF16, at=WB)[0]
    outn = place("outn", [128, DT * CH], BF16, at=WB + 16384)[0]
    q1c = place("q1c", [128, D], F32, at=WB + 16384)[0]
    wstage = place("wstage", [128, D], F32, at=WB + 16384)[0]
    Wbf2 = place("Wbf2", [128, DT * D], BF16, at=WB + 24576)[0]
    w2qT = place("w2qT", [128, HT * D], BF16, at=WB)[0]
    endF2 = WB + HT * D * 2
    assert endF2 <= TOP, f"arena overflow endF2={endF2} > {TOP}"
    wst = place("wst", [128, 4 * D], F32, at=PB)[0]
    wst2 = place("wst2", [128, 4 * D], F32, at=PB + 16384)[0]
    tbuf = place("tbuf", [128, 4 * D], BF16, at=PB + 32768)[0]
    stg = place("stg", [128, 4 * D], BF16, at=PB + 40960)[0]
    tern2 = place("tern2", [128, 4 * D], BF16, at=PB + 49152)[0]

    pbk = [nc.alloc_psum_tensor(f"pb{i}", [128, 512], F32).ap() for i in range(8)]
    nc.bump_psum(banks=-2)
    ptb = [nc.alloc_psum_tensor(f"ptb{i}", [128, 1024], BF16).ap() for i in range(2)]

    s_ld = Sem(nc, "s_ld")
    s_sp = Sem(nc, "s_sp")
    s_out = Sem(nc, "s_out")
    s_pe = Sem(nc, "s_pe")
    s_v = Sem(nc, "s_v")
    s_s = Sem(nc, "s_s")
    s_g = Sem(nc, "s_g")

    (C_M1, C_M2, C_T1, C_NT1, C_T2, C_NT2, C_MEAN, C_VAR, C_INV, C_NMI,
     C_AMX, C_R127, C_C2, C_VE, C_Y, C_TA, C_TB, C_SUM, C_M127_2, C_M127_1,
     C_DMY) = range(21)

    def col(i):
        return colpk[:, i:i + 1]

    TBL_LNEXP = 6
    TBL_SILU = 18

    with nc.Block() as blk:
        PIN_TABLES = False

        def pin_table(s, idx):
            if not PIN_TABLES:
                return
            s.add_instruction(mybir.InstLoadActFuncSet(
                name=nc.get_next_instruction_name(), act_func_set_id=idx, ins=[], outs=[]))

        def emit_newton(g, ve_ap, y_ap, ta=None, tb=None, niter=14):
            ta = col(C_TA) if ta is None else ta
            tb = col(C_TB) if tb is None else tb
            g.tensor_single_scalar(y_ap, ve_ap, 2.0, ALU.is_lt)
            for lo, hi, val in ((2.0, 8.0, 0.5), (8.0, 128.0, 0.125),
                                (128.0, 2048.0, 0.03125)):
                g.tensor_single_scalar(ta, ve_ap, lo, ALU.is_ge)
                g.tensor_single_scalar(tb, ve_ap, hi, ALU.is_lt)
                g.tensor_mul(ta, ta, tb)
                g.tensor_scalar(ta, ta, val, None, ALU.mult)
                g.tensor_add(y_ap, y_ap, ta)
            g.tensor_single_scalar(ta, ve_ap, 2048.0, ALU.is_ge)
            g.tensor_scalar(ta, ta, 0.0078125, None, ALU.mult)
            g.tensor_add(y_ap, y_ap, ta)
            for _i in range(niter):
                g.tensor_mul(ta, y_ap, y_ap)
                g.tensor_mul(ta, ta, ve_ap)
                g.tensor_scalar(ta, ta, -0.5, 1.5, ALU.mult, ALU.add)
                g.tensor_mul(y_ap, y_ap, ta)

        def vgap(v, n=1):
            for _ in range(n):
                v.tensor_single_scalar(col(C_DMY), epscol, 1.0, ALU.mult)

        def tap(name, src_ap):
            if not DBG:
                return
            h = nc.dram_tensor(name, list(src_ap.shape), src_ap.dtype, kind="ExternalOutput").ap()
            @blk.gpsimd
            def _(g):
                s_v.wait(g)
                s_s.wait(g)
                s_pe.wait(g)
                s_out.dinc(g.dma_start(h, src_ap))

        @blk.sync
        def _(sync):
            s_ld.dinc(sync.dma_start(idf, idf_h))
            s_ld.dinc(sync.dma_start(idb, idb_h))
            s_ld.dinc(sync.dma_start(bqcol, bq_h.rearrange("(t p) -> p t", p=128), allow_slow_non_contiguous=True))
            s_ld.dinc(sync.dma_start(bkcol, bk_h.rearrange("(t p) -> p t", p=128), allow_slow_non_contiguous=True))
            s_ld.dinc(sync.dma_start(cwcol.rearrange("p (t k) -> p t k", k=3), cw_h.rearrange("(t p) k -> p t k", p=128), allow_slow_non_contiguous=True))
            s_ld.dinc(sync.dma_start(cbcol, cb_h.rearrange("(t p) -> p t", p=128), allow_slow_non_contiguous=True))

        @blk.vector
        def _(v):
            s_ld.wait(v)
            v.memset(onescol_b, 1.0)
            v.memset(onesrow_b, 1.0)
            v.memset(onesrow_f, 1.0)
            s_v.inc(v.memset(epscol, EPS))

        for wi in range(2):
            w_src = w1_h if wi == 0 else w2_h
            nrow = H if wi == 0 else D
            NCHW = 8
            mcol_i = C_M1 if wi == 0 else C_M2
            tcol_i = C_T1 if wi == 0 else C_T2
            ntcol_i = C_NT1 if wi == 0 else C_NT2
            wview = w_src.rearrange("(c q p) d -> c p q d", c=NCHW, p=128)
            for ch in range(NCHW):
                buf = wst if ch % 2 == 0 else wst2
                @blk.sync
                def _(sync):
                    s_v.wait(sync)
                    s_ld.dinc(sync.dma_start(buf.rearrange("p (q d) -> p q d", d=D), wview[ch]))
                @blk.vector
                def _(v, ch=ch, buf=buf):
                    s_ld.wait(v)
                    if ch == 0:
                        s_v.inc(v.reduce_sum(col(C_TA), buf, AX.X, apply_absolute_value=True))
                    else:
                        v.reduce_sum(col(C_TB), buf, AX.X, apply_absolute_value=True)
                        vgap(v)
                        vgap(v)
                        s_v.inc(v.tensor_add(col(C_TA), col(C_TA), col(C_TB)))
            @blk.tensor
            def _(te):
                s_v.wait(te)
                s_pe.inc(te.matmul(pbk[6][0:1, 0:128], col(C_TA), idf, start=True, stop=True,
                                   skip_group_check=True))
            @blk.scalar
            def _(s, wi=wi):
                if wi == 0:
                    pin_table(s, TBL_LNEXP)
                s_pe.wait(s)
                s_s.inc(s.copy(rowf[0:1, 0:128], pbk[6][0:1, 0:128]))
            @blk.vector
            def _(v):
                s_s.wait(v)
                s_v.inc(v.reduce_sum(rowf[0:1, 128:129], rowf[0:1, 0:128], AX.X))
            @blk.tensor
            def _(te):
                s_v.wait(te)
                s_pe.inc(te.matmul(pbk[6][:, 128:129], onesrow_f, rowf[0:1, 128:129],
                                   start=True, stop=True, skip_group_check=True))
            @blk.vector
            def _(v, nrow=nrow, mcol_i=mcol_i, tcol_i=tcol_i, ntcol_i=ntcol_i, wi=wi):
                s_pe.wait(v)
                v.tensor_single_scalar(col(mcol_i), pbk[6][:, 128:129],
                                       1.0 / (nrow * (H * D // nrow)), ALU.mult)
                vgap(v)
                vgap(v)
                v.tensor_single_scalar(col(mcol_i), col(mcol_i), EPS, ALU.max)
                vgap(v)
                vgap(v)
                v.tensor_single_scalar(col(tcol_i), col(mcol_i), 0.5, ALU.mult)
                vgap(v)
                vgap(v)
                v.tensor_single_scalar(col(ntcol_i), col(tcol_i), -1.0, ALU.mult)
                s_v.inc(v.tensor_single_scalar(col(C_M127_2 if wi == 1 else C_M127_1),
                                                col(mcol_i), 1.0 / 127.0, ALU.mult))
            for ch in range(NCHW):
                buf = wst if ch % 2 == 0 else wst2
                @blk.sync
                def _(sync):
                    s_v.wait(sync)
                    s_ld.dinc(sync.dma_start(buf.rearrange("p (q d) -> p q d", d=D), wview[ch]))
                @blk.vector
                def _(v, ch=ch, buf=buf, tcol_i=tcol_i, ntcol_i=ntcol_i, wi=wi):
                    s_ld.wait(v)
                    if ch > 0 or wi > 0:
                        s_sp.wait(v)
                    v.tensor_scalar(tbuf, buf, col(tcol_i), None, ALU.is_ge)
                    v.tensor_scalar(tern2, buf, col(ntcol_i), None, ALU.is_le)
                    vgap(v)
                    s_v.inc(v.tensor_sub(tbuf, tbuf, tern2))
                for q in range(4):
                    for dt in range(DT):
                        bank = ptb[1][:, (dt % 4) * 128:(dt % 4) * 128 + 128]
                        @blk.tensor
                        def _(te, q=q, dt=dt, bank=bank):
                            s_v.wait(te)
                            s_pe.inc(te.transpose(bank, tbuf[:, q * D + dt * 128: q * D + dt * 128 + 128], idb))
                        @blk.vector
                        def _(v, q=q, dt=dt, bank=bank, wi=wi):
                            s_pe.wait(v)
                            if wi == 0:
                                dst = stg[:, dt * 512 + q * 128: dt * 512 + q * 128 + 128]
                            else:
                                dst = stg[:, (q * DT + dt) * 128: (q * DT + dt) * 128 + 128]
                            s_v.inc(v.tensor_copy(dst, bank))
                @blk.gpsimd
                def _(g, ch=ch, wi=wi):
                    s_v.wait(g)
                    if wi == 0:
                        dst = w1qT_h.rearrange("p (dt u) -> p dt u", dt=DT)[:, :, ch * 512:(ch + 1) * 512]
                        s_sp.dinc(g.dma_start(dst, stg.rearrange("p (dt u) -> p dt u", dt=DT)))
                    else:
                        dst = w2qT_h.rearrange("p (ht dd) -> p ht dd", ht=HT)[:, :, ch * 128:(ch + 1) * 128]
                        s_sp.dinc(g.dma_start(dst, stg.rearrange("p (ht j) -> p ht j", ht=HT)))

        def emit_var(v, src):
            v.reduce_sum(col(C_MEAN), src, AX.X)
            v.reduce_sum(col(C_VAR), sqb, AX.X)
            v.tensor_single_scalar(col(C_MEAN), col(C_MEAN), 1.0 / D, ALU.mult)
            v.tensor_single_scalar(col(C_VAR), col(C_VAR), 1.0 / D, ALU.mult)
            v.tensor_mul(col(C_SUM), col(C_MEAN), col(C_MEAN))
            vgap(v, 2)
            return v.tensor_sub(col(C_VAR), col(C_VAR), col(C_SUM))

        def emit_stats(src):
            @blk.scalar
            def _(s):
                s_v.wait(s)
                s_s.inc(s.activation(sqb, src, AF.Square))
            @blk.vector
            def _(v):
                s_s.wait(v)
                emit_var(v, src)
                vgap(v, 2)
                s_v.inc(v.tensor_single_scalar(col(C_VE), col(C_VAR), EPS, ALU.add))
            @blk.gpsimd
            def _(g):
                s_v.wait(g)
                emit_newton(g, col(C_VE), col(C_Y))
                s_g.inc(g.tensor_mul(col(C_NMI), col(C_MEAN), col(C_Y)))
            @blk.vector
            def _(v):
                s_g.wait(v)
                s_v.inc(v.tensor_single_scalar(col(C_NMI), col(C_NMI), -1.0, ALU.mult))

        tap("d_m", colpk)

        def emit_sqrow(slc, dstrow, pe_ready):
            _sb = s_s.n
            _pb = s_pe.n
            bank = pbk[4][0:1, 0:512]
            @blk.scalar
            def _(s):
                s.wait_ge(s_pe.h, pe_ready)
                k = 0
                for ncv in range(4):
                    for dt in range(DT):
                        sq = sqb2a if dt % 2 == 0 else sqb2b
                        if k >= 2:
                            s.wait_ge(s_pe.h, _pb + (k - 1))
                        s_s.inc(s.activation(sq, slc(dt, ncv * 512, ncv * 512 + 512), AF.Square))
                        k += 1
                    s.wait_ge(s_pe.h, _pb + (ncv + 1) * DT)
                    s_s.inc(s.activation(dstrow[0:1, ncv * 512:(ncv + 1) * 512], bank, AF.Identity,
                                         bias=0.0, scale=-0.5))
            @blk.tensor
            def _(te):
                k = 0
                for ncv in range(4):
                    for dt in range(DT):
                        sq = sqb2a if dt % 2 == 0 else sqb2b
                        te.wait_ge(s_s.h, _sb + k + 1 + ncv)
                        s_pe.inc(te.matmul(bank, onescol_b, sq, start=(dt == 0), stop=(dt == DT - 1),
                                           skip_group_check=True))
                        k += 1

        def batch(b):
            xr = x_h[b].rearrange("(t p) d -> t p d", p=128)

            @blk.sync
            def _(sync):
                s_s.wait(sync)
                s_ld.dinc(sync.dma_start(ckb[0:1, 0:D], bv_h.rearrange("d -> () d")))
                s_ld.dinc(sync.dma_start(sqb[0:1, 0:D], bo_h.rearrange("d -> () d")))
            @blk.vector
            def _(v):
                s_pe.wait(v)
                s_ld.wait(v)
                for dt in range(DT):
                    v.memset(BIGA[:, dt * BIGW: dt * BIGW + 1], 0.0)
                v.tensor_copy(bvrow, ckb[0:1, 0:D])
                v.tensor_copy(borow, sqb[0:1, 0:D])
                s_v.inc(v.memset(BIGA[:, (DT - 1) * BIGW + 2049: (DT - 1) * BIGW + 2050], 0.0))
            @blk.vector
            def _(v):
                for dt in range(DT - 1):
                    v.memset(BIGA[:, dt * BIGW + 2049: dt * BIGW + 2050], 0.0)
                s_v.inc(v.tensor_single_scalar(col(C_DMY), epscol, 1.0, ALU.mult))

            for t in range(NT):
                xb = xrel if t % 2 == 0 else xrel2
                @blk.sync
                def _(sync, t=t, xb=xb):
                    s_s.wait(sync)
                    s_v.wait(sync)
                    s_ld.dinc(sync.dma_start(xb, xr[t]))
                @blk.scalar
                def _(s, xb=xb, t=t):
                    s_ld.wait(s)
                    s_s.inc(s.activation(sqb, xb, AF.Square))
                @blk.vector
                def _(v, xb=xb, t=t):
                    s_s.wait(v)
                    v.reduce_sum(mcol16[:, t:t + 1], xb, AX.X)
                    v.reduce_sum(vcol16[:, t:t + 1], sqb, AX.X)
                    s_v.inc(v.tensor_single_scalar(mcol16[:, t:t + 1], mcol16[:, t:t + 1],
                                                   1.0 / D, ALU.mult))
            @blk.vector
            def _(v):
                v.tensor_single_scalar(vcol16, vcol16, 1.0 / D, ALU.mult)
                v.tensor_mul(icol16, mcol16, mcol16)
                vgap(v, 2)
                v.tensor_sub(vcol16, vcol16, icol16)
                vgap(v, 2)
                s_v.inc(v.tensor_single_scalar(vcol16, vcol16, EPS, ALU.add))
            @blk.gpsimd
            def _(g):
                s_v.wait(g)
                emit_newton(g, vcol16, icol16, sqb[:, 0:16], sqb[:, 16:32])
                s_g.inc(g.tensor_mul(mcol16, mcol16, icol16))
            @blk.vector
            def _(v):
                s_g.wait(v)
                s_v.inc(v.tensor_single_scalar(mcol16, mcol16, -1.0, ALU.mult))
            for t in range(NT):
                xb = xrel if t % 2 == 0 else xrel2
                lb = lnta if t % 2 == 0 else lntb
                @blk.sync
                def _(sync, t=t, xb=xb):
                    s_s.wait(sync)
                    s_v.wait(sync)
                    s_ld.dinc(sync.dma_start(xb, xr[t]))
                @blk.scalar
                def _(s, xb=xb, lb=lb, t=t):
                    s_ld.wait(s)
                    s_s.inc(s.activation(lb, xb, AF.Identity, bias=mcol16[:, t:t + 1],
                                         scale=icol16[:, t:t + 1]))
                for dt in range(DT):
                    bank = ptb[dt % 2][:, 0:128]
                    @blk.tensor
                    def _(te, lb=lb, dt=dt, bank=bank):
                        s_s.wait(te)
                        s_v.wait(te)
                        s_pe.inc(te.transpose(bank, lb[:, dt * 128: dt * 128 + 128], idb))
                    @blk.vector
                    def _(v, t=t, dt=dt, bank=bank):
                        s_pe.wait(v)
                        s_v.inc(v.tensor_copy(
                            BIGA[:, dt * BIGW + 1 + t * 128: dt * BIGW + 1 + t * 128 + 128], bank))

            def load_weight(w_src, dst, pe_guard=None):
                for dt in range(DT):
                    @blk.sync
                    def _(sync, dt=dt, w_src=w_src, pe_guard=pe_guard):
                        s_v.wait(sync)
                        if dt == 0 and pe_guard is not None:
                            sync.wait_ge(s_pe.h, pe_guard)
                        elif dt == 0:
                            s_pe.wait(sync)
                        s_ld.dinc(sync.dma_start(wstage, w_src[dt * 128:(dt + 1) * 128, :]))
                    @blk.vector
                    def _(v, dt=dt, dst=dst):
                        s_ld.wait(v)
                        s_v.inc(v.tensor_copy(dst[:, dt * D:(dt + 1) * D], wstage))

            load_weight(Wv_h, Wbf)
            _vev = {}
            @blk.tensor
            def _(te):
                s_v.wait(te)
                for mt in range(NT):
                    for ncv in range(2):
                        bank = pbk[(mt * 2 + ncv) % 4][:, 0:512]
                        k = mt * 2 + ncv
                        if k >= 4:
                            te.wait_ge(s_s.h, _vev[k - 4])
                        for dt in range(DT):
                            te.matmul(bank, BIGA[:, dt * BIGW + 1 + mt * 128: dt * BIGW + 1 + mt * 128 + 128],
                                      Wbf[:, dt * D + ncv * 512: dt * D + ncv * 512 + 512],
                                      start=(dt == 0), stop=False, skip_group_check=True)
                        s_pe.inc(te.matmul(bank, onesrow_b[0:1, 0:128], bvrow[:, ncv * 512: ncv * 512 + 512],
                                           start=False, stop=True, skip_group_check=True))
                        _vev[k] = s_s.n + k + 1
            m_vproj = s_pe.n
            @blk.scalar
            def _(s):
                base = s_pe.n - NT * 2
                for mt in range(NT):
                    for ncv in range(2):
                        k = mt * 2 + ncv
                        bank = pbk[k % 4][:, 0:512]
                        s.wait_ge(s_pe.h, base + k + 1)
                        s_s.inc(s.copy(vtok[:, mt * D + ncv * 512: mt * D + ncv * 512 + 512], bank))

            load_weight(Wk_h, Wbf2)
            _kev = {}
            _kpe0 = s_pe.n
            @blk.tensor
            def _(te):
                s_v.wait(te)
                for mt in range(DT):
                    for ncv in range(4):
                        k = mt * 4 + ncv
                        bank = pbk[k % 4][:, 0:512]
                        if k >= 4:
                            te.wait_ge(s_s.h, _kev[k - 4])
                        ii = None
                        for dt in range(DT):
                            ii = te.matmul(bank, Wbf2[:, dt * D + mt * 128: dt * D + mt * 128 + 128],
                                           BIGA[:, dt * BIGW + 1 + ncv * 512: dt * BIGW + 1 + ncv * 512 + 512],
                                           start=(dt == 0), stop=(dt == DT - 1), skip_group_check=True)
                        s_pe.inc(ii)
                        _kev[k] = s_s.n + k + 1
            m_kproj = s_pe.n
            @blk.scalar
            def _(s, _kpe0=_kpe0):
                for mt in range(DT):
                    for ncv in range(4):
                        k = mt * 4 + ncv
                        bank = pbk[k % 4][:, 0:512]
                        s.wait_ge(s_pe.h, _kpe0 + k + 1)
                        s_s.inc(s.activation(kT[:, mt * S + ncv * 512: mt * S + ncv * 512 + 512], bank,
                                             AF.Identity, bias=bkcol[:, mt:mt + 1], scale=1.0))
            emit_sqrow(lambda dt, lo, hi: kT[:, dt * S + lo: dt * S + hi], ksqrow, m_kproj)

            load_weight(Wq_h, Wbf, pe_guard=m_vproj)
            for ncv in range(4):
                @blk.tensor
                def _(te, ncv=ncv):
                    if ncv == 0:
                        s_v.wait(te)
                    s_s.wait(te)
                    ii = None
                    for mt in range(DT):
                        for dt in range(DT):
                            ii = te.matmul(pbk[mt][:, 0:512],
                                           Wbf[:, dt * D + mt * 128: dt * D + mt * 128 + 128],
                                           BIGA[:, dt * BIGW + 1 + ncv * 512: dt * BIGW + 1 + ncv * 512 + 512],
                                           start=(dt == 0), stop=(dt == DT - 1), skip_group_check=True)
                    s_pe.inc(ii)
                @blk.scalar
                def _(s, ncv=ncv):
                    s_pe.wait(s)
                    ii = None
                    for mt in range(DT):
                        ii = s.activation(BIGA[:, mt * BIGW + 1 + ncv * 512: mt * BIGW + 1 + ncv * 512 + 512],
                                          pbk[mt][:, 0:512], AF.Identity, bias=bqcol[:, mt:mt + 1], scale=1.0)
                    s_s.inc(ii)
            m_qproj = s_pe.n
            emit_sqrow(lambda dt, lo, hi: BIGA[:, dt * BIGW + 1 + lo: dt * BIGW + 1 + hi],
                       qsqrow, m_qproj)

            load_weight(Wo_h, Wbf2, pe_guard=m_kproj)
            if b == 0:
                tap("d_kt", kT)
                tap("d_vt", vtok)
                tap("d_q", BIGA)
                tap("d_qsq", qsqrow)
                tap("d_ksq", ksqrow)
                tap("d_wbf", Wbf)

            m_qk = {}
            m_exp = {}
            m_ev = {}
            m_evac = {}
            m_ckb = {}
            for c in range(NCH):
                opsum = [pbk[2 + i // 2][:, (i % 2) * CH:(i % 2) * CH + CH] for i in range(DT)]
                dn = pbk[6][0:1, 0:CH]
                _n = s_pe.n
                for step in range(NT + 1):
                    if step < NT:
                        _n += 1
                        m_qk[(c, step)] = _n
                    if step >= 1:
                        _n += 1
                        m_ev[(c, step - 1)] = _n
                for jt in range(NT):
                    m_exp[(c, jt)] = s_s.n + jt + 1
                @blk.tensor
                def _(te, c=c):
                    for step in range(NT + 1):
                        jt = step
                        if jt < NT:
                            qk = pbk[jt % 2][:, 0:CH]
                            prev = (c, jt - 2) if jt >= 2 else (c - 1, NT + jt - 2)
                            if prev in m_exp and (prev[0] >= 0):
                                te.wait_ge(s_s.h, m_exp[prev])
                            if jt == 0 and c > 0:
                                te.wait_ge(s_v.h, m_evac[c - 1])
                                te.wait_ge(s_s.h, m_ckb[c - 1])
                            for dt in range(DT):
                                te.matmul(qk, kT[:, dt * S + jt * 128: dt * S + jt * 128 + 128],
                                          BIGA[:, dt * BIGW + 1 + c * CH: dt * BIGW + 1 + c * CH + CH],
                                          start=(dt == 0), stop=False, skip_group_check=True)
                            te.matmul(qk, onesrow_b[0:1, 0:128], qsqrow[0:1, c * CH: c * CH + CH],
                                      start=False, stop=False, skip_group_check=True)
                            s_pe.inc(te.matmul(qk, ksqrow[0:1, jt * 128: jt * 128 + 128],
                                               onesrow_b[0:1, 0:CH],
                                               start=False, stop=True, skip_group_check=True))
                            assert s_pe.n == m_qk[(c, jt)], (s_pe.n, m_qk[(c, jt)])
                        if step >= 1:
                            jt1 = step - 1
                            eb1 = eba if jt1 % 2 == 0 else ebb
                            te.wait_ge(s_s.h, m_exp[(c, jt1)])
                            for i in range(DT):
                                te.matmul(opsum[i], vtok[:, jt1 * D + i * 128: jt1 * D + i * 128 + 128],
                                          eb1, start=(jt1 == 0 and i % 2 == 0), stop=(jt1 == NT - 1),
                                          skip_group_check=True)
                            s_pe.inc(te.matmul(dn, onescol_b, eb1, start=(jt1 == 0), stop=(jt1 == NT - 1),
                                               skip_group_check=True))
                            assert s_pe.n == m_ev[(c, jt1)], (s_pe.n, m_ev[(c, jt1)])
                @blk.scalar
                def _(s, c=c):
                    for jt in range(NT):
                        qk = pbk[jt % 2][:, 0:CH]
                        sb_ = sba if jt % 2 == 0 else sbb
                        eb_ = eba if jt % 2 == 0 else ebb
                        s.wait_ge(s_pe.h, m_qk[(c, jt)])
                        prev = (c, jt - 2) if jt >= 2 else (c - 1, NT + jt - 2)
                        if prev in m_ev and prev[0] >= 0:
                            s.wait_ge(s_pe.h, m_ev[prev])
                        s.activation(sb_, qk, AF.Exp, bias=0.0, scale=2.0)
                        s_s.inc(s.activation(eb_, sb_, AF.Exp, bias=0.0, scale=1.0))
                        assert s_s.n == m_exp[(c, jt)]
                if b == 0 and c == 0:
                    tap("d_sb", sbb)
                    tap("d_eb", ebb)
                @blk.vector
                def _(v):
                    s_pe.wait(v)
                    v.reciprocal(rowf[0:1, 0:CH], dn)
                    vgap(v)
                    vgap(v)
                    s_v.inc(v.tensor_copy(rowb[0:1, 0:CH], rowf[0:1, 0:CH]))
                @blk.tensor
                def _(te):
                    s_v.wait(te)
                    s_pe.inc(te.matmul(pbk[6][:, CH:CH + CH], onesrow_b[0:1, 0:128], rowb[0:1, 0:CH],
                                       start=True, stop=True, skip_group_check=True))
                @blk.vector
                def _(v, c=c):
                    s_pe.wait(v)
                    v.tensor_copy(rB, pbk[6][:, CH:CH + CH])
                    vgap(v)
                    vgap(v)
                    ii = None
                    for i in range(DT):
                        ii = v.tensor_mul(outn[:, i * CH:(i + 1) * CH], opsum[i], rB)
                    s_v.inc(ii)
                    m_evac[c] = s_v.mark()
                if b == 0 and c == 0:
                    tap("d_outn", outn)
                    tap("d_rb", rB)
                    tap("d_rowf", rowf)
                for imt in range(2):
                    aop = [pbk[2 + 2 * imt + ncv][:, 0:512] for ncv in range(2)]
                    @blk.tensor
                    def _(te, imt=imt, aop=aop):
                        s_v.wait(te)
                        ii = None
                        for ncv in range(2):
                            for i in range(DT):
                                te.matmul(aop[ncv], outn[:, i * CH + imt * 128: i * CH + imt * 128 + 128],
                                          Wbf2[:, i * D + ncv * 512: i * D + ncv * 512 + 512],
                                          start=(i == 0), stop=False, skip_group_check=True)
                            ii = te.matmul(aop[ncv], onesrow_b[0:1, 0:128], borow[:, ncv * 512: ncv * 512 + 512],
                                           start=False, stop=True, skip_group_check=True)
                        s_pe.inc(ii)
                for imt in range(2):
                    aop = [pbk[2 + 2 * imt + ncv][:, 0:512] for ncv in range(2)]
                    @blk.scalar
                    def _(s, c=c, aop=aop):
                        s_pe.wait(s)
                        s_v.wait(s)
                        s.copy(ckb[:, 0:512], aop[0])
                        s_s.inc(s.copy(ckb[:, 512:1024], aop[1]))
                        m_ckb[c] = s_s.mark()
                    t_glob = c * 2 + imt
                    xb = xrel if t_glob % 2 == 0 else xrel2
                    @blk.sync
                    def _(sync, t_glob=t_glob, xb=xb):
                        s_sp.wait(sync)
                        s_v.wait(sync)
                        s_ld.dinc(sync.dma_start(xb, xr[t_glob]))
                    emit_stats(ckb)
                    @blk.scalar
                    def _(s):
                        s_v.wait(s)
                        s_s.inc(s.activation(lnbuf, ckb, AF.Identity, bias=col(C_NMI), scale=col(C_Y)))
                    @blk.vector
                    def _(v, xb=xb):
                        s_s.wait(v)
                        s_ld.wait(v)
                        s_v.inc(v.tensor_add(xb, xb, lnbuf))
                    if b == 0 and c == 0:
                        tap("d_ao" + str(imt), ckb)
                        tap("d_xnew" + str(imt), xb)
                    @blk.gpsimd
                    def _(g, t_glob=t_glob, xb=xb):
                        s_v.wait(g)
                        s_sp.dinc(g.dma_start(xsp_h.rearrange("(t p) d -> t p d", p=128)[t_glob], xb))
                    emit_stats(xb)
                    @blk.scalar
                    def _(s, xb=xb):
                        s_v.wait(s)
                        s_s.inc(s.activation(lnbuf, xb, AF.Identity, bias=col(C_NMI), scale=col(C_Y)))
                    @blk.vector
                    def _(v, t_glob=t_glob):
                        s_s.wait(v)
                        v.reduce_max(col(C_AMX), lnbuf, AX.X, apply_absolute_value=True)
                        vgap(v)
                        vgap(v)
                        v.tensor_single_scalar(col(C_AMX), col(C_AMX), EPS, ALU.max)
                        vgap(v)
                        vgap(v)
                        v.reciprocal(col(C_C2), col(C_AMX))
                        v.tensor_mul(c1col[:, t_glob:t_glob + 1], col(C_AMX), col(C_M127_1))
                        vgap(v)
                        v.tensor_single_scalar(col(C_R127), col(C_C2), 127.0, ALU.mult)
                        vgap(v, 2)
                        v.tensor_scalar(sqb, lnbuf, col(C_R127), 128.0, ALU.mult, ALU.add)
                        vgap(v)
                        v.tensor_copy(q_i32, sqb)
                        vgap(v)
                        v.tensor_copy(q1c, q_i32)
                        vgap(v)
                        v.scalar_tensor_tensor(ckb, sqb, 0.5, q1c, ALU.add, ALU.subtract)
                        vgap(v)
                        v.tensor_single_scalar(ckb, ckb, 1.0, ALU.is_ge)
                        vgap(v)
                        v.scalar_tensor_tensor(q1c, q1c, -128.0, ckb, ALU.add, ALU.add)
                        vgap(v)
                        s_v.inc(v.tensor_scalar(lnta if t_glob % 2 == 0 else lntb, q1c,
                                                c1col[:, t_glob:t_glob + 1], None, ALU.mult))
                    @blk.gpsimd
                    def _(g, t_glob=t_glob):
                        s_v.wait(g)
                        s_sp.dinc(g.dma_start(xq_h.rearrange("(t p) d -> t p d", p=128)[t_glob],
                                             lnta if t_glob % 2 == 0 else lntb))

            @blk.sync
            def _(sync):
                s_pe.wait(sync)
                s_sp.wait(sync)
                s_s.wait(sync)
                s_ld.dinc(sync.dma_start(w1qT, w1qT_h))
                s_ld.dinc(sync.dma_start(w2qT, w2qT_h))
            for t in range(NT):
                lb = lnta if t % 2 == 0 else lntb
                @blk.sync
                def _(sync, t=t, lb=lb):
                    s_v.wait(sync)
                    s_ld.dinc(sync.dma_start(lb, xq_h.rearrange("(t p) d -> t p d", p=128)[t]))
                for dt in range(DT):
                    bank = ptb[dt % 2][:, 0:128]
                    @blk.tensor
                    def _(te, dt=dt, lb=lb, bank=bank):
                        s_ld.wait(te)
                        s_v.wait(te)
                        s_pe.inc(te.transpose(bank, lb[:, dt * 128: dt * 128 + 128], idb))
                    @blk.vector
                    def _(v, t=t, dt=dt, bank=bank):
                        s_pe.wait(v)
                        s_v.inc(v.tensor_copy(
                            BIGA[:, dt * BIGW + 1 + t * 128: dt * BIGW + 1 + t * 128 + 128], bank))

            if b == 0:
                tap("d_xqt", BIGA)
                tap("d_w1", w1qT)
                tap("d_w2", w2qT)

            m_u2 = {}
            m_silu = {}
            m_hmm = {}
            for c in range(NFC):
                _vb = s_v.n
                for s_i in range(HT + 2):
                    if s_i - 2 >= 0:
                        m_u2[(c, s_i - 2)] = _vb + s_i + 1
                _sb = s_s.n
                for ht in range(HT):
                    m_silu[(c, ht)] = _sb + ht + 1
                @blk.tensor
                def _(te, c=c):
                    if c == 0:
                        te.wait_ge(s_ld.h, s_ld.n)
                        te.wait_ge(s_v.h, _vb)
                        te.wait_ge(s_s.h, _sb)
                    for ht in range(HT):
                        hb = pbk[ht % 4][:, 0:130]
                        prev = (c, ht - 4) if ht >= 4 else (c - 1, HT + ht - 4)
                        if prev in m_u2 and prev[0] >= 0:
                            te.wait_ge(s_v.h, m_u2[prev])
                        ii = None
                        for dt in range(DT):
                            ii = te.matmul(hb, w1qT[:, dt * H + ht * 128: dt * H + ht * 128 + 128],
                                           BIGA[:, dt * BIGW + c * FC: dt * BIGW + c * FC + 130],
                                           start=(dt == 0), stop=(dt == DT - 1), skip_group_check=True)
                        s_pe.inc(ii)
                        m_hmm[(c, ht)] = s_pe.n
                @blk.vector
                def _(v, c=c, _vb=_vb):
                    for s_i in range(HT + 2):
                        ht0, ht1, ht2 = s_i, s_i - 1, s_i - 2
                        did = None
                        if ht0 < HT:
                            v.wait_ge(s_pe.h, m_hmm[(c, ht0)])
                            hb = pbk[ht0 % 4][:, 0:130]
                            did = v.tensor_scalar(cva if ht0 % 2 == 0 else cvb, hb[:, 0:128],
                                                  cwcol[:, ht0 * 3: ht0 * 3 + 1], None, ALU.mult)
                        if 0 <= ht1 < HT:
                            hb = pbk[ht1 % 4][:, 0:130]
                            did = v.scalar_tensor_tensor(cvc if ht1 % 2 == 0 else cvd, hb[:, 1:129],
                                                         cwcol[:, ht1 * 3 + 1: ht1 * 3 + 2],
                                                         cva if ht1 % 2 == 0 else cvb,
                                                         ALU.mult, ALU.add)
                        if 0 <= ht2 < HT:
                            if (c, ht2 - 2) in m_silu:
                                v.wait_ge(s_s.h, m_silu[(c, ht2 - 2)])
                            hb = pbk[ht2 % 4][:, 0:130]
                            did = v.scalar_tensor_tensor(cve if ht2 % 2 == 0 else cvf, hb[:, 2:130],
                                                         cwcol[:, ht2 * 3 + 2: ht2 * 3 + 3],
                                                         cvc if ht2 % 2 == 0 else cvd,
                                                         ALU.mult, ALU.add)
                        s_v.inc(did)
                        assert s_v.n == _vb + s_i + 1
                @blk.scalar
                def _(s, c=c):
                    for ht in range(HT):
                        s.wait_ge(s_v.h, m_u2[(c, ht)])
                        s_s.inc(s.activation(YS[:, ht * FC:(ht + 1) * FC],
                                             cve if ht % 2 == 0 else cvf, AF.Silu,
                                             bias=cbcol[:, ht:ht + 1], scale=1.0))
                        assert s_s.n == m_silu[(c, ht)]
                @blk.vector
                def _(v):
                    s_s.wait(v)
                    s_v.inc(v.reduce_max(cva, YS.rearrange("p (ht i) -> p i ht", ht=HT), AX.X,
                                         apply_absolute_value=True))
                @blk.tensor
                def _(te):
                    s_v.wait(te)
                    s_pe.inc(te.transpose(pbk[6][:, 0:128], cva, idf))
                @blk.vector
                def _(v):
                    s_pe.wait(v)
                    v.reduce_max(col(C_AMX), pbk[6][:, 0:128], AX.X)
                    vgap(v)
                    vgap(v)
                    v.tensor_single_scalar(col(C_AMX), col(C_AMX), EPS, ALU.max)
                    vgap(v)
                    vgap(v)
                    v.reciprocal(col(C_R127), col(C_AMX))
                    v.tensor_mul(col(C_C2), col(C_AMX), col(C_M127_2))
                    vgap(v)
                    s_v.inc(v.tensor_single_scalar(col(C_R127), col(C_R127), 127.0, ALU.mult))
                @blk.tensor
                def _(te):
                    s_v.wait(te)
                    s_pe.inc(te.matmul(pbk[6][0:1, 128:256], col(C_R127), idf, start=True, stop=True,
                                       skip_group_check=True))
                @blk.scalar
                def _(s):
                    s_pe.wait(s)
                    s_s.inc(s.copy(rowf[0:1, 0:FC], pbk[6][0:1, 128:256]))
                @blk.tensor
                def _(te):
                    s_s.wait(te)
                    s_pe.inc(te.matmul(pbk[6][:, 256:384], onesrow_f, rowf[0:1, 0:FC],
                                       start=True, stop=True, skip_group_check=True))
                @blk.vector
                def _(v):
                    s_pe.wait(v)
                    s_v.inc(v.tensor_copy(r2B, pbk[6][:, 256:384]))
                for qq in range(4):
                    sl = slice(qq * 8 * FC, (qq + 1) * 8 * FC)
                    @blk.vector
                    def _(v, sl=sl, c=c, qq=qq):
                        s_v.wait(v)
                        if c > 0 and qq == 0:
                            s_pe.wait(v)
                        ysq = YS[:, sl]
                        vgap(v)
                        v.tensor_tensor(ysq.rearrange("p (a i) -> p a i", a=8),
                                        ysq.rearrange("p (a i) -> p a i", a=8),
                                        r2B.rearrange("p i -> p () i").broadcast_to([128, 8, FC]),
                                        ALU.mult)
                        vgap(v)
                        v.tensor_single_scalar(ysq, ysq, 128.0, ALU.add)
                        vgap(v)
                        v.tensor_copy(q_i32, ysq)
                        vgap(v)
                        v.tensor_copy(sqb, q_i32)
                        vgap(v)
                        v.scalar_tensor_tensor(ysq, ysq, 0.5, sqb, ALU.add, ALU.subtract)
                        vgap(v)
                        v.tensor_single_scalar(ysq, ysq, 1.0, ALU.is_ge)
                        vgap(v)
                        s_v.inc(v.scalar_tensor_tensor(hqb[:, sl], sqb, -128.0, ysq, ALU.add, ALU.add))
                if b == 0 and c == 0:
                    tap("d_ys", YS)
                    tap("d_hq", hqb)
                    tap("d_c1", c1col)
                for ncv in range(2):
                    bank = pbk[4 + ncv][:, 0:512]
                    @blk.tensor
                    def _(te, c=c, ncv=ncv, bank=bank):
                        s_v.wait(te)
                        s_s.wait(te)
                        ii = None
                        for ht in range(HT):
                            ii = te.matmul(bank, hqb[:, ht * FC: ht * FC + 128],
                                           w2qT[:, ht * D + ncv * 512: ht * D + ncv * 512 + 512],
                                           start=(ht == 0), stop=(ht == HT - 1), skip_group_check=True)
                        s_pe.inc(ii)
                m_w2 = s_pe.n
                @blk.sync
                def _(sync, c=c):
                    s_v.wait(sync)
                    s_out.wait(sync, s_out.n)
                    s_ld.dinc(sync.dma_start(xrel, xsp_h.rearrange("(t p) d -> t p d", p=128)[c]))
                _sb2 = s_s.n
                _vb2 = s_v.n
                _outb = s_out.n
                @blk.scalar
                def _(s, c=c, m_w2=m_w2, _vb2=_vb2, _outb=_outb):
                    s.wait_ge(s_pe.h, m_w2)
                    s_s.inc(s.activation(ckb[:, 0:512], pbk[4][:, 0:512], AF.Identity,
                                         bias=0.0, scale=col(C_C2)))
                    s_s.inc(s.activation(ckb[:, 512:1024], pbk[5][:, 0:512], AF.Identity,
                                         bias=0.0, scale=col(C_C2)))
                    s_s.inc(s.activation(sqb, ckb, AF.Square))
                    s.wait_ge(s_v.h, _vb2 + 2)
                    s.wait_ge(s_out.h, _outb)
                    s_s.inc(s.activation(lnbuf, ckb, AF.Identity, bias=col(C_NMI), scale=col(C_Y)))
                m_sq = _sb2 + 3
                m_lnfo = _sb2 + 4
                @blk.vector
                def _(v, m_sq=m_sq):
                    v.wait_ge(s_s.h, m_sq)
                    emit_var(v, ckb)
                    vgap(v, 2)
                    s_v.inc(v.tensor_single_scalar(col(C_VE), col(C_VAR), EPS, ALU.add))
                @blk.gpsimd
                def _(g):
                    s_v.wait(g)
                    emit_newton(g, col(C_VE), col(C_Y))
                    s_g.inc(g.tensor_mul(col(C_NMI), col(C_MEAN), col(C_Y)))
                @blk.vector
                def _(v):
                    s_g.wait(v)
                    s_v.inc(v.tensor_single_scalar(col(C_NMI), col(C_NMI), -1.0, ALU.mult))
                @blk.vector
                def _(v, m_lnfo=m_lnfo):
                    v.wait_ge(s_s.h, m_lnfo)
                    s_ld.wait(v)
                    s_v.inc(v.tensor_add(lnbuf, lnbuf, xrel))
                @blk.gpsimd
                def _(g, b=b, c=c):
                    s_v.wait(g)
                    s_out.dinc(g.dma_start(y_h[b].rearrange("(t p) d -> t p d", p=128)[c], lnbuf))

        for b in range(NB):
            batch(b)

        @blk.gpsimd
        def _(g):
            g.wait_ge(s_out.h, s_out.n)

    return nc


def kernel(L0_attn_x, L0_attn_Wq, L0_attn_bq, L0_attn_Wk, L0_attn_bk, L0_attn_Wv,
           L0_attn_bv, L0_attn_Wo, L0_attn_bo, L0_ffn_w1, L0_ffn_conv_w,
           L0_ffn_conv_b, L0_ffn_w2):
    x = np.ascontiguousarray(np.asarray(L0_attn_x, dtype=np.float32))
    assert x.shape == (32, S, D)
    base = dict(
        Wq=np.ascontiguousarray(L0_attn_Wq, dtype=np.float32),
        Wk=np.ascontiguousarray(L0_attn_Wk, dtype=np.float32),
        Wv=np.ascontiguousarray(L0_attn_Wv, dtype=np.float32),
        Wo=np.ascontiguousarray(L0_attn_Wo, dtype=np.float32),
        bq=np.ascontiguousarray(L0_attn_bq, dtype=np.float32),
        bk=np.ascontiguousarray(L0_attn_bk, dtype=np.float32),
        bv=np.ascontiguousarray(L0_attn_bv, dtype=np.float32),
        bo=np.ascontiguousarray(L0_attn_bo, dtype=np.float32),
        w1=np.ascontiguousarray(L0_ffn_w1, dtype=np.float32),
        cw=np.ascontiguousarray(np.asarray(L0_ffn_conv_w, dtype=np.float32).reshape(H, 3)),
        cb=np.ascontiguousarray(L0_ffn_conv_b, dtype=np.float32),
        w2=np.ascontiguousarray(L0_ffn_w2, dtype=np.float32),
        idf=np.eye(128, dtype=np.float32),
        idb=np.eye(128, dtype=ml_dtypes.bfloat16),
    )
    nc = build_nc()
    in_maps = [dict(base, x=np.ascontiguousarray(x[4 * c:4 * c + 4])) for c in range(8)]
    res = run_bass_kernel_spmd(nc, in_maps, core_ids=list(range(8)))
    return np.concatenate([r["y"] for r in res.results], axis=0)


# version 07
# speedup vs baseline: 1.5965x; 1.0202x over previous
import numpy as np
import ml_dtypes
import concourse.bass as bass
import concourse.mybir as mybir
from concourse.bass_utils import run_bass_kernel_spmd

F32 = mybir.dt.float32
BF16 = mybir.dt.bfloat16
I32 = mybir.dt.int32
AF = mybir.ActivationFunctionType
ALU = mybir.AluOpType
AX = mybir.AxisListType

NB = 4
S = 2048
D = 1024
H = 4096
NT = S // 128
DT = D // 128
HT = H // 128
CH = 256
NCH = S // CH
FC = 128
NFC = S // FC
EPS = 1e-5
BIGW = 2050
DBG = False


class Sem:
    def __init__(self, nc, name):
        self.h = nc.alloc_semaphore(name)
        self.n = 0

    def inc(self, instr, k=1):
        self.n += k
        instr.then_inc(self.h, k)
        return instr

    def dinc(self, instr):
        self.n += 16
        instr.then_inc(self.h, 16)
        return instr

    def wait(self, eng, val=None):
        eng.wait_ge(self.h, self.n if val is None else val)

    def mark(self):
        return self.n


def build_nc():
    nc = bass.Bass("TRN2", target_bir_lowering=False)

    x_h = nc.dram_tensor("x", [NB, S, D], F32, kind="ExternalInput").ap()
    Wq_h = nc.dram_tensor("Wq", [D, D], F32, kind="ExternalInput").ap()
    Wk_h = nc.dram_tensor("Wk", [D, D], F32, kind="ExternalInput").ap()
    Wv_h = nc.dram_tensor("Wv", [D, D], F32, kind="ExternalInput").ap()
    Wo_h = nc.dram_tensor("Wo", [D, D], F32, kind="ExternalInput").ap()
    bq_h = nc.dram_tensor("bq", [D], F32, kind="ExternalInput").ap()
    bk_h = nc.dram_tensor("bk", [D], F32, kind="ExternalInput").ap()
    bv_h = nc.dram_tensor("bv", [D], F32, kind="ExternalInput").ap()
    bo_h = nc.dram_tensor("bo", [D], F32, kind="ExternalInput").ap()
    w1_h = nc.dram_tensor("w1", [H, D], F32, kind="ExternalInput").ap()
    cw_h = nc.dram_tensor("cw", [H, 3], F32, kind="ExternalInput").ap()
    cb_h = nc.dram_tensor("cb", [H], F32, kind="ExternalInput").ap()
    w2_h = nc.dram_tensor("w2", [D, H], F32, kind="ExternalInput").ap()
    idf_h = nc.dram_tensor("idf", [128, 128], F32, kind="ExternalInput").ap()
    idb_h = nc.dram_tensor("idb", [128, 128], BF16, kind="ExternalInput").ap()

    y_h = nc.dram_tensor("y", [NB, S, D], F32, kind="ExternalOutput").ap()

    w1qT_h = nc.dram_tensor("w1qT_h", [128, DT * H], BF16, kind="Internal").ap()
    w2qT_h = nc.dram_tensor("w2qT_h", [128, HT * D], BF16, kind="Internal").ap()
    xsp_h = nc.dram_tensor("xsp_h", [S, D], F32, kind="Internal").ap()
    xq_h = nc.dram_tensor("xq_h", [S, D], BF16, kind="Internal").ap()

    BASE = 16512
    TOP = 229344

    def esz(dt_):
        return 4 if dt_ in (F32, I32) else 2

    cur = [BASE]

    def place(name, shape, dtype, at=None):
        nbytes = int(np.prod(shape[1:])) * esz(dtype)
        if at is None:
            at = (cur[0] + 31) & ~31
            cur[0] = at + nbytes
        assert at + nbytes <= TOP, f"SBUF overflow at {name}: {at + nbytes} > {TOP}"
        return nc.alloc_sbuf_tensor_at(name, list(shape), dtype, offset=at).ap(), at, at + nbytes

    def P(name, shape, dtype):
        return place(name, shape, dtype)[0]

    idf = P("idf_s", [128, 128], F32)
    idb = P("idb_s", [128, 128], BF16)
    onescol_b = P("onescol_b", [128, 1], BF16)
    onesrow_b = P("onesrow_b", [1, 256], BF16)
    onesrow_f = P("onesrow_f", [1, 128], F32)
    epscol = P("epscol", [128, 1], F32)
    colpk = P("colpk", [128, 24], F32)
    bqcol = P("bqcol", [128, DT], F32)
    bkcol = P("bkcol", [128, DT], F32)
    cwcol = P("cwcol", [128, 3 * HT], F32)
    cbcol = P("cbcol", [128, HT], F32)
    rowf = P("rowf", [1, 256], F32)
    rowb = P("rowb", [1, 256], BF16)
    c1col = P("c1col", [128, NT], F32)
    mcol16 = P("mcol16", [128, NT], F32)
    vcol16 = P("vcol16", [128, NT], F32)
    icol16 = P("icol16", [128, NT], F32)
    xrel = P("xrel", [128, D], F32)
    lnbuf, lnbuf_at, _ = place("lnbuf", [128, D], F32)
    q_i32 = place("q_i32", [128, D], I32, at=lnbuf_at)[0]
    sqb = P("sqb", [128, D], F32)
    ckb = P("ckb", [128, D], F32)

    ATT0 = (cur[0] + 63) & ~63
    a_cur = [ATT0]

    def PA(name, shape, dtype):
        at = (a_cur[0] + 31) & ~31
        ap, _, end = place(name, shape, dtype, at=at)
        a_cur[0] = end
        return ap

    qsqrow = PA("qsqrow", [1, S], BF16)
    ksqrow = PA("ksqrow", [1, S], BF16)
    sqb2a = PA("sqb2a", [128, 512], BF16)
    sqb2b = PA("sqb2b", [128, 512], BF16)
    lnta = PA("lnta", [128, D], BF16)
    lntb = PA("lntb", [128, D], BF16)
    rB = PA("rB", [128, CH], F32)
    sba = PA("sba", [128, CH], F32)
    ebc = PA("ebc", [128, CH], BF16)
    eba = PA("eba", [128, CH], BF16)
    ebb = PA("ebb", [128, CH], BF16)
    xrel2 = PA("xrel2", [128, D], F32)
    bvrow = PA("bvrow", [1, D], BF16)
    borow = PA("borow", [1, D], BF16)

    f_cur = [ATT0]

    def PF(name, shape, dtype):
        at = (f_cur[0] + 31) & ~31
        ap, _, end = place(name, shape, dtype, at=at)
        f_cur[0] = end
        return ap

    YS = PF("YS", [128, HT * FC], F32)
    hqb = PF("hqb", [128, HT * FC], BF16)
    cva = PF("cva", [128, 128], F32)
    cvb = PF("cvb", [128, 128], F32)
    cvc = PF("cvc", [128, 128], F32)
    cvd = PF("cvd", [128, 128], F32)
    cve = PF("cve", [128, 128], F32)
    cvf = PF("cvf", [128, 128], F32)
    r2B = PF("r2B", [128, FC], F32)

    PB = (max(a_cur[0], f_cur[0]) + 63) & ~63
    kT = place("kT", [128, DT * S], BF16, at=PB)[0]
    vtok = place("vtok", [128, NT * D], BF16, at=PB + 32768)[0]
    w1qT = place("w1qT", [128, DT * H], BF16, at=PB)[0]
    BIGA = place("BIGA", [128, DT * BIGW], BF16, at=PB + 65536)[0]
    WB = (PB + 65536 + 2 * DT * BIGW + 63) & ~63
    Wbf = place("Wbf", [128, DT * D], BF16, at=WB)[0]
    outn = place("outn", [128, DT * CH], BF16, at=WB + 16384)[0]
    q1c = place("q1c", [128, D], F32, at=WB + 16384)[0]
    wstage = place("wstage", [128, D], F32, at=WB + 16384)[0]
    Wbf2 = place("Wbf2", [128, DT * D], BF16, at=WB + 24576)[0]
    w2qT = place("w2qT", [128, HT * D], BF16, at=WB)[0]
    endF2 = WB + HT * D * 2
    assert endF2 <= TOP, f"arena overflow endF2={endF2} > {TOP}"
    wst = place("wst", [128, 4 * D], F32, at=PB)[0]
    wst2 = place("wst2", [128, 4 * D], F32, at=PB + 16384)[0]
    tbuf = place("tbuf", [128, 4 * D], BF16, at=PB + 32768)[0]
    stg = place("stg", [128, 4 * D], BF16, at=PB + 40960)[0]
    tern2 = place("tern2", [128, 4 * D], BF16, at=PB + 49152)[0]

    pbk = [nc.alloc_psum_tensor(f"pb{i}", [128, 512], F32).ap() for i in range(8)]
    nc.bump_psum(banks=-2)
    ptb = [nc.alloc_psum_tensor(f"ptb{i}", [128, 1024], BF16).ap() for i in range(2)]

    s_ld = Sem(nc, "s_ld")
    s_sp = Sem(nc, "s_sp")
    s_out = Sem(nc, "s_out")
    s_pe = Sem(nc, "s_pe")
    s_v = Sem(nc, "s_v")
    s_s = Sem(nc, "s_s")
    s_g = Sem(nc, "s_g")

    (C_M1, C_M2, C_T1, C_NT1, C_T2, C_NT2, C_MEAN, C_VAR, C_INV, C_NMI,
     C_AMX, C_R127, C_C2, C_VE, C_Y, C_TA, C_TB, C_SUM, C_M127_2, C_M127_1,
     C_DMY) = range(21)

    def col(i):
        return colpk[:, i:i + 1]

    TBL_LNEXP = 6
    TBL_SILU = 18

    with nc.Block() as blk:
        PIN_TABLES = False

        def pin_table(s, idx):
            if not PIN_TABLES:
                return
            s.add_instruction(mybir.InstLoadActFuncSet(
                name=nc.get_next_instruction_name(), act_func_set_id=idx, ins=[], outs=[]))

        def emit_newton(g, ve_ap, y_ap, ta=None, tb=None, niter=14):
            ta = col(C_TA) if ta is None else ta
            tb = col(C_TB) if tb is None else tb
            g.tensor_single_scalar(y_ap, ve_ap, 2.0, ALU.is_lt)
            for lo, hi, val in ((2.0, 8.0, 0.5), (8.0, 128.0, 0.125),
                                (128.0, 2048.0, 0.03125)):
                g.tensor_single_scalar(ta, ve_ap, lo, ALU.is_ge)
                g.tensor_single_scalar(tb, ve_ap, hi, ALU.is_lt)
                g.tensor_mul(ta, ta, tb)
                g.tensor_scalar(ta, ta, val, None, ALU.mult)
                g.tensor_add(y_ap, y_ap, ta)
            g.tensor_single_scalar(ta, ve_ap, 2048.0, ALU.is_ge)
            g.tensor_scalar(ta, ta, 0.0078125, None, ALU.mult)
            g.tensor_add(y_ap, y_ap, ta)
            for _i in range(niter):
                g.tensor_mul(ta, y_ap, y_ap)
                g.tensor_mul(ta, ta, ve_ap)
                g.tensor_scalar(ta, ta, -0.5, 1.5, ALU.mult, ALU.add)
                g.tensor_mul(y_ap, y_ap, ta)

        def vgap(v, n=1):
            for _ in range(n):
                v.tensor_single_scalar(col(C_DMY), epscol, 1.0, ALU.mult)

        def tap(name, src_ap):
            if not DBG:
                return
            h = nc.dram_tensor(name, list(src_ap.shape), src_ap.dtype, kind="ExternalOutput").ap()
            @blk.gpsimd
            def _(g):
                s_v.wait(g)
                s_s.wait(g)
                s_pe.wait(g)
                s_out.dinc(g.dma_start(h, src_ap))

        @blk.sync
        def _(sync):
            s_ld.dinc(sync.dma_start(idf, idf_h))
            s_ld.dinc(sync.dma_start(idb, idb_h))
            s_ld.dinc(sync.dma_start(bqcol, bq_h.rearrange("(t p) -> p t", p=128), allow_slow_non_contiguous=True))
            s_ld.dinc(sync.dma_start(bkcol, bk_h.rearrange("(t p) -> p t", p=128), allow_slow_non_contiguous=True))
            s_ld.dinc(sync.dma_start(cwcol.rearrange("p (t k) -> p t k", k=3), cw_h.rearrange("(t p) k -> p t k", p=128), allow_slow_non_contiguous=True))
            s_ld.dinc(sync.dma_start(cbcol, cb_h.rearrange("(t p) -> p t", p=128), allow_slow_non_contiguous=True))

        @blk.vector
        def _(v):
            s_ld.wait(v)
            v.memset(onescol_b, 1.0)
            v.memset(onesrow_b, 1.0)
            v.memset(onesrow_f, 1.0)
            s_v.inc(v.memset(epscol, EPS))

        for wi in range(2):
            w_src = w1_h if wi == 0 else w2_h
            nrow = H if wi == 0 else D
            NCHW = 8
            mcol_i = C_M1 if wi == 0 else C_M2
            tcol_i = C_T1 if wi == 0 else C_T2
            ntcol_i = C_NT1 if wi == 0 else C_NT2
            wview = w_src.rearrange("(c q p) d -> c p q d", c=NCHW, p=128)
            for ch in range(NCHW):
                buf = wst if ch % 2 == 0 else wst2
                @blk.sync
                def _(sync):
                    s_v.wait(sync)
                    s_ld.dinc(sync.dma_start(buf.rearrange("p (q d) -> p q d", d=D), wview[ch]))
                @blk.vector
                def _(v, ch=ch, buf=buf):
                    s_ld.wait(v)
                    if ch == 0:
                        s_v.inc(v.reduce_sum(col(C_TA), buf, AX.X, apply_absolute_value=True))
                    else:
                        v.reduce_sum(col(C_TB), buf, AX.X, apply_absolute_value=True)
                        vgap(v)
                        vgap(v)
                        s_v.inc(v.tensor_add(col(C_TA), col(C_TA), col(C_TB)))
            @blk.tensor
            def _(te):
                s_v.wait(te)
                s_pe.inc(te.matmul(pbk[6][0:1, 0:128], col(C_TA), idf, start=True, stop=True,
                                   skip_group_check=True))
            @blk.scalar
            def _(s, wi=wi):
                if wi == 0:
                    pin_table(s, TBL_LNEXP)
                s_pe.wait(s)
                s_s.inc(s.copy(rowf[0:1, 0:128], pbk[6][0:1, 0:128]))
            @blk.vector
            def _(v):
                s_s.wait(v)
                s_v.inc(v.reduce_sum(rowf[0:1, 128:129], rowf[0:1, 0:128], AX.X))
            @blk.tensor
            def _(te):
                s_v.wait(te)
                s_pe.inc(te.matmul(pbk[6][:, 128:129], onesrow_f, rowf[0:1, 128:129],
                                   start=True, stop=True, skip_group_check=True))
            @blk.vector
            def _(v, nrow=nrow, mcol_i=mcol_i, tcol_i=tcol_i, ntcol_i=ntcol_i, wi=wi):
                s_pe.wait(v)
                v.tensor_single_scalar(col(mcol_i), pbk[6][:, 128:129],
                                       1.0 / (nrow * (H * D // nrow)), ALU.mult)
                vgap(v)
                vgap(v)
                v.tensor_single_scalar(col(mcol_i), col(mcol_i), EPS, ALU.max)
                vgap(v)
                vgap(v)
                v.tensor_single_scalar(col(tcol_i), col(mcol_i), 0.5, ALU.mult)
                vgap(v)
                vgap(v)
                v.tensor_single_scalar(col(ntcol_i), col(tcol_i), -1.0, ALU.mult)
                s_v.inc(v.tensor_single_scalar(col(C_M127_2 if wi == 1 else C_M127_1),
                                                col(mcol_i), 1.0 / 127.0, ALU.mult))
            for ch in range(NCHW):
                buf = wst if ch % 2 == 0 else wst2
                @blk.sync
                def _(sync):
                    s_v.wait(sync)
                    s_ld.dinc(sync.dma_start(buf.rearrange("p (q d) -> p q d", d=D), wview[ch]))
                @blk.vector
                def _(v, ch=ch, buf=buf, tcol_i=tcol_i, ntcol_i=ntcol_i, wi=wi):
                    s_ld.wait(v)
                    if ch > 0 or wi > 0:
                        s_sp.wait(v)
                    v.tensor_scalar(tbuf, buf, col(tcol_i), None, ALU.is_ge)
                    v.tensor_scalar(tern2, buf, col(ntcol_i), None, ALU.is_le)
                    vgap(v)
                    s_v.inc(v.tensor_sub(tbuf, tbuf, tern2))
                for q in range(4):
                    for dt in range(DT):
                        bank = ptb[1][:, (dt % 4) * 128:(dt % 4) * 128 + 128]
                        @blk.tensor
                        def _(te, q=q, dt=dt, bank=bank):
                            s_v.wait(te)
                            s_pe.inc(te.transpose(bank, tbuf[:, q * D + dt * 128: q * D + dt * 128 + 128], idb))
                        @blk.vector
                        def _(v, q=q, dt=dt, bank=bank, wi=wi):
                            s_pe.wait(v)
                            if wi == 0:
                                dst = stg[:, dt * 512 + q * 128: dt * 512 + q * 128 + 128]
                            else:
                                dst = stg[:, (q * DT + dt) * 128: (q * DT + dt) * 128 + 128]
                            s_v.inc(v.tensor_copy(dst, bank))
                @blk.gpsimd
                def _(g, ch=ch, wi=wi):
                    s_v.wait(g)
                    if wi == 0:
                        dst = w1qT_h.rearrange("p (dt u) -> p dt u", dt=DT)[:, :, ch * 512:(ch + 1) * 512]
                        s_sp.dinc(g.dma_start(dst, stg.rearrange("p (dt u) -> p dt u", dt=DT)))
                    else:
                        dst = w2qT_h.rearrange("p (ht dd) -> p ht dd", ht=HT)[:, :, ch * 128:(ch + 1) * 128]
                        s_sp.dinc(g.dma_start(dst, stg.rearrange("p (ht j) -> p ht j", ht=HT)))

        def emit_var(v, src):
            v.reduce_sum(col(C_MEAN), src, AX.X)
            v.reduce_sum(col(C_VAR), sqb, AX.X)
            v.tensor_single_scalar(col(C_MEAN), col(C_MEAN), 1.0 / D, ALU.mult)
            v.tensor_single_scalar(col(C_VAR), col(C_VAR), 1.0 / D, ALU.mult)
            v.tensor_mul(col(C_SUM), col(C_MEAN), col(C_MEAN))
            vgap(v, 2)
            return v.tensor_sub(col(C_VAR), col(C_VAR), col(C_SUM))

        def emit_stats(src):
            @blk.scalar
            def _(s):
                s_v.wait(s)
                s_s.inc(s.activation(sqb, src, AF.Square))
            @blk.vector
            def _(v):
                s_s.wait(v)
                emit_var(v, src)
                vgap(v, 2)
                s_v.inc(v.tensor_single_scalar(col(C_VE), col(C_VAR), EPS, ALU.add))
            @blk.gpsimd
            def _(g):
                s_v.wait(g)
                emit_newton(g, col(C_VE), col(C_Y))
                s_g.inc(g.tensor_mul(col(C_NMI), col(C_MEAN), col(C_Y)))
            @blk.vector
            def _(v):
                s_g.wait(v)
                s_v.inc(v.tensor_single_scalar(col(C_NMI), col(C_NMI), -1.0, ALU.mult))

        tap("d_m", colpk)

        def emit_sqrow(slc, dstrow, pe_ready):
            _sb = s_s.n
            _pb = s_pe.n
            bank = pbk[4][0:1, 0:512]
            @blk.scalar
            def _(s):
                s.wait_ge(s_pe.h, pe_ready)
                k = 0
                for ncv in range(4):
                    for dt in range(DT):
                        sq = sqb2a if dt % 2 == 0 else sqb2b
                        if k >= 2:
                            s.wait_ge(s_pe.h, _pb + (k - 1))
                        s_s.inc(s.activation(sq, slc(dt, ncv * 512, ncv * 512 + 512), AF.Square))
                        k += 1
                    s.wait_ge(s_pe.h, _pb + (ncv + 1) * DT)
                    s_s.inc(s.activation(dstrow[0:1, ncv * 512:(ncv + 1) * 512], bank, AF.Identity,
                                         bias=0.0, scale=-0.5))
            @blk.tensor
            def _(te):
                k = 0
                for ncv in range(4):
                    for dt in range(DT):
                        sq = sqb2a if dt % 2 == 0 else sqb2b
                        te.wait_ge(s_s.h, _sb + k + 1 + ncv)
                        s_pe.inc(te.matmul(bank, onescol_b, sq, start=(dt == 0), stop=(dt == DT - 1),
                                           skip_group_check=True))
                        k += 1

        def batch(b):
            xr = x_h[b].rearrange("(t p) d -> t p d", p=128)

            @blk.sync
            def _(sync):
                s_s.wait(sync)
                s_ld.dinc(sync.dma_start(ckb[0:1, 0:D], bv_h.rearrange("d -> () d")))
                s_ld.dinc(sync.dma_start(sqb[0:1, 0:D], bo_h.rearrange("d -> () d")))
            @blk.vector
            def _(v):
                s_pe.wait(v)
                s_ld.wait(v)
                for dt in range(DT):
                    v.memset(BIGA[:, dt * BIGW: dt * BIGW + 1], 0.0)
                v.tensor_copy(bvrow, ckb[0:1, 0:D])
                v.tensor_copy(borow, sqb[0:1, 0:D])
                s_v.inc(v.memset(BIGA[:, (DT - 1) * BIGW + 2049: (DT - 1) * BIGW + 2050], 0.0))
            @blk.vector
            def _(v):
                for dt in range(DT - 1):
                    v.memset(BIGA[:, dt * BIGW + 2049: dt * BIGW + 2050], 0.0)
                s_v.inc(v.tensor_single_scalar(col(C_DMY), epscol, 1.0, ALU.mult))

            for t in range(NT):
                xb = xrel if t % 2 == 0 else xrel2
                @blk.sync
                def _(sync, t=t, xb=xb):
                    s_s.wait(sync)
                    s_v.wait(sync)
                    s_ld.dinc(sync.dma_start(xb, xr[t]))
                @blk.scalar
                def _(s, xb=xb, t=t):
                    s_ld.wait(s)
                    s_s.inc(s.activation(sqb, xb, AF.Square))
                @blk.vector
                def _(v, xb=xb, t=t):
                    s_s.wait(v)
                    v.reduce_sum(mcol16[:, t:t + 1], xb, AX.X)
                    v.reduce_sum(vcol16[:, t:t + 1], sqb, AX.X)
                    s_v.inc(v.tensor_single_scalar(mcol16[:, t:t + 1], mcol16[:, t:t + 1],
                                                   1.0 / D, ALU.mult))
            @blk.vector
            def _(v):
                v.tensor_single_scalar(vcol16, vcol16, 1.0 / D, ALU.mult)
                v.tensor_mul(icol16, mcol16, mcol16)
                vgap(v, 2)
                v.tensor_sub(vcol16, vcol16, icol16)
                vgap(v, 2)
                s_v.inc(v.tensor_single_scalar(vcol16, vcol16, EPS, ALU.add))
            @blk.gpsimd
            def _(g):
                s_v.wait(g)
                emit_newton(g, vcol16, icol16, sqb[:, 0:16], sqb[:, 16:32])
                s_g.inc(g.tensor_mul(mcol16, mcol16, icol16))
            @blk.vector
            def _(v):
                s_g.wait(v)
                s_v.inc(v.tensor_single_scalar(mcol16, mcol16, -1.0, ALU.mult))
            for t in range(NT):
                xb = xrel if t % 2 == 0 else xrel2
                lb = lnta if t % 2 == 0 else lntb
                @blk.sync
                def _(sync, t=t, xb=xb):
                    s_s.wait(sync)
                    s_v.wait(sync)
                    s_ld.dinc(sync.dma_start(xb, xr[t]))
                @blk.scalar
                def _(s, xb=xb, lb=lb, t=t):
                    s_ld.wait(s)
                    s_s.inc(s.activation(lb, xb, AF.Identity, bias=mcol16[:, t:t + 1],
                                         scale=icol16[:, t:t + 1]))
                for dt in range(DT):
                    bank = ptb[dt % 2][:, 0:128]
                    @blk.tensor
                    def _(te, lb=lb, dt=dt, bank=bank):
                        s_s.wait(te)
                        s_v.wait(te)
                        s_pe.inc(te.transpose(bank, lb[:, dt * 128: dt * 128 + 128], idb))
                    @blk.vector
                    def _(v, t=t, dt=dt, bank=bank):
                        s_pe.wait(v)
                        s_v.inc(v.tensor_copy(
                            BIGA[:, dt * BIGW + 1 + t * 128: dt * BIGW + 1 + t * 128 + 128], bank))

            def load_weight(w_src, dst, pe_guard=None):
                for dt in range(DT):
                    @blk.sync
                    def _(sync, dt=dt, w_src=w_src, pe_guard=pe_guard):
                        s_v.wait(sync)
                        if dt == 0 and pe_guard is not None:
                            sync.wait_ge(s_pe.h, pe_guard)
                        elif dt == 0:
                            s_pe.wait(sync)
                        s_ld.dinc(sync.dma_start(wstage, w_src[dt * 128:(dt + 1) * 128, :]))
                    @blk.vector
                    def _(v, dt=dt, dst=dst):
                        s_ld.wait(v)
                        s_v.inc(v.tensor_copy(dst[:, dt * D:(dt + 1) * D], wstage))

            load_weight(Wv_h, Wbf)
            _vev = {}
            @blk.tensor
            def _(te):
                s_v.wait(te)
                for mt in range(NT):
                    for ncv in range(2):
                        bank = pbk[(mt * 2 + ncv) % 4][:, 0:512]
                        k = mt * 2 + ncv
                        if k >= 4:
                            te.wait_ge(s_s.h, _vev[k - 4])
                        for dt in range(DT):
                            te.matmul(bank, BIGA[:, dt * BIGW + 1 + mt * 128: dt * BIGW + 1 + mt * 128 + 128],
                                      Wbf[:, dt * D + ncv * 512: dt * D + ncv * 512 + 512],
                                      start=(dt == 0), stop=False, skip_group_check=True)
                        s_pe.inc(te.matmul(bank, onesrow_b[0:1, 0:128], bvrow[:, ncv * 512: ncv * 512 + 512],
                                           start=False, stop=True, skip_group_check=True))
                        _vev[k] = s_s.n + k + 1
            m_vproj = s_pe.n
            @blk.scalar
            def _(s):
                base = s_pe.n - NT * 2
                for mt in range(NT):
                    for ncv in range(2):
                        k = mt * 2 + ncv
                        bank = pbk[k % 4][:, 0:512]
                        s.wait_ge(s_pe.h, base + k + 1)
                        s_s.inc(s.copy(vtok[:, mt * D + ncv * 512: mt * D + ncv * 512 + 512], bank))

            load_weight(Wk_h, Wbf2)
            _kev = {}
            _kpe0 = s_pe.n
            @blk.tensor
            def _(te):
                s_v.wait(te)
                for mt in range(DT):
                    for ncv in range(4):
                        k = mt * 4 + ncv
                        bank = pbk[k % 4][:, 0:512]
                        if k >= 4:
                            te.wait_ge(s_s.h, _kev[k - 4])
                        ii = None
                        for dt in range(DT):
                            ii = te.matmul(bank, Wbf2[:, dt * D + mt * 128: dt * D + mt * 128 + 128],
                                           BIGA[:, dt * BIGW + 1 + ncv * 512: dt * BIGW + 1 + ncv * 512 + 512],
                                           start=(dt == 0), stop=(dt == DT - 1), skip_group_check=True)
                        s_pe.inc(ii)
                        _kev[k] = s_s.n + k + 1
            m_kproj = s_pe.n
            @blk.scalar
            def _(s, _kpe0=_kpe0):
                for mt in range(DT):
                    for ncv in range(4):
                        k = mt * 4 + ncv
                        bank = pbk[k % 4][:, 0:512]
                        s.wait_ge(s_pe.h, _kpe0 + k + 1)
                        s_s.inc(s.activation(kT[:, mt * S + ncv * 512: mt * S + ncv * 512 + 512], bank,
                                             AF.Identity, bias=bkcol[:, mt:mt + 1], scale=1.0))
            emit_sqrow(lambda dt, lo, hi: kT[:, dt * S + lo: dt * S + hi], ksqrow, m_kproj)

            load_weight(Wq_h, Wbf, pe_guard=m_vproj)
            for ncv in range(4):
                @blk.tensor
                def _(te, ncv=ncv):
                    if ncv == 0:
                        s_v.wait(te)
                    s_s.wait(te)
                    ii = None
                    for mt in range(DT):
                        for dt in range(DT):
                            ii = te.matmul(pbk[mt][:, 0:512],
                                           Wbf[:, dt * D + mt * 128: dt * D + mt * 128 + 128],
                                           BIGA[:, dt * BIGW + 1 + ncv * 512: dt * BIGW + 1 + ncv * 512 + 512],
                                           start=(dt == 0), stop=(dt == DT - 1), skip_group_check=True)
                    s_pe.inc(ii)
                @blk.scalar
                def _(s, ncv=ncv):
                    s_pe.wait(s)
                    ii = None
                    for mt in range(DT):
                        ii = s.activation(BIGA[:, mt * BIGW + 1 + ncv * 512: mt * BIGW + 1 + ncv * 512 + 512],
                                          pbk[mt][:, 0:512], AF.Identity, bias=bqcol[:, mt:mt + 1], scale=1.0)
                    s_s.inc(ii)
            m_qproj = s_pe.n
            emit_sqrow(lambda dt, lo, hi: BIGA[:, dt * BIGW + 1 + lo: dt * BIGW + 1 + hi],
                       qsqrow, m_qproj)

            load_weight(Wo_h, Wbf2, pe_guard=m_kproj)
            if b == 0:
                tap("d_kt", kT)
                tap("d_vt", vtok)
                tap("d_q", BIGA)
                tap("d_qsq", qsqrow)
                tap("d_ksq", ksqrow)
                tap("d_wbf", Wbf)

            m_qk = {}
            m_exp = {}
            m_ev = {}
            m_evac = {}
            m_ckb = {}
            for c in range(NCH):
                opsum = [pbk[2 + i // 2][:, (i % 2) * CH:(i % 2) * CH + CH] for i in range(DT)]
                dn = pbk[6][0:1, 0:CH]
                _n = s_pe.n
                for step in range(NT + 2):
                    if step < NT:
                        _n += 1
                        m_qk[(c, step)] = _n
                    if step >= 2:
                        _n += 1
                        m_ev[(c, step - 2)] = _n
                for jt in range(NT):
                    m_exp[(c, jt)] = s_s.n + jt + 1
                EBS = [eba, ebb, ebc]
                QKB = [0, 1, 7]
                @blk.tensor
                def _(te, c=c):
                    for step in range(NT + 2):
                        jt = step
                        if jt < NT:
                            qk = pbk[QKB[jt % 3]][:, 0:CH]
                            prev = (c, jt - 3) if jt >= 3 else (c - 1, NT + jt - 3)
                            if prev in m_exp and (prev[0] >= 0):
                                te.wait_ge(s_s.h, m_exp[prev])
                            if jt == 0 and c > 0:
                                te.wait_ge(s_v.h, m_evac[c - 1])
                                te.wait_ge(s_s.h, m_ckb[c - 1])
                            for dt in range(DT):
                                te.matmul(qk, kT[:, dt * S + jt * 128: dt * S + jt * 128 + 128],
                                          BIGA[:, dt * BIGW + 1 + c * CH: dt * BIGW + 1 + c * CH + CH],
                                          start=(dt == 0), stop=False, skip_group_check=True)
                            te.matmul(qk, onesrow_b[0:1, 0:128], qsqrow[0:1, c * CH: c * CH + CH],
                                      start=False, stop=False, skip_group_check=True)
                            s_pe.inc(te.matmul(qk, ksqrow[0:1, jt * 128: jt * 128 + 128],
                                               onesrow_b[0:1, 0:CH],
                                               start=False, stop=True, skip_group_check=True))
                            assert s_pe.n == m_qk[(c, jt)], (s_pe.n, m_qk[(c, jt)])
                        if step >= 2:
                            jt1 = step - 2
                            eb1 = EBS[jt1 % 3]
                            te.wait_ge(s_s.h, m_exp[(c, jt1)])
                            for i in range(DT):
                                te.matmul(opsum[i], vtok[:, jt1 * D + i * 128: jt1 * D + i * 128 + 128],
                                          eb1, start=(jt1 == 0 and i % 2 == 0), stop=(jt1 == NT - 1),
                                          skip_group_check=True)
                            s_pe.inc(te.matmul(dn, onescol_b, eb1, start=(jt1 == 0), stop=(jt1 == NT - 1),
                                               skip_group_check=True))
                            assert s_pe.n == m_ev[(c, jt1)], (s_pe.n, m_ev[(c, jt1)])
                @blk.scalar
                def _(s, c=c):
                    for jt in range(NT):
                        qk = pbk[QKB[jt % 3]][:, 0:CH]
                        eb_ = EBS[jt % 3]
                        s.wait_ge(s_pe.h, m_qk[(c, jt)])
                        prev = (c, jt - 3) if jt >= 3 else (c - 1, NT + jt - 3)
                        if prev in m_ev and prev[0] >= 0:
                            s.wait_ge(s_pe.h, m_ev[prev])
                        s.activation(sba, qk, AF.Exp, bias=0.0, scale=2.0)
                        s_s.inc(s.activation(eb_, sba, AF.Exp, bias=0.0, scale=1.0))
                        assert s_s.n == m_exp[(c, jt)]
                if b == 0 and c == 0:
                    tap("d_sb", sba)
                    tap("d_eb", ebb)
                @blk.vector
                def _(v):
                    s_pe.wait(v)
                    v.reciprocal(rowf[0:1, 0:CH], dn)
                    vgap(v)
                    vgap(v)
                    s_v.inc(v.tensor_copy(rowb[0:1, 0:CH], rowf[0:1, 0:CH]))
                @blk.tensor
                def _(te):
                    s_v.wait(te)
                    s_pe.inc(te.matmul(pbk[6][:, CH:CH + CH], onesrow_b[0:1, 0:128], rowb[0:1, 0:CH],
                                       start=True, stop=True, skip_group_check=True))
                @blk.vector
                def _(v, c=c):
                    s_pe.wait(v)
                    v.tensor_copy(rB, pbk[6][:, CH:CH + CH])
                    vgap(v)
                    vgap(v)
                    ii = None
                    for i in range(DT):
                        ii = v.tensor_mul(outn[:, i * CH:(i + 1) * CH], opsum[i], rB)
                    s_v.inc(ii)
                    m_evac[c] = s_v.mark()
                if b == 0 and c == 0:
                    tap("d_outn", outn)
                    tap("d_rb", rB)
                    tap("d_rowf", rowf)
                for imt in range(2):
                    aop = [pbk[2 + 2 * imt + ncv][:, 0:512] for ncv in range(2)]
                    @blk.tensor
                    def _(te, imt=imt, aop=aop):
                        s_v.wait(te)
                        ii = None
                        for ncv in range(2):
                            for i in range(DT):
                                te.matmul(aop[ncv], outn[:, i * CH + imt * 128: i * CH + imt * 128 + 128],
                                          Wbf2[:, i * D + ncv * 512: i * D + ncv * 512 + 512],
                                          start=(i == 0), stop=False, skip_group_check=True)
                            ii = te.matmul(aop[ncv], onesrow_b[0:1, 0:128], borow[:, ncv * 512: ncv * 512 + 512],
                                           start=False, stop=True, skip_group_check=True)
                        s_pe.inc(ii)
                for imt in range(2):
                    aop = [pbk[2 + 2 * imt + ncv][:, 0:512] for ncv in range(2)]
                    t_glob = c * 2 + imt
                    xb = xrel if t_glob % 2 == 0 else xrel2
                    lnt = lnta if t_glob % 2 == 0 else lntb
                    sb0 = s_s.n
                    vb0 = s_v.n
                    gb0 = s_g.n
                    ldm = s_ld.n + 16
                    @blk.sync
                    def _(sync, t_glob=t_glob, xb=xb):
                        sync.wait_ge(s_sp.h, s_sp.n)
                        sync.wait_ge(s_v.h, s_v.n)
                        s_ld.dinc(sync.dma_start(xb, xr[t_glob]))
                    @blk.scalar
                    def _(s, aop=aop, xb=xb, sb0=sb0, vb0=vb0):
                        s_pe.wait(s)
                        s.wait_ge(s_v.h, vb0)
                        s.copy(ckb[:, 0:512], aop[0])
                        s_s.inc(s.copy(ckb[:, 512:1024], aop[1]))
                        s_s.inc(s.activation(sqb, ckb, AF.Square))
                        s.wait_ge(s_v.h, vb0 + 2)
                        s_s.inc(s.activation(lnbuf, ckb, AF.Identity, bias=col(C_NMI), scale=col(C_Y)))
                        s.wait_ge(s_v.h, vb0 + 3)
                        s_s.inc(s.activation(sqb, xb, AF.Square))
                        s.wait_ge(s_v.h, vb0 + 5)
                        s_s.inc(s.activation(lnbuf, xb, AF.Identity, bias=col(C_NMI), scale=col(C_Y)))
                    m_ckb[c] = sb0 + 1
                    @blk.vector
                    def _(v, xb=xb, sb0=sb0, gb0=gb0, ldm=ldm):
                        v.wait_ge(s_s.h, sb0 + 2)
                        emit_var(v, ckb)
                        vgap(v, 2)
                        s_v.inc(v.tensor_single_scalar(col(C_VE), col(C_VAR), EPS, ALU.add))
                        v.wait_ge(s_g.h, gb0 + 1)
                        s_v.inc(v.tensor_single_scalar(col(C_NMI), col(C_NMI), -1.0, ALU.mult))
                        v.wait_ge(s_s.h, sb0 + 3)
                        v.wait_ge(s_ld.h, ldm)
                        s_v.inc(v.tensor_add(xb, xb, lnbuf))
                        v.wait_ge(s_s.h, sb0 + 4)
                        emit_var(v, xb)
                        vgap(v, 2)
                        s_v.inc(v.tensor_single_scalar(col(C_VE), col(C_VAR), EPS, ALU.add))
                        v.wait_ge(s_g.h, gb0 + 2)
                        s_v.inc(v.tensor_single_scalar(col(C_NMI), col(C_NMI), -1.0, ALU.mult))
                    @blk.gpsimd
                    def _(g, t_glob=t_glob, xb=xb, vb0=vb0):
                        g.wait_ge(s_v.h, vb0 + 1)
                        emit_newton(g, col(C_VE), col(C_Y))
                        s_g.inc(g.tensor_mul(col(C_NMI), col(C_MEAN), col(C_Y)))
                        g.wait_ge(s_v.h, vb0 + 3)
                        s_sp.dinc(g.dma_start(xsp_h.rearrange("(t p) d -> t p d", p=128)[t_glob], xb))
                        g.wait_ge(s_v.h, vb0 + 4)
                        emit_newton(g, col(C_VE), col(C_Y))
                        s_g.inc(g.tensor_mul(col(C_NMI), col(C_MEAN), col(C_Y)))
                    @blk.vector
                    def _(v, t_glob=t_glob, sb0=sb0):
                        v.wait_ge(s_s.h, sb0 + 5)
                        v.reduce_max(col(C_AMX), lnbuf, AX.X, apply_absolute_value=True)
                        vgap(v, 2)
                        v.tensor_single_scalar(col(C_AMX), col(C_AMX), EPS, ALU.max)
                        vgap(v, 2)
                        v.reciprocal(col(C_C2), col(C_AMX))
                        v.tensor_mul(c1col[:, t_glob:t_glob + 1], col(C_AMX), col(C_M127_1))
                        vgap(v)
                        v.tensor_single_scalar(col(C_R127), col(C_C2), 127.0, ALU.mult)
                        vgap(v, 2)
                        v.tensor_scalar(sqb, lnbuf, col(C_R127), 128.0, ALU.mult, ALU.add)
                        vgap(v)
                        v.tensor_copy(q_i32, sqb)
                        vgap(v)
                        v.tensor_copy(q1c, q_i32)
                        vgap(v)
                        v.scalar_tensor_tensor(ckb, sqb, 0.5, q1c, ALU.add, ALU.subtract)
                        vgap(v)
                        v.tensor_single_scalar(ckb, ckb, 1.0, ALU.is_ge)
                        vgap(v)
                        v.scalar_tensor_tensor(q1c, q1c, -128.0, ckb, ALU.add, ALU.add)
                        vgap(v)
                        s_v.inc(v.tensor_scalar(lnt, q1c, c1col[:, t_glob:t_glob + 1], None, ALU.mult))
                    @blk.gpsimd
                    def _(g, t_glob=t_glob, lnt=lnt):
                        s_v.wait(g)
                        s_sp.dinc(g.dma_start(xq_h.rearrange("(t p) d -> t p d", p=128)[t_glob], lnt))

            @blk.sync
            def _(sync):
                s_pe.wait(sync)
                s_sp.wait(sync)
                s_s.wait(sync)
                s_ld.dinc(sync.dma_start(w1qT, w1qT_h))
                s_ld.dinc(sync.dma_start(w2qT, w2qT_h))
            for t in range(NT):
                lb = lnta if t % 2 == 0 else lntb
                @blk.sync
                def _(sync, t=t, lb=lb):
                    s_v.wait(sync)
                    s_ld.dinc(sync.dma_start(lb, xq_h.rearrange("(t p) d -> t p d", p=128)[t]))
                for dt in range(DT):
                    bank = ptb[dt % 2][:, 0:128]
                    @blk.tensor
                    def _(te, dt=dt, lb=lb, bank=bank):
                        s_ld.wait(te)
                        s_v.wait(te)
                        s_pe.inc(te.transpose(bank, lb[:, dt * 128: dt * 128 + 128], idb))
                    @blk.vector
                    def _(v, t=t, dt=dt, bank=bank):
                        s_pe.wait(v)
                        s_v.inc(v.tensor_copy(
                            BIGA[:, dt * BIGW + 1 + t * 128: dt * BIGW + 1 + t * 128 + 128], bank))

            if b == 0:
                tap("d_xqt", BIGA)
                tap("d_w1", w1qT)
                tap("d_w2", w2qT)

            m_u2 = {}
            m_silu = {}
            m_hmm = {}
            for c in range(NFC):
                _vb = s_v.n
                for s_i in range(HT + 2):
                    if s_i - 2 >= 0:
                        m_u2[(c, s_i - 2)] = _vb + s_i + 1
                _sb = s_s.n
                for ht in range(HT):
                    m_silu[(c, ht)] = _sb + ht + 1
                @blk.tensor
                def _(te, c=c):
                    if c == 0:
                        te.wait_ge(s_ld.h, s_ld.n)
                        te.wait_ge(s_v.h, _vb)
                        te.wait_ge(s_s.h, _sb)
                    for ht in range(HT):
                        hb = pbk[ht % 4][:, 0:130]
                        prev = (c, ht - 4) if ht >= 4 else (c - 1, HT + ht - 4)
                        if prev in m_u2 and prev[0] >= 0:
                            te.wait_ge(s_v.h, m_u2[prev])
                        ii = None
                        for dt in range(DT):
                            ii = te.matmul(hb, w1qT[:, dt * H + ht * 128: dt * H + ht * 128 + 128],
                                           BIGA[:, dt * BIGW + c * FC: dt * BIGW + c * FC + 130],
                                           start=(dt == 0), stop=(dt == DT - 1), skip_group_check=True)
                        s_pe.inc(ii)
                        m_hmm[(c, ht)] = s_pe.n
                @blk.vector
                def _(v, c=c, _vb=_vb):
                    for s_i in range(HT + 2):
                        ht0, ht1, ht2 = s_i, s_i - 1, s_i - 2
                        did = None
                        if ht0 < HT:
                            v.wait_ge(s_pe.h, m_hmm[(c, ht0)])
                            hb = pbk[ht0 % 4][:, 0:130]
                            did = v.tensor_scalar(cva if ht0 % 2 == 0 else cvb, hb[:, 0:128],
                                                  cwcol[:, ht0 * 3: ht0 * 3 + 1], None, ALU.mult)
                        if 0 <= ht1 < HT:
                            hb = pbk[ht1 % 4][:, 0:130]
                            did = v.scalar_tensor_tensor(cvc if ht1 % 2 == 0 else cvd, hb[:, 1:129],
                                                         cwcol[:, ht1 * 3 + 1: ht1 * 3 + 2],
                                                         cva if ht1 % 2 == 0 else cvb,
                                                         ALU.mult, ALU.add)
                        if 0 <= ht2 < HT:
                            if (c, ht2 - 2) in m_silu:
                                v.wait_ge(s_s.h, m_silu[(c, ht2 - 2)])
                            hb = pbk[ht2 % 4][:, 0:130]
                            did = v.scalar_tensor_tensor(cve if ht2 % 2 == 0 else cvf, hb[:, 2:130],
                                                         cwcol[:, ht2 * 3 + 2: ht2 * 3 + 3],
                                                         cvc if ht2 % 2 == 0 else cvd,
                                                         ALU.mult, ALU.add)
                        s_v.inc(did)
                        assert s_v.n == _vb + s_i + 1
                @blk.scalar
                def _(s, c=c):
                    for ht in range(HT):
                        s.wait_ge(s_v.h, m_u2[(c, ht)])
                        s_s.inc(s.activation(YS[:, ht * FC:(ht + 1) * FC],
                                             cve if ht % 2 == 0 else cvf, AF.Silu,
                                             bias=cbcol[:, ht:ht + 1], scale=1.0))
                        assert s_s.n == m_silu[(c, ht)]
                @blk.vector
                def _(v):
                    s_s.wait(v)
                    s_v.inc(v.reduce_max(cva, YS.rearrange("p (ht i) -> p i ht", ht=HT), AX.X,
                                         apply_absolute_value=True))
                @blk.tensor
                def _(te):
                    s_v.wait(te)
                    s_pe.inc(te.transpose(pbk[6][:, 0:128], cva, idf))
                @blk.vector
                def _(v):
                    s_pe.wait(v)
                    v.reduce_max(col(C_AMX), pbk[6][:, 0:128], AX.X)
                    vgap(v)
                    vgap(v)
                    v.tensor_single_scalar(col(C_AMX), col(C_AMX), EPS, ALU.max)
                    vgap(v)
                    vgap(v)
                    v.reciprocal(col(C_R127), col(C_AMX))
                    v.tensor_mul(col(C_C2), col(C_AMX), col(C_M127_2))
                    vgap(v)
                    s_v.inc(v.tensor_single_scalar(col(C_R127), col(C_R127), 127.0, ALU.mult))
                @blk.tensor
                def _(te):
                    s_v.wait(te)
                    s_pe.inc(te.matmul(pbk[6][0:1, 128:256], col(C_R127), idf, start=True, stop=True,
                                       skip_group_check=True))
                @blk.scalar
                def _(s):
                    s_pe.wait(s)
                    s_s.inc(s.copy(rowf[0:1, 0:FC], pbk[6][0:1, 128:256]))
                @blk.tensor
                def _(te):
                    s_s.wait(te)
                    s_pe.inc(te.matmul(pbk[6][:, 256:384], onesrow_f, rowf[0:1, 0:FC],
                                       start=True, stop=True, skip_group_check=True))
                @blk.vector
                def _(v):
                    s_pe.wait(v)
                    s_v.inc(v.tensor_copy(r2B, pbk[6][:, 256:384]))
                for qq in range(4):
                    sl = slice(qq * 8 * FC, (qq + 1) * 8 * FC)
                    @blk.vector
                    def _(v, sl=sl, c=c, qq=qq):
                        s_v.wait(v)
                        if c > 0 and qq == 0:
                            s_pe.wait(v)
                        ysq = YS[:, sl]
                        vgap(v)
                        v.tensor_tensor(ysq.rearrange("p (a i) -> p a i", a=8),
                                        ysq.rearrange("p (a i) -> p a i", a=8),
                                        r2B.rearrange("p i -> p () i").broadcast_to([128, 8, FC]),
                                        ALU.mult)
                        vgap(v)
                        v.tensor_single_scalar(ysq, ysq, 128.0, ALU.add)
                        vgap(v)
                        v.tensor_copy(q_i32, ysq)
                        vgap(v)
                        v.tensor_copy(sqb, q_i32)
                        vgap(v)
                        v.scalar_tensor_tensor(ysq, ysq, 0.5, sqb, ALU.add, ALU.subtract)
                        vgap(v)
                        v.tensor_single_scalar(ysq, ysq, 1.0, ALU.is_ge)
                        vgap(v)
                        s_v.inc(v.scalar_tensor_tensor(hqb[:, sl], sqb, -128.0, ysq, ALU.add, ALU.add))
                if b == 0 and c == 0:
                    tap("d_ys", YS)
                    tap("d_hq", hqb)
                    tap("d_c1", c1col)
                for ncv in range(2):
                    bank = pbk[4 + ncv][:, 0:512]
                    @blk.tensor
                    def _(te, c=c, ncv=ncv, bank=bank):
                        s_v.wait(te)
                        s_s.wait(te)
                        ii = None
                        for ht in range(HT):
                            ii = te.matmul(bank, hqb[:, ht * FC: ht * FC + 128],
                                           w2qT[:, ht * D + ncv * 512: ht * D + ncv * 512 + 512],
                                           start=(ht == 0), stop=(ht == HT - 1), skip_group_check=True)
                        s_pe.inc(ii)
                m_w2 = s_pe.n
                @blk.sync
                def _(sync, c=c):
                    s_v.wait(sync)
                    s_out.wait(sync, s_out.n)
                    s_ld.dinc(sync.dma_start(xrel, xsp_h.rearrange("(t p) d -> t p d", p=128)[c]))
                _sb2 = s_s.n
                _vb2 = s_v.n
                _outb = s_out.n
                @blk.scalar
                def _(s, c=c, m_w2=m_w2, _vb2=_vb2, _outb=_outb):
                    s.wait_ge(s_pe.h, m_w2)
                    s_s.inc(s.activation(ckb[:, 0:512], pbk[4][:, 0:512], AF.Identity,
                                         bias=0.0, scale=col(C_C2)))
                    s_s.inc(s.activation(ckb[:, 512:1024], pbk[5][:, 0:512], AF.Identity,
                                         bias=0.0, scale=col(C_C2)))
                    s_s.inc(s.activation(sqb, ckb, AF.Square))
                    s.wait_ge(s_v.h, _vb2 + 2)
                    s.wait_ge(s_out.h, _outb)
                    s_s.inc(s.activation(lnbuf, ckb, AF.Identity, bias=col(C_NMI), scale=col(C_Y)))
                m_sq = _sb2 + 3
                m_lnfo = _sb2 + 4
                @blk.vector
                def _(v, m_sq=m_sq):
                    v.wait_ge(s_s.h, m_sq)
                    emit_var(v, ckb)
                    vgap(v, 2)
                    s_v.inc(v.tensor_single_scalar(col(C_VE), col(C_VAR), EPS, ALU.add))
                @blk.gpsimd
                def _(g):
                    s_v.wait(g)
                    emit_newton(g, col(C_VE), col(C_Y))
                    s_g.inc(g.tensor_mul(col(C_NMI), col(C_MEAN), col(C_Y)))
                @blk.vector
                def _(v):
                    s_g.wait(v)
                    s_v.inc(v.tensor_single_scalar(col(C_NMI), col(C_NMI), -1.0, ALU.mult))
                @blk.vector
                def _(v, m_lnfo=m_lnfo):
                    v.wait_ge(s_s.h, m_lnfo)
                    s_ld.wait(v)
                    s_v.inc(v.tensor_add(lnbuf, lnbuf, xrel))
                @blk.gpsimd
                def _(g, b=b, c=c):
                    s_v.wait(g)
                    s_out.dinc(g.dma_start(y_h[b].rearrange("(t p) d -> t p d", p=128)[c], lnbuf))

        for b in range(NB):
            batch(b)

        @blk.gpsimd
        def _(g):
            g.wait_ge(s_out.h, s_out.n)

    return nc


def kernel(L0_attn_x, L0_attn_Wq, L0_attn_bq, L0_attn_Wk, L0_attn_bk, L0_attn_Wv,
           L0_attn_bv, L0_attn_Wo, L0_attn_bo, L0_ffn_w1, L0_ffn_conv_w,
           L0_ffn_conv_b, L0_ffn_w2):
    x = np.ascontiguousarray(np.asarray(L0_attn_x, dtype=np.float32))
    assert x.shape == (32, S, D)
    base = dict(
        Wq=np.ascontiguousarray(L0_attn_Wq, dtype=np.float32),
        Wk=np.ascontiguousarray(L0_attn_Wk, dtype=np.float32),
        Wv=np.ascontiguousarray(L0_attn_Wv, dtype=np.float32),
        Wo=np.ascontiguousarray(L0_attn_Wo, dtype=np.float32),
        bq=np.ascontiguousarray(L0_attn_bq, dtype=np.float32),
        bk=np.ascontiguousarray(L0_attn_bk, dtype=np.float32),
        bv=np.ascontiguousarray(L0_attn_bv, dtype=np.float32),
        bo=np.ascontiguousarray(L0_attn_bo, dtype=np.float32),
        w1=np.ascontiguousarray(L0_ffn_w1, dtype=np.float32),
        cw=np.ascontiguousarray(np.asarray(L0_ffn_conv_w, dtype=np.float32).reshape(H, 3)),
        cb=np.ascontiguousarray(L0_ffn_conv_b, dtype=np.float32),
        w2=np.ascontiguousarray(L0_ffn_w2, dtype=np.float32),
        idf=np.eye(128, dtype=np.float32),
        idb=np.eye(128, dtype=ml_dtypes.bfloat16),
    )
    nc = build_nc()
    in_maps = [dict(base, x=np.ascontiguousarray(x[4 * c:4 * c + 4])) for c in range(8)]
    res = run_bass_kernel_spmd(nc, in_maps, core_ids=list(range(8)))
    return np.concatenate([r["y"] for r in res.results], axis=0)


# version 08
# speedup vs baseline: 1.6341x; 1.0235x over previous
import numpy as np
import ml_dtypes
import concourse.bass as bass
import concourse.mybir as mybir
from concourse.bass_utils import run_bass_kernel_spmd

F32 = mybir.dt.float32
BF16 = mybir.dt.bfloat16
I32 = mybir.dt.int32
AF = mybir.ActivationFunctionType
ALU = mybir.AluOpType
AX = mybir.AxisListType

NB = 4
S = 2048
D = 1024
H = 4096
NT = S // 128
DT = D // 128
HT = H // 128
CH = 256
NCH = S // CH
FC = 128
NFC = S // FC
EPS = 1e-5
BIGW = 2050
DBG = False


class Sem:
    def __init__(self, nc, name):
        self.h = nc.alloc_semaphore(name)
        self.n = 0

    def inc(self, instr, k=1):
        self.n += k
        instr.then_inc(self.h, k)
        return instr

    def dinc(self, instr):
        self.n += 16
        instr.then_inc(self.h, 16)
        return instr

    def wait(self, eng, val=None):
        eng.wait_ge(self.h, self.n if val is None else val)

    def mark(self):
        return self.n


def build_nc():
    nc = bass.Bass("TRN2", target_bir_lowering=False)

    x_h = nc.dram_tensor("x", [NB, S, D], F32, kind="ExternalInput").ap()
    Wq_h = nc.dram_tensor("Wq", [D, D], F32, kind="ExternalInput").ap()
    Wk_h = nc.dram_tensor("Wk", [D, D], F32, kind="ExternalInput").ap()
    Wv_h = nc.dram_tensor("Wv", [D, D], F32, kind="ExternalInput").ap()
    Wo_h = nc.dram_tensor("Wo", [D, D], F32, kind="ExternalInput").ap()
    bq_h = nc.dram_tensor("bq", [D], F32, kind="ExternalInput").ap()
    bk_h = nc.dram_tensor("bk", [D], F32, kind="ExternalInput").ap()
    bv_h = nc.dram_tensor("bv", [D], F32, kind="ExternalInput").ap()
    bo_h = nc.dram_tensor("bo", [D], F32, kind="ExternalInput").ap()
    w1_h = nc.dram_tensor("w1", [H, D], F32, kind="ExternalInput").ap()
    cw_h = nc.dram_tensor("cw", [H, 3], F32, kind="ExternalInput").ap()
    cb_h = nc.dram_tensor("cb", [H], F32, kind="ExternalInput").ap()
    w2_h = nc.dram_tensor("w2", [D, H], F32, kind="ExternalInput").ap()
    idf_h = nc.dram_tensor("idf", [128, 128], F32, kind="ExternalInput").ap()
    idb_h = nc.dram_tensor("idb", [128, 128], BF16, kind="ExternalInput").ap()

    y_h = nc.dram_tensor("y", [NB, S, D], F32, kind="ExternalOutput").ap()

    w1qT_h = nc.dram_tensor("w1qT_h", [128, DT * H], BF16, kind="Internal").ap()
    w2qT_h = nc.dram_tensor("w2qT_h", [128, HT * D], BF16, kind="Internal").ap()
    xsp_h = nc.dram_tensor("xsp_h", [S, D], F32, kind="Internal").ap()
    xq_h = nc.dram_tensor("xq_h", [S, D], BF16, kind="Internal").ap()

    BASE = 16512
    TOP = 229344

    def esz(dt_):
        return 4 if dt_ in (F32, I32) else 2

    cur = [BASE]

    def place(name, shape, dtype, at=None):
        nbytes = int(np.prod(shape[1:])) * esz(dtype)
        if at is None:
            at = (cur[0] + 31) & ~31
            cur[0] = at + nbytes
        assert at + nbytes <= TOP, f"SBUF overflow at {name}: {at + nbytes} > {TOP}"
        return nc.alloc_sbuf_tensor_at(name, list(shape), dtype, offset=at).ap(), at, at + nbytes

    def P(name, shape, dtype):
        return place(name, shape, dtype)[0]

    idf = P("idf_s", [128, 128], F32)
    idb = P("idb_s", [128, 128], BF16)
    onescol_b = P("onescol_b", [128, 1], BF16)
    onesrow_b = P("onesrow_b", [1, 256], BF16)
    onesrow_f = P("onesrow_f", [1, 128], F32)
    epscol = P("epscol", [128, 1], F32)
    colpk = P("colpk", [128, 24], F32)
    bqcol = P("bqcol", [128, DT], F32)
    bkcol = P("bkcol", [128, DT], F32)
    cwcol = P("cwcol", [128, 3 * HT], F32)
    cbcol = P("cbcol", [128, HT], F32)
    rowf = P("rowf", [1, 256], F32)
    rowb = P("rowb", [1, 256], BF16)
    c1col = P("c1col", [128, NT], F32)
    mcol16 = P("mcol16", [128, NT], F32)
    vcol16 = P("vcol16", [128, NT], F32)
    icol16 = P("icol16", [128, NT], F32)
    xrel = P("xrel", [128, D], F32)
    lnbuf, lnbuf_at, _ = place("lnbuf", [128, D], F32)
    q_i32 = place("q_i32", [128, D], I32, at=lnbuf_at)[0]
    sqb = P("sqb", [128, D], F32)
    ckb, ckb_at, _ = place("ckb", [128, D], F32)
    ibufB = place("ibufB", [128, D], I32, at=ckb_at)[0]

    ATT0 = (cur[0] + 63) & ~63
    a_cur = [ATT0]

    def PA(name, shape, dtype):
        at = (a_cur[0] + 31) & ~31
        ap, _, end = place(name, shape, dtype, at=at)
        a_cur[0] = end
        return ap

    qsqrow = PA("qsqrow", [1, S], BF16)
    ksqrow = PA("ksqrow", [1, S], BF16)
    sqb2a = PA("sqb2a", [128, 512], BF16)
    sqb2b = PA("sqb2b", [128, 512], BF16)
    lnta = PA("lnta", [128, D], BF16)
    lntb = PA("lntb", [128, D], BF16)
    rB = PA("rB", [128, CH], F32)
    sba = PA("sba", [128, CH], F32)
    ebc = PA("ebc", [128, CH], BF16)
    eba = PA("eba", [128, CH], BF16)
    ebb = PA("ebb", [128, CH], BF16)
    xrel2 = PA("xrel2", [128, D], F32)
    bvrow = PA("bvrow", [1, D], BF16)
    borow = PA("borow", [1, D], BF16)

    f_cur = [ATT0]

    def PF(name, shape, dtype):
        at = (f_cur[0] + 31) & ~31
        ap, _, end = place(name, shape, dtype, at=at)
        f_cur[0] = end
        return ap

    YS = PF("YS", [128, HT * FC], F32)
    hqb = PF("hqb", [128, HT * FC], BF16)
    cva = PF("cva", [128, 128], F32)
    cvb = PF("cvb", [128, 128], F32)
    cvc = PF("cvc", [128, 128], F32)
    cvd = PF("cvd", [128, 128], F32)
    cve = PF("cve", [128, 128], F32)
    cvf = PF("cvf", [128, 128], F32)
    r2B = PF("r2B", [128, FC], F32)

    PB = (max(a_cur[0], f_cur[0]) + 63) & ~63
    kT = place("kT", [128, DT * S], BF16, at=PB)[0]
    vtok = place("vtok", [128, NT * D], BF16, at=PB + 32768)[0]
    w1qT = place("w1qT", [128, DT * H], BF16, at=PB)[0]
    BIGA = place("BIGA", [128, DT * BIGW], BF16, at=PB + 65536)[0]
    WB = (PB + 65536 + 2 * DT * BIGW + 63) & ~63
    Wbf = place("Wbf", [128, DT * D], BF16, at=WB)[0]
    outn = place("outn", [128, DT * CH], BF16, at=WB + 16384)[0]
    q1c = place("q1c", [128, D], F32, at=WB + 16384)[0]
    wstage = place("wstage", [128, D], F32, at=WB + 16384)[0]
    Wbf2 = place("Wbf2", [128, DT * D], BF16, at=WB + 24576)[0]
    w2qT = place("w2qT", [128, HT * D], BF16, at=WB)[0]
    endF2 = WB + HT * D * 2
    assert endF2 <= TOP, f"arena overflow endF2={endF2} > {TOP}"
    wst = place("wst", [128, 4 * D], F32, at=PB)[0]
    wst2 = place("wst2", [128, 4 * D], F32, at=PB + 16384)[0]
    tbuf = place("tbuf", [128, 4 * D], BF16, at=PB + 32768)[0]
    stg = place("stg", [128, 4 * D], BF16, at=PB + 40960)[0]
    tern2 = place("tern2", [128, 4 * D], BF16, at=PB + 49152)[0]

    pbk = [nc.alloc_psum_tensor(f"pb{i}", [128, 512], F32).ap() for i in range(8)]
    nc.bump_psum(banks=-2)
    ptb = [nc.alloc_psum_tensor(f"ptb{i}", [128, 1024], BF16).ap() for i in range(2)]

    s_ld = Sem(nc, "s_ld")
    s_sp = Sem(nc, "s_sp")
    s_out = Sem(nc, "s_out")
    s_pe = Sem(nc, "s_pe")
    s_v = Sem(nc, "s_v")
    s_s = Sem(nc, "s_s")
    s_g = Sem(nc, "s_g")

    (C_M1, C_M2, C_T1, C_NT1, C_T2, C_NT2, C_MEAN, C_VAR, C_INV, C_NMI,
     C_AMX, C_R127, C_C2, C_VE, C_Y, C_TA, C_TB, C_SUM, C_M127_2, C_M127_1,
     C_DMY) = range(21)

    def col(i):
        return colpk[:, i:i + 1]

    TBL_LNEXP = 6
    TBL_SILU = 18

    with nc.Block() as blk:
        PIN_TABLES = False

        def pin_table(s, idx):
            if not PIN_TABLES:
                return
            s.add_instruction(mybir.InstLoadActFuncSet(
                name=nc.get_next_instruction_name(), act_func_set_id=idx, ins=[], outs=[]))

        def emit_newton(g, ve_ap, y_ap, ta=None, tb=None, niter=14):
            ta = col(C_TA) if ta is None else ta
            tb = col(C_TB) if tb is None else tb
            g.tensor_single_scalar(y_ap, ve_ap, 2.0, ALU.is_lt)
            for lo, hi, val in ((2.0, 8.0, 0.5), (8.0, 128.0, 0.125),
                                (128.0, 2048.0, 0.03125)):
                g.tensor_single_scalar(ta, ve_ap, lo, ALU.is_ge)
                g.tensor_single_scalar(tb, ve_ap, hi, ALU.is_lt)
                g.tensor_mul(ta, ta, tb)
                g.tensor_scalar(ta, ta, val, None, ALU.mult)
                g.tensor_add(y_ap, y_ap, ta)
            g.tensor_single_scalar(ta, ve_ap, 2048.0, ALU.is_ge)
            g.tensor_scalar(ta, ta, 0.0078125, None, ALU.mult)
            g.tensor_add(y_ap, y_ap, ta)
            for _i in range(niter):
                g.tensor_mul(ta, y_ap, y_ap)
                g.tensor_mul(ta, ta, ve_ap)
                g.tensor_scalar(ta, ta, -0.5, 1.5, ALU.mult, ALU.add)
                g.tensor_mul(y_ap, y_ap, ta)

        def vgap(v, n=1):
            for _ in range(n):
                v.tensor_single_scalar(col(C_DMY), epscol, 1.0, ALU.mult)

        def tap(name, src_ap):
            if not DBG:
                return
            h = nc.dram_tensor(name, list(src_ap.shape), src_ap.dtype, kind="ExternalOutput").ap()
            @blk.gpsimd
            def _(g):
                s_v.wait(g)
                s_s.wait(g)
                s_pe.wait(g)
                s_out.dinc(g.dma_start(h, src_ap))

        @blk.sync
        def _(sync):
            s_ld.dinc(sync.dma_start(idf, idf_h))
            s_ld.dinc(sync.dma_start(idb, idb_h))
            s_ld.dinc(sync.dma_start(bqcol, bq_h.rearrange("(t p) -> p t", p=128), allow_slow_non_contiguous=True))
            s_ld.dinc(sync.dma_start(bkcol, bk_h.rearrange("(t p) -> p t", p=128), allow_slow_non_contiguous=True))
            s_ld.dinc(sync.dma_start(cwcol.rearrange("p (t k) -> p t k", k=3), cw_h.rearrange("(t p) k -> p t k", p=128), allow_slow_non_contiguous=True))
            s_ld.dinc(sync.dma_start(cbcol, cb_h.rearrange("(t p) -> p t", p=128), allow_slow_non_contiguous=True))

        @blk.vector
        def _(v):
            s_ld.wait(v)
            v.memset(onescol_b, 1.0)
            v.memset(onesrow_b, 1.0)
            v.memset(onesrow_f, 1.0)
            s_v.inc(v.memset(epscol, EPS))

        for wi in range(2):
            w_src = w1_h if wi == 0 else w2_h
            nrow = H if wi == 0 else D
            NCHW = 8
            mcol_i = C_M1 if wi == 0 else C_M2
            tcol_i = C_T1 if wi == 0 else C_T2
            ntcol_i = C_NT1 if wi == 0 else C_NT2
            wview = w_src.rearrange("(c q p) d -> c p q d", c=NCHW, p=128)
            for ch in range(NCHW):
                buf = wst if ch % 2 == 0 else wst2
                @blk.sync
                def _(sync):
                    s_v.wait(sync)
                    s_ld.dinc(sync.dma_start(buf.rearrange("p (q d) -> p q d", d=D), wview[ch]))
                @blk.vector
                def _(v, ch=ch, buf=buf):
                    s_ld.wait(v)
                    if ch == 0:
                        s_v.inc(v.reduce_sum(col(C_TA), buf, AX.X, apply_absolute_value=True))
                    else:
                        v.reduce_sum(col(C_TB), buf, AX.X, apply_absolute_value=True)
                        vgap(v)
                        vgap(v)
                        s_v.inc(v.tensor_add(col(C_TA), col(C_TA), col(C_TB)))
            @blk.tensor
            def _(te):
                s_v.wait(te)
                s_pe.inc(te.matmul(pbk[6][0:1, 0:128], col(C_TA), idf, start=True, stop=True,
                                   skip_group_check=True))
            @blk.scalar
            def _(s, wi=wi):
                if wi == 0:
                    pin_table(s, TBL_LNEXP)
                s_pe.wait(s)
                s_s.inc(s.copy(rowf[0:1, 0:128], pbk[6][0:1, 0:128]))
            @blk.vector
            def _(v):
                s_s.wait(v)
                s_v.inc(v.reduce_sum(rowf[0:1, 128:129], rowf[0:1, 0:128], AX.X))
            @blk.tensor
            def _(te):
                s_v.wait(te)
                s_pe.inc(te.matmul(pbk[6][:, 128:129], onesrow_f, rowf[0:1, 128:129],
                                   start=True, stop=True, skip_group_check=True))
            @blk.vector
            def _(v, nrow=nrow, mcol_i=mcol_i, tcol_i=tcol_i, ntcol_i=ntcol_i, wi=wi):
                s_pe.wait(v)
                v.tensor_single_scalar(col(mcol_i), pbk[6][:, 128:129],
                                       1.0 / (nrow * (H * D // nrow)), ALU.mult)
                vgap(v)
                vgap(v)
                v.tensor_single_scalar(col(mcol_i), col(mcol_i), EPS, ALU.max)
                vgap(v)
                vgap(v)
                v.tensor_single_scalar(col(tcol_i), col(mcol_i), 0.5, ALU.mult)
                vgap(v)
                vgap(v)
                v.tensor_single_scalar(col(ntcol_i), col(tcol_i), -1.0, ALU.mult)
                s_v.inc(v.tensor_single_scalar(col(C_M127_2 if wi == 1 else C_M127_1),
                                                col(mcol_i), 1.0 / 127.0, ALU.mult))
            for ch in range(NCHW):
                buf = wst if ch % 2 == 0 else wst2
                @blk.sync
                def _(sync):
                    s_v.wait(sync)
                    s_ld.dinc(sync.dma_start(buf.rearrange("p (q d) -> p q d", d=D), wview[ch]))
                @blk.vector
                def _(v, ch=ch, buf=buf, tcol_i=tcol_i, ntcol_i=ntcol_i, wi=wi):
                    s_ld.wait(v)
                    if ch > 0 or wi > 0:
                        s_sp.wait(v)
                    v.tensor_scalar(tbuf, buf, col(tcol_i), None, ALU.is_ge)
                    v.tensor_scalar(tern2, buf, col(ntcol_i), None, ALU.is_le)
                    vgap(v)
                    s_v.inc(v.tensor_sub(tbuf, tbuf, tern2))
                for q in range(4):
                    for dt in range(DT):
                        bank = ptb[1][:, (dt % 4) * 128:(dt % 4) * 128 + 128]
                        @blk.tensor
                        def _(te, q=q, dt=dt, bank=bank):
                            s_v.wait(te)
                            s_pe.inc(te.transpose(bank, tbuf[:, q * D + dt * 128: q * D + dt * 128 + 128], idb))
                        @blk.vector
                        def _(v, q=q, dt=dt, bank=bank, wi=wi):
                            s_pe.wait(v)
                            if wi == 0:
                                dst = stg[:, dt * 512 + q * 128: dt * 512 + q * 128 + 128]
                            else:
                                dst = stg[:, (q * DT + dt) * 128: (q * DT + dt) * 128 + 128]
                            s_v.inc(v.tensor_copy(dst, bank))
                @blk.gpsimd
                def _(g, ch=ch, wi=wi):
                    s_v.wait(g)
                    if wi == 0:
                        dst = w1qT_h.rearrange("p (dt u) -> p dt u", dt=DT)[:, :, ch * 512:(ch + 1) * 512]
                        s_sp.dinc(g.dma_start(dst, stg.rearrange("p (dt u) -> p dt u", dt=DT)))
                    else:
                        dst = w2qT_h.rearrange("p (ht dd) -> p ht dd", ht=HT)[:, :, ch * 128:(ch + 1) * 128]
                        s_sp.dinc(g.dma_start(dst, stg.rearrange("p (ht j) -> p ht j", ht=HT)))

        def emit_var(v, src):
            v.reduce_sum(col(C_MEAN), src, AX.X)
            v.reduce_sum(col(C_VAR), sqb, AX.X)
            v.tensor_single_scalar(col(C_MEAN), col(C_MEAN), 1.0 / D, ALU.mult)
            v.tensor_single_scalar(col(C_VAR), col(C_VAR), 1.0 / D, ALU.mult)
            v.tensor_mul(col(C_SUM), col(C_MEAN), col(C_MEAN))
            vgap(v, 2)
            return v.tensor_sub(col(C_VAR), col(C_VAR), col(C_SUM))

        def emit_stats(src):
            @blk.scalar
            def _(s):
                s_v.wait(s)
                s_s.inc(s.activation(sqb, src, AF.Square))
            @blk.vector
            def _(v):
                s_s.wait(v)
                emit_var(v, src)
                vgap(v, 2)
                s_v.inc(v.tensor_single_scalar(col(C_VE), col(C_VAR), EPS, ALU.add))
            @blk.gpsimd
            def _(g):
                s_v.wait(g)
                emit_newton(g, col(C_VE), col(C_Y))
                s_g.inc(g.tensor_mul(col(C_NMI), col(C_MEAN), col(C_Y)))
            @blk.vector
            def _(v):
                s_g.wait(v)
                s_v.inc(v.tensor_single_scalar(col(C_NMI), col(C_NMI), -1.0, ALU.mult))

        tap("d_m", colpk)

        def emit_sqrow(slc, dstrow, pe_ready):
            _sb = s_s.n
            _pb = s_pe.n
            bank = pbk[4][0:1, 0:512]
            @blk.scalar
            def _(s):
                s.wait_ge(s_pe.h, pe_ready)
                k = 0
                for ncv in range(4):
                    for dt in range(DT):
                        sq = sqb2a if dt % 2 == 0 else sqb2b
                        if k >= 2:
                            s.wait_ge(s_pe.h, _pb + (k - 1))
                        s_s.inc(s.activation(sq, slc(dt, ncv * 512, ncv * 512 + 512), AF.Square))
                        k += 1
                    s.wait_ge(s_pe.h, _pb + (ncv + 1) * DT)
                    s_s.inc(s.activation(dstrow[0:1, ncv * 512:(ncv + 1) * 512], bank, AF.Identity,
                                         bias=0.0, scale=-0.5))
            @blk.tensor
            def _(te):
                k = 0
                for ncv in range(4):
                    for dt in range(DT):
                        sq = sqb2a if dt % 2 == 0 else sqb2b
                        te.wait_ge(s_s.h, _sb + k + 1 + ncv)
                        s_pe.inc(te.matmul(bank, onescol_b, sq, start=(dt == 0), stop=(dt == DT - 1),
                                           skip_group_check=True))
                        k += 1

        def batch(b):
            xr = x_h[b].rearrange("(t p) d -> t p d", p=128)

            @blk.sync
            def _(sync):
                s_s.wait(sync)
                s_ld.dinc(sync.dma_start(ckb[0:1, 0:D], bv_h.rearrange("d -> () d")))
                s_ld.dinc(sync.dma_start(sqb[0:1, 0:D], bo_h.rearrange("d -> () d")))
            @blk.vector
            def _(v):
                s_pe.wait(v)
                s_ld.wait(v)
                for dt in range(DT):
                    v.memset(BIGA[:, dt * BIGW: dt * BIGW + 1], 0.0)
                v.tensor_copy(bvrow, ckb[0:1, 0:D])
                v.tensor_copy(borow, sqb[0:1, 0:D])
                s_v.inc(v.memset(BIGA[:, (DT - 1) * BIGW + 2049: (DT - 1) * BIGW + 2050], 0.0))
            @blk.vector
            def _(v):
                for dt in range(DT - 1):
                    v.memset(BIGA[:, dt * BIGW + 2049: dt * BIGW + 2050], 0.0)
                s_v.inc(v.tensor_single_scalar(col(C_DMY), epscol, 1.0, ALU.mult))

            for t in range(NT):
                xb = xrel if t % 2 == 0 else xrel2
                @blk.sync
                def _(sync, t=t, xb=xb):
                    s_s.wait(sync)
                    s_v.wait(sync)
                    s_ld.dinc(sync.dma_start(xb, xr[t]))
                @blk.scalar
                def _(s, xb=xb, t=t):
                    s_ld.wait(s)
                    s_s.inc(s.activation(sqb, xb, AF.Square))
                @blk.vector
                def _(v, xb=xb, t=t):
                    s_s.wait(v)
                    v.reduce_sum(mcol16[:, t:t + 1], xb, AX.X)
                    v.reduce_sum(vcol16[:, t:t + 1], sqb, AX.X)
                    s_v.inc(v.tensor_single_scalar(mcol16[:, t:t + 1], mcol16[:, t:t + 1],
                                                   1.0 / D, ALU.mult))
            @blk.vector
            def _(v):
                v.tensor_single_scalar(vcol16, vcol16, 1.0 / D, ALU.mult)
                v.tensor_mul(icol16, mcol16, mcol16)
                vgap(v, 2)
                v.tensor_sub(vcol16, vcol16, icol16)
                vgap(v, 2)
                s_v.inc(v.tensor_single_scalar(vcol16, vcol16, EPS, ALU.add))
            @blk.gpsimd
            def _(g):
                s_v.wait(g)
                emit_newton(g, vcol16, icol16, sqb[:, 0:16], sqb[:, 16:32])
                s_g.inc(g.tensor_mul(mcol16, mcol16, icol16))
            @blk.vector
            def _(v):
                s_g.wait(v)
                s_v.inc(v.tensor_single_scalar(mcol16, mcol16, -1.0, ALU.mult))
            for t in range(NT):
                xb = xrel if t % 2 == 0 else xrel2
                lb = lnta if t % 2 == 0 else lntb
                @blk.sync
                def _(sync, t=t, xb=xb):
                    s_s.wait(sync)
                    s_v.wait(sync)
                    s_ld.dinc(sync.dma_start(xb, xr[t]))
                @blk.scalar
                def _(s, xb=xb, lb=lb, t=t):
                    s_ld.wait(s)
                    s_s.inc(s.activation(lb, xb, AF.Identity, bias=mcol16[:, t:t + 1],
                                         scale=icol16[:, t:t + 1]))
                _tb0 = s_pe.n
                _vb0 = s_v.n
                @blk.tensor
                def _(te, lb=lb, _vb0=_vb0):
                    s_s.wait(te)
                    te.wait_ge(s_v.h, _vb0)
                    for dt in range(DT):
                        bank = ptb[dt % 2][:, 0:128]
                        if dt >= 2:
                            te.wait_ge(s_v.h, _vb0 + dt - 1)
                        s_pe.inc(te.transpose(bank, lb[:, dt * 128: dt * 128 + 128], idb))
                @blk.vector
                def _(v, t=t, _tb0=_tb0):
                    for dt in range(DT):
                        bank = ptb[dt % 2][:, 0:128]
                        v.wait_ge(s_pe.h, _tb0 + dt + 1)
                        s_v.inc(v.tensor_copy(
                            BIGA[:, dt * BIGW + 1 + t * 128: dt * BIGW + 1 + t * 128 + 128], bank))

            def load_weight(w_src, dst, pe_guard=None):
                for dt in range(DT):
                    @blk.sync
                    def _(sync, dt=dt, w_src=w_src, pe_guard=pe_guard):
                        s_v.wait(sync)
                        if dt == 0 and pe_guard is not None:
                            sync.wait_ge(s_pe.h, pe_guard)
                        elif dt == 0:
                            s_pe.wait(sync)
                        s_ld.dinc(sync.dma_start(wstage, w_src[dt * 128:(dt + 1) * 128, :]))
                    @blk.vector
                    def _(v, dt=dt, dst=dst):
                        s_ld.wait(v)
                        s_v.inc(v.tensor_copy(dst[:, dt * D:(dt + 1) * D], wstage))

            load_weight(Wv_h, Wbf)
            _vev = {}
            @blk.tensor
            def _(te):
                s_v.wait(te)
                for mt in range(NT):
                    for ncv in range(2):
                        bank = pbk[(mt * 2 + ncv) % 4][:, 0:512]
                        k = mt * 2 + ncv
                        if k >= 4:
                            te.wait_ge(s_s.h, _vev[k - 4])
                        for dt in range(DT):
                            te.matmul(bank, BIGA[:, dt * BIGW + 1 + mt * 128: dt * BIGW + 1 + mt * 128 + 128],
                                      Wbf[:, dt * D + ncv * 512: dt * D + ncv * 512 + 512],
                                      start=(dt == 0), stop=False, skip_group_check=True)
                        s_pe.inc(te.matmul(bank, onesrow_b[0:1, 0:128], bvrow[:, ncv * 512: ncv * 512 + 512],
                                           start=False, stop=True, skip_group_check=True))
                        _vev[k] = s_s.n + k + 1
            m_vproj = s_pe.n
            @blk.scalar
            def _(s):
                base = s_pe.n - NT * 2
                for mt in range(NT):
                    for ncv in range(2):
                        k = mt * 2 + ncv
                        bank = pbk[k % 4][:, 0:512]
                        s.wait_ge(s_pe.h, base + k + 1)
                        s_s.inc(s.copy(vtok[:, mt * D + ncv * 512: mt * D + ncv * 512 + 512], bank))

            load_weight(Wk_h, Wbf2)
            _kev = {}
            _kpe0 = s_pe.n
            @blk.tensor
            def _(te):
                s_v.wait(te)
                for mt in range(DT):
                    for ncv in range(4):
                        k = mt * 4 + ncv
                        bank = pbk[k % 4][:, 0:512]
                        if k >= 4:
                            te.wait_ge(s_s.h, _kev[k - 4])
                        ii = None
                        for dt in range(DT):
                            ii = te.matmul(bank, Wbf2[:, dt * D + mt * 128: dt * D + mt * 128 + 128],
                                           BIGA[:, dt * BIGW + 1 + ncv * 512: dt * BIGW + 1 + ncv * 512 + 512],
                                           start=(dt == 0), stop=(dt == DT - 1), skip_group_check=True)
                        s_pe.inc(ii)
                        _kev[k] = s_s.n + k + 1
            m_kproj = s_pe.n
            @blk.scalar
            def _(s, _kpe0=_kpe0):
                for mt in range(DT):
                    for ncv in range(4):
                        k = mt * 4 + ncv
                        bank = pbk[k % 4][:, 0:512]
                        s.wait_ge(s_pe.h, _kpe0 + k + 1)
                        s_s.inc(s.activation(kT[:, mt * S + ncv * 512: mt * S + ncv * 512 + 512], bank,
                                             AF.Identity, bias=bkcol[:, mt:mt + 1], scale=1.0))
            emit_sqrow(lambda dt, lo, hi: kT[:, dt * S + lo: dt * S + hi], ksqrow, m_kproj)

            load_weight(Wq_h, Wbf, pe_guard=m_vproj)
            for ncv in range(4):
                @blk.tensor
                def _(te, ncv=ncv):
                    if ncv == 0:
                        s_v.wait(te)
                    s_s.wait(te)
                    ii = None
                    for mt in range(DT):
                        for dt in range(DT):
                            ii = te.matmul(pbk[mt][:, 0:512],
                                           Wbf[:, dt * D + mt * 128: dt * D + mt * 128 + 128],
                                           BIGA[:, dt * BIGW + 1 + ncv * 512: dt * BIGW + 1 + ncv * 512 + 512],
                                           start=(dt == 0), stop=(dt == DT - 1), skip_group_check=True)
                    s_pe.inc(ii)
                @blk.scalar
                def _(s, ncv=ncv):
                    s_pe.wait(s)
                    ii = None
                    for mt in range(DT):
                        ii = s.activation(BIGA[:, mt * BIGW + 1 + ncv * 512: mt * BIGW + 1 + ncv * 512 + 512],
                                          pbk[mt][:, 0:512], AF.Identity, bias=bqcol[:, mt:mt + 1], scale=1.0)
                    s_s.inc(ii)
            m_qproj = s_pe.n
            emit_sqrow(lambda dt, lo, hi: BIGA[:, dt * BIGW + 1 + lo: dt * BIGW + 1 + hi],
                       qsqrow, m_qproj)

            load_weight(Wo_h, Wbf2, pe_guard=m_kproj)
            if b == 0:
                tap("d_kt", kT)
                tap("d_vt", vtok)
                tap("d_q", BIGA)
                tap("d_qsq", qsqrow)
                tap("d_ksq", ksqrow)
                tap("d_wbf", Wbf)

            m_qk = {}
            m_exp = {}
            m_ev = {}
            m_evac = {}
            m_ckb = {}
            for c in range(NCH):
                opsum = [pbk[2 + i // 2][:, (i % 2) * CH:(i % 2) * CH + CH] for i in range(DT)]
                dn = pbk[6][0:1, 0:CH]
                _n = s_pe.n
                for step in range(NT + 2):
                    if step < NT:
                        _n += 1
                        m_qk[(c, step)] = _n
                    if step >= 2:
                        _n += 1
                        m_ev[(c, step - 2)] = _n
                for jt in range(NT):
                    m_exp[(c, jt)] = s_s.n + jt + 1
                EBS = [eba, ebb, ebc]
                QKB = [0, 1, 7]
                @blk.tensor
                def _(te, c=c):
                    for step in range(NT + 2):
                        jt = step
                        if jt < NT:
                            qk = pbk[QKB[jt % 3]][:, 0:CH]
                            prev = (c, jt - 3) if jt >= 3 else (c - 1, NT + jt - 3)
                            if prev in m_exp and (prev[0] >= 0):
                                te.wait_ge(s_s.h, m_exp[prev])
                            if jt == 0 and c > 0:
                                te.wait_ge(s_v.h, m_evac[c - 1])
                                te.wait_ge(s_s.h, m_ckb[c - 1])
                            for dt in range(DT):
                                te.matmul(qk, kT[:, dt * S + jt * 128: dt * S + jt * 128 + 128],
                                          BIGA[:, dt * BIGW + 1 + c * CH: dt * BIGW + 1 + c * CH + CH],
                                          start=(dt == 0), stop=False, skip_group_check=True)
                            te.matmul(qk, onesrow_b[0:1, 0:128], qsqrow[0:1, c * CH: c * CH + CH],
                                      start=False, stop=False, skip_group_check=True)
                            s_pe.inc(te.matmul(qk, ksqrow[0:1, jt * 128: jt * 128 + 128],
                                               onesrow_b[0:1, 0:CH],
                                               start=False, stop=True, skip_group_check=True))
                            assert s_pe.n == m_qk[(c, jt)], (s_pe.n, m_qk[(c, jt)])
                        if step >= 2:
                            jt1 = step - 2
                            eb1 = EBS[jt1 % 3]
                            te.wait_ge(s_s.h, m_exp[(c, jt1)])
                            for i in range(DT):
                                te.matmul(opsum[i], vtok[:, jt1 * D + i * 128: jt1 * D + i * 128 + 128],
                                          eb1, start=(jt1 == 0 and i % 2 == 0), stop=(jt1 == NT - 1),
                                          skip_group_check=True)
                            s_pe.inc(te.matmul(dn, onescol_b, eb1, start=(jt1 == 0), stop=(jt1 == NT - 1),
                                               skip_group_check=True))
                            assert s_pe.n == m_ev[(c, jt1)], (s_pe.n, m_ev[(c, jt1)])
                @blk.scalar
                def _(s, c=c):
                    for jt in range(NT):
                        qk = pbk[QKB[jt % 3]][:, 0:CH]
                        eb_ = EBS[jt % 3]
                        s.wait_ge(s_pe.h, m_qk[(c, jt)])
                        prev = (c, jt - 3) if jt >= 3 else (c - 1, NT + jt - 3)
                        if prev in m_ev and prev[0] >= 0:
                            s.wait_ge(s_pe.h, m_ev[prev])
                        s.activation(sba, qk, AF.Exp, bias=0.0, scale=2.0)
                        s_s.inc(s.activation(eb_, sba, AF.Exp, bias=0.0, scale=1.0))
                        assert s_s.n == m_exp[(c, jt)]
                if b == 0 and c == 0:
                    tap("d_sb", sba)
                    tap("d_eb", ebb)
                @blk.vector
                def _(v):
                    s_pe.wait(v)
                    v.reciprocal(rowf[0:1, 0:CH], dn)
                    vgap(v)
                    vgap(v)
                    s_v.inc(v.tensor_copy(rowb[0:1, 0:CH], rowf[0:1, 0:CH]))
                @blk.tensor
                def _(te):
                    s_v.wait(te)
                    s_pe.inc(te.matmul(pbk[6][:, CH:CH + CH], onesrow_b[0:1, 0:128], rowb[0:1, 0:CH],
                                       start=True, stop=True, skip_group_check=True))
                @blk.vector
                def _(v, c=c):
                    s_pe.wait(v)
                    v.tensor_copy(rB, pbk[6][:, CH:CH + CH])
                    vgap(v)
                    vgap(v)
                    ii = None
                    for i in range(DT):
                        ii = v.tensor_mul(outn[:, i * CH:(i + 1) * CH], opsum[i], rB)
                    s_v.inc(ii)
                    m_evac[c] = s_v.mark()
                if b == 0 and c == 0:
                    tap("d_outn", outn)
                    tap("d_rb", rB)
                    tap("d_rowf", rowf)
                for imt in range(2):
                    aop = [pbk[2 + 2 * imt + ncv][:, 0:512] for ncv in range(2)]
                    @blk.tensor
                    def _(te, imt=imt, aop=aop):
                        s_v.wait(te)
                        ii = None
                        for ncv in range(2):
                            for i in range(DT):
                                te.matmul(aop[ncv], outn[:, i * CH + imt * 128: i * CH + imt * 128 + 128],
                                          Wbf2[:, i * D + ncv * 512: i * D + ncv * 512 + 512],
                                          start=(i == 0), stop=False, skip_group_check=True)
                            ii = te.matmul(aop[ncv], onesrow_b[0:1, 0:128], borow[:, ncv * 512: ncv * 512 + 512],
                                           start=False, stop=True, skip_group_check=True)
                        s_pe.inc(ii)
                for imt in range(2):
                    aop = [pbk[2 + 2 * imt + ncv][:, 0:512] for ncv in range(2)]
                    t_glob = c * 2 + imt
                    xb = xrel if t_glob % 2 == 0 else xrel2
                    lnt = lnta if t_glob % 2 == 0 else lntb
                    sb0 = s_s.n
                    vb0 = s_v.n
                    gb0 = s_g.n
                    ldm = s_ld.n + 16
                    @blk.sync
                    def _(sync, t_glob=t_glob, xb=xb):
                        sync.wait_ge(s_sp.h, s_sp.n)
                        sync.wait_ge(s_v.h, s_v.n)
                        s_ld.dinc(sync.dma_start(xb, xr[t_glob]))
                    @blk.scalar
                    def _(s, aop=aop, xb=xb, sb0=sb0, vb0=vb0):
                        s_pe.wait(s)
                        s.wait_ge(s_v.h, vb0)
                        s.copy(ckb[:, 0:512], aop[0])
                        s_s.inc(s.copy(ckb[:, 512:1024], aop[1]))
                        s_s.inc(s.activation(sqb, ckb, AF.Square))
                        s.wait_ge(s_v.h, vb0 + 2)
                        s_s.inc(s.activation(lnbuf, ckb, AF.Identity, bias=col(C_NMI), scale=col(C_Y)))
                        s.wait_ge(s_v.h, vb0 + 3)
                        s_s.inc(s.activation(sqb, xb, AF.Square))
                        s.wait_ge(s_v.h, vb0 + 5)
                        s_s.inc(s.activation(lnbuf, xb, AF.Identity, bias=col(C_NMI), scale=col(C_Y)))
                    m_ckb[c] = sb0 + 1
                    @blk.vector
                    def _(v, xb=xb, sb0=sb0, gb0=gb0, ldm=ldm):
                        v.wait_ge(s_s.h, sb0 + 2)
                        emit_var(v, ckb)
                        vgap(v, 2)
                        s_v.inc(v.tensor_single_scalar(col(C_VE), col(C_VAR), EPS, ALU.add))
                        v.wait_ge(s_g.h, gb0 + 1)
                        s_v.inc(v.tensor_single_scalar(col(C_NMI), col(C_NMI), -1.0, ALU.mult))
                        v.wait_ge(s_s.h, sb0 + 3)
                        v.wait_ge(s_ld.h, ldm)
                        s_v.inc(v.tensor_add(xb, xb, lnbuf))
                        v.wait_ge(s_s.h, sb0 + 4)
                        emit_var(v, xb)
                        vgap(v, 2)
                        s_v.inc(v.tensor_single_scalar(col(C_VE), col(C_VAR), EPS, ALU.add))
                        v.wait_ge(s_g.h, gb0 + 2)
                        s_v.inc(v.tensor_single_scalar(col(C_NMI), col(C_NMI), -1.0, ALU.mult))
                    @blk.gpsimd
                    def _(g, t_glob=t_glob, xb=xb, vb0=vb0):
                        g.wait_ge(s_v.h, vb0 + 1)
                        emit_newton(g, col(C_VE), col(C_Y))
                        s_g.inc(g.tensor_mul(col(C_NMI), col(C_MEAN), col(C_Y)))
                        g.wait_ge(s_v.h, vb0 + 3)
                        s_sp.dinc(g.dma_start(xsp_h.rearrange("(t p) d -> t p d", p=128)[t_glob], xb))
                        g.wait_ge(s_v.h, vb0 + 4)
                        emit_newton(g, col(C_VE), col(C_Y))
                        s_g.inc(g.tensor_mul(col(C_NMI), col(C_MEAN), col(C_Y)))
                    @blk.vector
                    def _(v, t_glob=t_glob, sb0=sb0):
                        v.wait_ge(s_s.h, sb0 + 5)
                        v.reduce_max(col(C_AMX), lnbuf, AX.X, apply_absolute_value=True)
                        vgap(v, 2)
                        v.tensor_single_scalar(col(C_AMX), col(C_AMX), EPS, ALU.max)
                        vgap(v, 2)
                        v.reciprocal(col(C_C2), col(C_AMX))
                        v.tensor_mul(c1col[:, t_glob:t_glob + 1], col(C_AMX), col(C_M127_1))
                        vgap(v)
                        v.tensor_single_scalar(col(C_R127), col(C_C2), 127.0, ALU.mult)
                        vgap(v, 2)
                        v.tensor_scalar(sqb, lnbuf, col(C_R127), 128.0, ALU.mult, ALU.add)
                        vgap(v)
                        v.tensor_copy(q_i32, sqb)
                        vgap(v)
                        v.tensor_copy(q1c, q_i32)
                        vgap(v)
                        v.scalar_tensor_tensor(ckb, sqb, 0.5, q1c, ALU.add, ALU.subtract)
                        vgap(v)
                        v.tensor_single_scalar(ckb, ckb, 1.0, ALU.is_ge)
                        vgap(v)
                        v.scalar_tensor_tensor(q1c, q1c, -128.0, ckb, ALU.add, ALU.add)
                        vgap(v)
                        s_v.inc(v.tensor_scalar(lnt, q1c, c1col[:, t_glob:t_glob + 1], None, ALU.mult))
                    @blk.gpsimd
                    def _(g, t_glob=t_glob, lnt=lnt):
                        s_v.wait(g)
                        s_sp.dinc(g.dma_start(xq_h.rearrange("(t p) d -> t p d", p=128)[t_glob], lnt))

            @blk.sync
            def _(sync):
                s_pe.wait(sync)
                s_sp.wait(sync)
                s_s.wait(sync)
                s_ld.dinc(sync.dma_start(w1qT, w1qT_h))
                s_ld.dinc(sync.dma_start(w2qT, w2qT_h))
            for t in range(NT):
                lb = lnta if t % 2 == 0 else lntb
                @blk.sync
                def _(sync, t=t, lb=lb):
                    s_v.wait(sync)
                    s_ld.dinc(sync.dma_start(lb, xq_h.rearrange("(t p) d -> t p d", p=128)[t]))
                _tb0 = s_pe.n
                _vb0 = s_v.n
                @blk.tensor
                def _(te, lb=lb, _vb0=_vb0):
                    s_ld.wait(te)
                    te.wait_ge(s_v.h, _vb0)
                    for dt in range(DT):
                        bank = ptb[dt % 2][:, 0:128]
                        if dt >= 2:
                            te.wait_ge(s_v.h, _vb0 + dt - 1)
                        s_pe.inc(te.transpose(bank, lb[:, dt * 128: dt * 128 + 128], idb))
                @blk.vector
                def _(v, t=t, _tb0=_tb0):
                    for dt in range(DT):
                        bank = ptb[dt % 2][:, 0:128]
                        v.wait_ge(s_pe.h, _tb0 + dt + 1)
                        s_v.inc(v.tensor_copy(
                            BIGA[:, dt * BIGW + 1 + t * 128: dt * BIGW + 1 + t * 128 + 128], bank))

            if b == 0:
                tap("d_xqt", BIGA)
                tap("d_w1", w1qT)
                tap("d_w2", w2qT)

            m_u2 = {}
            m_silu = {}
            m_hmm = {}
            for c in range(NFC):
                _vb = s_v.n
                for s_i in range(HT + 2):
                    if s_i - 2 >= 0:
                        m_u2[(c, s_i - 2)] = _vb + s_i + 1
                _sb = s_s.n
                for ht in range(HT):
                    m_silu[(c, ht)] = _sb + ht + 1
                @blk.tensor
                def _(te, c=c):
                    if c == 0:
                        te.wait_ge(s_ld.h, s_ld.n)
                        te.wait_ge(s_v.h, _vb)
                        te.wait_ge(s_s.h, _sb)
                    for ht in range(HT):
                        hb = pbk[ht % 4][:, 0:130]
                        prev = (c, ht - 4) if ht >= 4 else (c - 1, HT + ht - 4)
                        if prev in m_u2 and prev[0] >= 0:
                            te.wait_ge(s_v.h, m_u2[prev])
                        ii = None
                        for dt in range(DT):
                            ii = te.matmul(hb, w1qT[:, dt * H + ht * 128: dt * H + ht * 128 + 128],
                                           BIGA[:, dt * BIGW + c * FC: dt * BIGW + c * FC + 130],
                                           start=(dt == 0), stop=(dt == DT - 1), skip_group_check=True)
                        s_pe.inc(ii)
                        m_hmm[(c, ht)] = s_pe.n
                @blk.vector
                def _(v, c=c, _vb=_vb):
                    for s_i in range(HT + 2):
                        ht0, ht1, ht2 = s_i, s_i - 1, s_i - 2
                        did = None
                        if ht0 < HT:
                            v.wait_ge(s_pe.h, m_hmm[(c, ht0)])
                            hb = pbk[ht0 % 4][:, 0:130]
                            did = v.tensor_scalar(cva if ht0 % 2 == 0 else cvb, hb[:, 0:128],
                                                  cwcol[:, ht0 * 3: ht0 * 3 + 1], None, ALU.mult)
                        if 0 <= ht1 < HT:
                            hb = pbk[ht1 % 4][:, 0:130]
                            did = v.scalar_tensor_tensor(cvc if ht1 % 2 == 0 else cvd, hb[:, 1:129],
                                                         cwcol[:, ht1 * 3 + 1: ht1 * 3 + 2],
                                                         cva if ht1 % 2 == 0 else cvb,
                                                         ALU.mult, ALU.add)
                        if 0 <= ht2 < HT:
                            if (c, ht2 - 2) in m_silu:
                                v.wait_ge(s_s.h, m_silu[(c, ht2 - 2)])
                            hb = pbk[ht2 % 4][:, 0:130]
                            did = v.scalar_tensor_tensor(cve if ht2 % 2 == 0 else cvf, hb[:, 2:130],
                                                         cwcol[:, ht2 * 3 + 2: ht2 * 3 + 3],
                                                         cvc if ht2 % 2 == 0 else cvd,
                                                         ALU.mult, ALU.add)
                        s_v.inc(did)
                        assert s_v.n == _vb + s_i + 1
                @blk.scalar
                def _(s, c=c):
                    for ht in range(HT):
                        s.wait_ge(s_v.h, m_u2[(c, ht)])
                        s_s.inc(s.activation(YS[:, ht * FC:(ht + 1) * FC],
                                             cve if ht % 2 == 0 else cvf, AF.Silu,
                                             bias=cbcol[:, ht:ht + 1], scale=1.0))
                        assert s_s.n == m_silu[(c, ht)]
                @blk.vector
                def _(v):
                    s_s.wait(v)
                    s_v.inc(v.reduce_max(cva, YS.rearrange("p (ht i) -> p i ht", ht=HT), AX.X,
                                         apply_absolute_value=True))
                @blk.tensor
                def _(te):
                    s_v.wait(te)
                    s_pe.inc(te.transpose(pbk[6][:, 0:128], cva, idf))
                @blk.vector
                def _(v):
                    s_pe.wait(v)
                    v.reduce_max(col(C_AMX), pbk[6][:, 0:128], AX.X)
                    vgap(v)
                    vgap(v)
                    v.tensor_single_scalar(col(C_AMX), col(C_AMX), EPS, ALU.max)
                    vgap(v)
                    vgap(v)
                    v.reciprocal(col(C_R127), col(C_AMX))
                    v.tensor_mul(col(C_C2), col(C_AMX), col(C_M127_2))
                    vgap(v)
                    s_v.inc(v.tensor_single_scalar(col(C_R127), col(C_R127), 127.0, ALU.mult))
                @blk.tensor
                def _(te):
                    s_v.wait(te)
                    s_pe.inc(te.matmul(pbk[6][0:1, 128:256], col(C_R127), idf, start=True, stop=True,
                                       skip_group_check=True))
                @blk.scalar
                def _(s):
                    s_pe.wait(s)
                    s_s.inc(s.copy(rowf[0:1, 0:FC], pbk[6][0:1, 128:256]))
                @blk.tensor
                def _(te):
                    s_s.wait(te)
                    s_pe.inc(te.matmul(pbk[6][:, 256:384], onesrow_f, rowf[0:1, 0:FC],
                                       start=True, stop=True, skip_group_check=True))
                @blk.vector
                def _(v):
                    s_pe.wait(v)
                    s_v.inc(v.tensor_copy(r2B, pbk[6][:, 256:384]))
                for qq in range(4):
                    sl = slice(qq * 8 * FC, (qq + 1) * 8 * FC)
                    @blk.vector
                    def _(v, sl=sl, c=c, qq=qq):
                        s_v.wait(v)
                        if qq == 0:
                            s_s.wait(v)
                            v.wait_ge(s_out.h, s_out.n)
                            if c > 0:
                                s_pe.wait(v)
                        ysq = YS[:, sl]
                        vgap(v)
                        v.tensor_tensor(ysq.rearrange("p (a i) -> p a i", a=8),
                                        ysq.rearrange("p (a i) -> p a i", a=8),
                                        r2B.rearrange("p i -> p () i").broadcast_to([128, 8, FC]),
                                        ALU.mult)
                        vgap(v)
                        v.tensor_single_scalar(ysq, ysq, 128.0, ALU.add)
                        vgap(v)
                        v.tensor_copy(q_i32, ysq)
                        vgap(v)
                        v.tensor_copy(sqb, q_i32)
                        vgap(v)
                        v.scalar_tensor_tensor(ysq, ysq, 0.5, sqb, ALU.add, ALU.subtract)
                        vgap(v)
                        v.tensor_single_scalar(ysq, ysq, 1.0, ALU.is_ge)
                        vgap(v)
                        s_v.inc(v.scalar_tensor_tensor(hqb[:, sl], sqb, -128.0, ysq, ALU.add, ALU.add))
                for ncv in range(2):
                    bank = pbk[4 + ncv][:, 0:512]
                    @blk.tensor
                    def _(te, c=c, ncv=ncv, bank=bank):
                        s_v.wait(te)
                        s_s.wait(te)
                        ii = None
                        for ht in range(HT):
                            ii = te.matmul(bank, hqb[:, ht * FC: ht * FC + 128],
                                           w2qT[:, ht * D + ncv * 512: ht * D + ncv * 512 + 512],
                                           start=(ht == 0), stop=(ht == HT - 1), skip_group_check=True)
                        s_pe.inc(ii)
                m_w2 = s_pe.n
                @blk.sync
                def _(sync, c=c):
                    s_v.wait(sync)
                    s_out.wait(sync, s_out.n)
                    s_ld.dinc(sync.dma_start(xrel, xsp_h.rearrange("(t p) d -> t p d", p=128)[c]))
                _sb2 = s_s.n
                _vb2 = s_v.n
                _outb = s_out.n
                @blk.scalar
                def _(s, c=c, m_w2=m_w2, _vb2=_vb2, _outb=_outb):
                    s.wait_ge(s_pe.h, m_w2)
                    s_s.inc(s.activation(ckb[:, 0:512], pbk[4][:, 0:512], AF.Identity,
                                         bias=0.0, scale=col(C_C2)))
                    s_s.inc(s.activation(ckb[:, 512:1024], pbk[5][:, 0:512], AF.Identity,
                                         bias=0.0, scale=col(C_C2)))
                    s_s.inc(s.activation(sqb, ckb, AF.Square))
                    s.wait_ge(s_v.h, _vb2 + 2)
                    s.wait_ge(s_out.h, _outb)
                    s_s.inc(s.activation(lnbuf, ckb, AF.Identity, bias=col(C_NMI), scale=col(C_Y)))
                m_sq = _sb2 + 3
                m_lnfo = _sb2 + 4
                @blk.vector
                def _(v, m_sq=m_sq):
                    v.wait_ge(s_s.h, m_sq)
                    emit_var(v, ckb)
                    vgap(v, 2)
                    s_v.inc(v.tensor_single_scalar(col(C_VE), col(C_VAR), EPS, ALU.add))
                @blk.gpsimd
                def _(g):
                    s_v.wait(g)
                    emit_newton(g, col(C_VE), col(C_Y))
                    s_g.inc(g.tensor_mul(col(C_NMI), col(C_MEAN), col(C_Y)))
                @blk.vector
                def _(v):
                    s_g.wait(v)
                    s_v.inc(v.tensor_single_scalar(col(C_NMI), col(C_NMI), -1.0, ALU.mult))
                @blk.vector
                def _(v, m_lnfo=m_lnfo):
                    v.wait_ge(s_s.h, m_lnfo)
                    s_ld.wait(v)
                    s_v.inc(v.tensor_add(lnbuf, lnbuf, xrel))
                @blk.gpsimd
                def _(g, b=b, c=c):
                    s_v.wait(g)
                    s_out.dinc(g.dma_start(y_h[b].rearrange("(t p) d -> t p d", p=128)[c], lnbuf))

        for b in range(NB):
            batch(b)

        @blk.gpsimd
        def _(g):
            g.wait_ge(s_out.h, s_out.n)

    return nc


def kernel(L0_attn_x, L0_attn_Wq, L0_attn_bq, L0_attn_Wk, L0_attn_bk, L0_attn_Wv,
           L0_attn_bv, L0_attn_Wo, L0_attn_bo, L0_ffn_w1, L0_ffn_conv_w,
           L0_ffn_conv_b, L0_ffn_w2):
    x = np.ascontiguousarray(np.asarray(L0_attn_x, dtype=np.float32))
    assert x.shape == (32, S, D)
    base = dict(
        Wq=np.ascontiguousarray(L0_attn_Wq, dtype=np.float32),
        Wk=np.ascontiguousarray(L0_attn_Wk, dtype=np.float32),
        Wv=np.ascontiguousarray(L0_attn_Wv, dtype=np.float32),
        Wo=np.ascontiguousarray(L0_attn_Wo, dtype=np.float32),
        bq=np.ascontiguousarray(L0_attn_bq, dtype=np.float32),
        bk=np.ascontiguousarray(L0_attn_bk, dtype=np.float32),
        bv=np.ascontiguousarray(L0_attn_bv, dtype=np.float32),
        bo=np.ascontiguousarray(L0_attn_bo, dtype=np.float32),
        w1=np.ascontiguousarray(L0_ffn_w1, dtype=np.float32),
        cw=np.ascontiguousarray(np.asarray(L0_ffn_conv_w, dtype=np.float32).reshape(H, 3)),
        cb=np.ascontiguousarray(L0_ffn_conv_b, dtype=np.float32),
        w2=np.ascontiguousarray(L0_ffn_w2, dtype=np.float32),
        idf=np.eye(128, dtype=np.float32),
        idb=np.eye(128, dtype=ml_dtypes.bfloat16),
    )
    nc = build_nc()
    in_maps = [dict(base, x=np.ascontiguousarray(x[4 * c:4 * c + 4])) for c in range(8)]
    res = run_bass_kernel_spmd(nc, in_maps, core_ids=list(range(8)))
    return np.concatenate([r["y"] for r in res.results], axis=0)


# version 09
# speedup vs baseline: 1.6707x; 1.0224x over previous
import numpy as np
import ml_dtypes
import concourse.bass as bass
import concourse.mybir as mybir
from concourse.bass_utils import run_bass_kernel_spmd

F32 = mybir.dt.float32
BF16 = mybir.dt.bfloat16
I32 = mybir.dt.int32
AF = mybir.ActivationFunctionType
ALU = mybir.AluOpType
AX = mybir.AxisListType

NB = 4
S = 2048
D = 1024
H = 4096
NT = S // 128
DT = D // 128
HT = H // 128
CH = 256
NCH = S // CH
FC = 128
NFC = S // FC
EPS = 1e-5
BIGW = 2050
DBG = False


class Sem:
    def __init__(self, nc, name):
        self.h = nc.alloc_semaphore(name)
        self.n = 0

    def inc(self, instr, k=1):
        self.n += k
        instr.then_inc(self.h, k)
        return instr

    def dinc(self, instr):
        self.n += 16
        instr.then_inc(self.h, 16)
        return instr

    def wait(self, eng, val=None):
        eng.wait_ge(self.h, self.n if val is None else val)

    def mark(self):
        return self.n


def build_nc():
    nc = bass.Bass("TRN2", target_bir_lowering=False)

    x_h = nc.dram_tensor("x", [NB, S, D], F32, kind="ExternalInput").ap()
    Wq_h = nc.dram_tensor("Wq", [D, D], F32, kind="ExternalInput").ap()
    Wk_h = nc.dram_tensor("Wk", [D, D], F32, kind="ExternalInput").ap()
    Wv_h = nc.dram_tensor("Wv", [D, D], F32, kind="ExternalInput").ap()
    Wo_h = nc.dram_tensor("Wo", [D, D], F32, kind="ExternalInput").ap()
    bq_h = nc.dram_tensor("bq", [D], F32, kind="ExternalInput").ap()
    bk_h = nc.dram_tensor("bk", [D], F32, kind="ExternalInput").ap()
    bv_h = nc.dram_tensor("bv", [D], F32, kind="ExternalInput").ap()
    bo_h = nc.dram_tensor("bo", [D], F32, kind="ExternalInput").ap()
    w1_h = nc.dram_tensor("w1", [H, D], F32, kind="ExternalInput").ap()
    cw_h = nc.dram_tensor("cw", [H, 3], F32, kind="ExternalInput").ap()
    cb_h = nc.dram_tensor("cb", [H], F32, kind="ExternalInput").ap()
    w2_h = nc.dram_tensor("w2", [D, H], F32, kind="ExternalInput").ap()
    idf_h = nc.dram_tensor("idf", [128, 128], F32, kind="ExternalInput").ap()
    idb_h = nc.dram_tensor("idb", [128, 128], BF16, kind="ExternalInput").ap()

    y_h = nc.dram_tensor("y", [NB, S, D], F32, kind="ExternalOutput").ap()

    w1qT_h = nc.dram_tensor("w1qT_h", [128, DT * H], BF16, kind="Internal").ap()
    w2qT_h = nc.dram_tensor("w2qT_h", [128, HT * D], BF16, kind="Internal").ap()
    xsp_h = nc.dram_tensor("xsp_h", [S, D], F32, kind="Internal").ap()
    xq_h = nc.dram_tensor("xq_h", [S, D], BF16, kind="Internal").ap()

    BASE = 16512
    TOP = 229344

    def esz(dt_):
        return 4 if dt_ in (F32, I32) else 2

    cur = [BASE]

    def place(name, shape, dtype, at=None):
        nbytes = int(np.prod(shape[1:])) * esz(dtype)
        if at is None:
            at = (cur[0] + 31) & ~31
            cur[0] = at + nbytes
        assert at + nbytes <= TOP, f"SBUF overflow at {name}: {at + nbytes} > {TOP}"
        return nc.alloc_sbuf_tensor_at(name, list(shape), dtype, offset=at).ap(), at, at + nbytes

    def P(name, shape, dtype):
        return place(name, shape, dtype)[0]

    idf = P("idf_s", [128, 128], F32)
    idb = P("idb_s", [128, 128], BF16)
    onescol_b = P("onescol_b", [128, 1], BF16)
    onesrow_b = P("onesrow_b", [1, 256], BF16)
    onesrow_f = P("onesrow_f", [1, 128], F32)
    epscol = P("epscol", [128, 1], F32)
    colpk = P("colpk", [128, 24], F32)
    bqcol = P("bqcol", [128, DT], F32)
    bkcol = P("bkcol", [128, DT], F32)
    cwcol = P("cwcol", [128, 3 * HT], F32)
    cbcol = P("cbcol", [128, HT], F32)
    rowf = P("rowf", [1, 256], F32)
    rowb = P("rowb", [1, 256], BF16)
    c1col = P("c1col", [128, NT], F32)
    mcol16 = P("mcol16", [128, NT], F32)
    vcol16 = P("vcol16", [128, NT], F32)
    icol16 = P("icol16", [128, NT], F32)
    xrel = P("xrel", [128, D], F32)
    lnbuf, lnbuf_at, _ = place("lnbuf", [128, D], F32)
    q_i32 = place("q_i32", [128, D], I32, at=lnbuf_at)[0]
    sqb = P("sqb", [128, D], F32)
    ckb, ckb_at, _ = place("ckb", [128, D], F32)
    ibufB = place("ibufB", [128, D], I32, at=ckb_at)[0]

    ATT0 = (cur[0] + 63) & ~63
    a_cur = [ATT0]

    def PA(name, shape, dtype):
        at = (a_cur[0] + 31) & ~31
        ap, _, end = place(name, shape, dtype, at=at)
        a_cur[0] = end
        return ap

    qsqrow = PA("qsqrow", [1, S], BF16)
    ksqrow = PA("ksqrow", [1, S], BF16)
    sqb2a = PA("sqb2a", [128, 512], BF16)
    sqb2b = PA("sqb2b", [128, 512], BF16)
    lnta = PA("lnta", [128, D], BF16)
    lntb = PA("lntb", [128, D], BF16)
    rB = PA("rB", [128, CH], F32)
    sba = PA("sba", [128, CH], F32)
    ebc = PA("ebc", [128, CH], BF16)
    eba = PA("eba", [128, CH], BF16)
    ebb = PA("ebb", [128, CH], BF16)
    xrel2 = PA("xrel2", [128, D], F32)
    bvrow = PA("bvrow", [1, D], BF16)
    borow = PA("borow", [1, D], BF16)

    f_cur = [ATT0]

    def PF(name, shape, dtype):
        at = (f_cur[0] + 31) & ~31
        ap, _, end = place(name, shape, dtype, at=at)
        f_cur[0] = end
        return ap

    YS = PF("YS", [128, HT * FC], F32)
    hqb = PF("hqb", [128, HT * FC], BF16)
    cva = PF("cva", [128, 128], F32)
    cvb = PF("cvb", [128, 128], F32)
    cvc = PF("cvc", [128, 128], F32)
    cvd = PF("cvd", [128, 128], F32)
    cve = PF("cve", [128, 128], F32)
    cvf = PF("cvf", [128, 128], F32)
    r2B = PF("r2B", [128, FC], F32)

    PB = (max(a_cur[0], f_cur[0]) + 63) & ~63
    kT = place("kT", [128, DT * S], BF16, at=PB)[0]
    vtok = place("vtok", [128, NT * D], BF16, at=PB + 32768)[0]
    w1qT = place("w1qT", [128, DT * H], BF16, at=PB)[0]
    BIGA = place("BIGA", [128, DT * BIGW], BF16, at=PB + 65536)[0]
    WB = (PB + 65536 + 2 * DT * BIGW + 63) & ~63
    Wbf = place("Wbf", [128, DT * D], BF16, at=WB)[0]
    outn = place("outn", [128, DT * CH], BF16, at=WB + 16384)[0]
    q1c = place("q1c", [128, D], F32, at=WB + 16384)[0]
    wstage = place("wstage", [128, D], F32, at=WB + 16384)[0]
    Wbf2 = place("Wbf2", [128, DT * D], BF16, at=WB + 24576)[0]
    w2qT = place("w2qT", [128, HT * D], BF16, at=WB)[0]
    endF2 = WB + HT * D * 2
    assert endF2 <= TOP, f"arena overflow endF2={endF2} > {TOP}"
    wst = place("wst", [128, 4 * D], F32, at=PB)[0]
    wst2 = place("wst2", [128, 4 * D], F32, at=PB + 16384)[0]
    tbuf = place("tbuf", [128, 4 * D], BF16, at=PB + 32768)[0]
    stg = place("stg", [128, 4 * D], BF16, at=PB + 40960)[0]
    tern2 = place("tern2", [128, 4 * D], BF16, at=PB + 49152)[0]

    pbk = [nc.alloc_psum_tensor(f"pb{i}", [128, 512], F32).ap() for i in range(8)]
    nc.bump_psum(banks=-2)
    ptb = [nc.alloc_psum_tensor(f"ptb{i}", [128, 1024], BF16).ap() for i in range(2)]

    s_ld = Sem(nc, "s_ld")
    s_sp = Sem(nc, "s_sp")
    s_out = Sem(nc, "s_out")
    s_pe = Sem(nc, "s_pe")
    s_v = Sem(nc, "s_v")
    s_s = Sem(nc, "s_s")
    s_g = Sem(nc, "s_g")

    (C_M1, C_M2, C_T1, C_NT1, C_T2, C_NT2, C_MEAN, C_VAR, C_INV, C_NMI,
     C_AMX, C_R127, C_C2, C_VE, C_Y, C_TA, C_TB, C_SUM, C_M127_2, C_M127_1,
     C_DMY) = range(21)

    def col(i):
        return colpk[:, i:i + 1]

    TBL_LNEXP = 6
    TBL_SILU = 18

    with nc.Block() as blk:
        PIN_TABLES = False

        def pin_table(s, idx):
            if not PIN_TABLES:
                return
            s.add_instruction(mybir.InstLoadActFuncSet(
                name=nc.get_next_instruction_name(), act_func_set_id=idx, ins=[], outs=[]))

        def emit_newton(g, ve_ap, y_ap, ta=None, tb=None, niter=14):
            ta = col(C_TA) if ta is None else ta
            tb = col(C_TB) if tb is None else tb
            g.tensor_single_scalar(y_ap, ve_ap, 2.0, ALU.is_lt)
            for lo, hi, val in ((2.0, 8.0, 0.5), (8.0, 128.0, 0.125),
                                (128.0, 2048.0, 0.03125)):
                g.tensor_single_scalar(ta, ve_ap, lo, ALU.is_ge)
                g.tensor_single_scalar(tb, ve_ap, hi, ALU.is_lt)
                g.tensor_mul(ta, ta, tb)
                g.tensor_scalar(ta, ta, val, None, ALU.mult)
                g.tensor_add(y_ap, y_ap, ta)
            g.tensor_single_scalar(ta, ve_ap, 2048.0, ALU.is_ge)
            g.tensor_scalar(ta, ta, 0.0078125, None, ALU.mult)
            g.tensor_add(y_ap, y_ap, ta)
            for _i in range(niter):
                g.tensor_mul(ta, y_ap, y_ap)
                g.tensor_mul(ta, ta, ve_ap)
                g.tensor_scalar(ta, ta, -0.5, 1.5, ALU.mult, ALU.add)
                g.tensor_mul(y_ap, y_ap, ta)

        def vgap(v, n=1):
            for _ in range(n):
                v.tensor_single_scalar(col(C_DMY), epscol, 1.0, ALU.mult)

        def tap(name, src_ap):
            if not DBG:
                return
            h = nc.dram_tensor(name, list(src_ap.shape), src_ap.dtype, kind="ExternalOutput").ap()
            @blk.gpsimd
            def _(g):
                s_v.wait(g)
                s_s.wait(g)
                s_pe.wait(g)
                s_out.dinc(g.dma_start(h, src_ap))

        @blk.sync
        def _(sync):
            s_ld.dinc(sync.dma_start(idf, idf_h))
            s_ld.dinc(sync.dma_start(idb, idb_h))
            s_ld.dinc(sync.dma_start(bqcol, bq_h.rearrange("(t p) -> p t", p=128), allow_slow_non_contiguous=True))
            s_ld.dinc(sync.dma_start(bkcol, bk_h.rearrange("(t p) -> p t", p=128), allow_slow_non_contiguous=True))
            s_ld.dinc(sync.dma_start(cwcol.rearrange("p (t k) -> p t k", k=3), cw_h.rearrange("(t p) k -> p t k", p=128), allow_slow_non_contiguous=True))
            s_ld.dinc(sync.dma_start(cbcol, cb_h.rearrange("(t p) -> p t", p=128), allow_slow_non_contiguous=True))

        @blk.vector
        def _(v):
            s_ld.wait(v)
            v.memset(onescol_b, 1.0)
            v.memset(onesrow_b, 1.0)
            v.memset(onesrow_f, 1.0)
            s_v.inc(v.memset(epscol, EPS))

        for wi in range(2):
            w_src = w1_h if wi == 0 else w2_h
            nrow = H if wi == 0 else D
            NCHW = 8
            mcol_i = C_M1 if wi == 0 else C_M2
            tcol_i = C_T1 if wi == 0 else C_T2
            ntcol_i = C_NT1 if wi == 0 else C_NT2
            wview = w_src.rearrange("(c q p) d -> c p q d", c=NCHW, p=128)
            for ch in range(NCHW):
                buf = wst if ch % 2 == 0 else wst2
                @blk.sync
                def _(sync):
                    s_v.wait(sync)
                    s_ld.dinc(sync.dma_start(buf.rearrange("p (q d) -> p q d", d=D), wview[ch]))
                @blk.vector
                def _(v, ch=ch, buf=buf):
                    s_ld.wait(v)
                    if ch == 0:
                        s_v.inc(v.reduce_sum(col(C_TA), buf, AX.X, apply_absolute_value=True))
                    else:
                        v.reduce_sum(col(C_TB), buf, AX.X, apply_absolute_value=True)
                        vgap(v)
                        vgap(v)
                        s_v.inc(v.tensor_add(col(C_TA), col(C_TA), col(C_TB)))
            @blk.tensor
            def _(te):
                s_v.wait(te)
                s_pe.inc(te.matmul(pbk[6][0:1, 0:128], col(C_TA), idf, start=True, stop=True,
                                   skip_group_check=True))
            @blk.scalar
            def _(s, wi=wi):
                if wi == 0:
                    pin_table(s, TBL_LNEXP)
                s_pe.wait(s)
                s_s.inc(s.copy(rowf[0:1, 0:128], pbk[6][0:1, 0:128]))
            @blk.vector
            def _(v):
                s_s.wait(v)
                s_v.inc(v.reduce_sum(rowf[0:1, 128:129], rowf[0:1, 0:128], AX.X))
            @blk.tensor
            def _(te):
                s_v.wait(te)
                s_pe.inc(te.matmul(pbk[6][:, 128:129], onesrow_f, rowf[0:1, 128:129],
                                   start=True, stop=True, skip_group_check=True))
            @blk.vector
            def _(v, nrow=nrow, mcol_i=mcol_i, tcol_i=tcol_i, ntcol_i=ntcol_i, wi=wi):
                s_pe.wait(v)
                v.tensor_single_scalar(col(mcol_i), pbk[6][:, 128:129],
                                       1.0 / (nrow * (H * D // nrow)), ALU.mult)
                vgap(v)
                vgap(v)
                v.tensor_single_scalar(col(mcol_i), col(mcol_i), EPS, ALU.max)
                vgap(v)
                vgap(v)
                v.tensor_single_scalar(col(tcol_i), col(mcol_i), 0.5, ALU.mult)
                vgap(v)
                vgap(v)
                v.tensor_single_scalar(col(ntcol_i), col(tcol_i), -1.0, ALU.mult)
                s_v.inc(v.tensor_single_scalar(col(C_M127_2 if wi == 1 else C_M127_1),
                                                col(mcol_i), 1.0 / 127.0, ALU.mult))
            for ch in range(NCHW):
                buf = wst if ch % 2 == 0 else wst2
                @blk.sync
                def _(sync):
                    s_v.wait(sync)
                    s_ld.dinc(sync.dma_start(buf.rearrange("p (q d) -> p q d", d=D), wview[ch]))
                @blk.vector
                def _(v, ch=ch, buf=buf, tcol_i=tcol_i, ntcol_i=ntcol_i, wi=wi):
                    s_ld.wait(v)
                    if ch > 0 or wi > 0:
                        s_sp.wait(v)
                    v.tensor_scalar(tbuf, buf, col(tcol_i), None, ALU.is_ge)
                    v.tensor_scalar(tern2, buf, col(ntcol_i), None, ALU.is_le)
                    vgap(v)
                    s_v.inc(v.tensor_sub(tbuf, tbuf, tern2))
                for q in range(4):
                    for dt in range(DT):
                        bank = ptb[1][:, (dt % 4) * 128:(dt % 4) * 128 + 128]
                        @blk.tensor
                        def _(te, q=q, dt=dt, bank=bank):
                            s_v.wait(te)
                            s_pe.inc(te.transpose(bank, tbuf[:, q * D + dt * 128: q * D + dt * 128 + 128], idb))
                        @blk.vector
                        def _(v, q=q, dt=dt, bank=bank, wi=wi):
                            s_pe.wait(v)
                            if wi == 0:
                                dst = stg[:, dt * 512 + q * 128: dt * 512 + q * 128 + 128]
                            else:
                                dst = stg[:, (q * DT + dt) * 128: (q * DT + dt) * 128 + 128]
                            s_v.inc(v.tensor_copy(dst, bank))
                @blk.gpsimd
                def _(g, ch=ch, wi=wi):
                    s_v.wait(g)
                    if wi == 0:
                        dst = w1qT_h.rearrange("p (dt u) -> p dt u", dt=DT)[:, :, ch * 512:(ch + 1) * 512]
                        s_sp.dinc(g.dma_start(dst, stg.rearrange("p (dt u) -> p dt u", dt=DT)))
                    else:
                        dst = w2qT_h.rearrange("p (ht dd) -> p ht dd", ht=HT)[:, :, ch * 128:(ch + 1) * 128]
                        s_sp.dinc(g.dma_start(dst, stg.rearrange("p (ht j) -> p ht j", ht=HT)))

        def emit_var(v, src):
            v.reduce_sum(col(C_MEAN), src, AX.X)
            v.reduce_sum(col(C_VAR), sqb, AX.X)
            v.tensor_single_scalar(col(C_MEAN), col(C_MEAN), 1.0 / D, ALU.mult)
            v.tensor_single_scalar(col(C_VAR), col(C_VAR), 1.0 / D, ALU.mult)
            v.tensor_mul(col(C_SUM), col(C_MEAN), col(C_MEAN))
            vgap(v, 2)
            return v.tensor_sub(col(C_VAR), col(C_VAR), col(C_SUM))

        def emit_stats(src):
            @blk.scalar
            def _(s):
                s_v.wait(s)
                s_s.inc(s.activation(sqb, src, AF.Square))
            @blk.vector
            def _(v):
                s_s.wait(v)
                emit_var(v, src)
                vgap(v, 2)
                s_v.inc(v.tensor_single_scalar(col(C_VE), col(C_VAR), EPS, ALU.add))
            @blk.gpsimd
            def _(g):
                s_v.wait(g)
                emit_newton(g, col(C_VE), col(C_Y))
                s_g.inc(g.tensor_mul(col(C_NMI), col(C_MEAN), col(C_Y)))
            @blk.vector
            def _(v):
                s_g.wait(v)
                s_v.inc(v.tensor_single_scalar(col(C_NMI), col(C_NMI), -1.0, ALU.mult))

        tap("d_m", colpk)

        def emit_sqrow(slc, dstrow, pe_ready):
            _sb = s_s.n
            _pb = s_pe.n
            bank = pbk[4][0:1, 0:512]
            @blk.scalar
            def _(s):
                s.wait_ge(s_pe.h, pe_ready)
                k = 0
                for ncv in range(4):
                    for dt in range(DT):
                        sq = sqb2a if dt % 2 == 0 else sqb2b
                        if k >= 2:
                            s.wait_ge(s_pe.h, _pb + (k - 1))
                        s_s.inc(s.activation(sq, slc(dt, ncv * 512, ncv * 512 + 512), AF.Square))
                        k += 1
                    s.wait_ge(s_pe.h, _pb + (ncv + 1) * DT)
                    s_s.inc(s.activation(dstrow[0:1, ncv * 512:(ncv + 1) * 512], bank, AF.Identity,
                                         bias=0.0, scale=-0.5))
            @blk.tensor
            def _(te):
                k = 0
                for ncv in range(4):
                    for dt in range(DT):
                        sq = sqb2a if dt % 2 == 0 else sqb2b
                        te.wait_ge(s_s.h, _sb + k + 1 + ncv)
                        s_pe.inc(te.matmul(bank, onescol_b, sq, start=(dt == 0), stop=(dt == DT - 1),
                                           skip_group_check=True))
                        k += 1

        def batch(b):
            xr = x_h[b].rearrange("(t p) d -> t p d", p=128)

            @blk.sync
            def _(sync):
                s_s.wait(sync)
                s_ld.dinc(sync.dma_start(ckb[0:1, 0:D], bv_h.rearrange("d -> () d")))
                s_ld.dinc(sync.dma_start(sqb[0:1, 0:D], bo_h.rearrange("d -> () d")))
            @blk.vector
            def _(v):
                s_pe.wait(v)
                s_ld.wait(v)
                for dt in range(DT):
                    v.memset(BIGA[:, dt * BIGW: dt * BIGW + 1], 0.0)
                v.tensor_copy(bvrow, ckb[0:1, 0:D])
                v.tensor_copy(borow, sqb[0:1, 0:D])
                s_v.inc(v.memset(BIGA[:, (DT - 1) * BIGW + 2049: (DT - 1) * BIGW + 2050], 0.0))
            @blk.vector
            def _(v):
                for dt in range(DT - 1):
                    v.memset(BIGA[:, dt * BIGW + 2049: dt * BIGW + 2050], 0.0)
                s_v.inc(v.tensor_single_scalar(col(C_DMY), epscol, 1.0, ALU.mult))

            _lb0 = s_ld.n
            _sb0 = s_s.n
            _vb0 = s_v.n
            @blk.sync
            def _(sync):
                for t in range(NT):
                    xb = xrel if t % 2 == 0 else xrel2
                    if t == 0:
                        sync.wait_ge(s_s.h, _sb0)
                        sync.wait_ge(s_v.h, _vb0)
                    if t >= 2:
                        sync.wait_ge(s_s.h, _sb0 + t - 1)
                        sync.wait_ge(s_v.h, _vb0 + t - 1)
                    s_ld.dinc(sync.dma_start(xb, xr[t]))
            @blk.scalar
            def _(s):
                for t in range(NT):
                    xb = xrel if t % 2 == 0 else xrel2
                    sq = sqb if t % 2 == 0 else ckb
                    s.wait_ge(s_ld.h, _lb0 + 16 * (t + 1))
                    if t >= 2:
                        s.wait_ge(s_v.h, _vb0 + t - 1)
                    s_s.inc(s.activation(sq, xb, AF.Square))
            @blk.vector
            def _(v):
                for t in range(NT):
                    xb = xrel if t % 2 == 0 else xrel2
                    sq = sqb if t % 2 == 0 else ckb
                    v.wait_ge(s_s.h, _sb0 + t + 1)
                    v.reduce_sum(mcol16[:, t:t + 1], xb, AX.X)
                    v.reduce_sum(vcol16[:, t:t + 1], sq, AX.X)
                    s_v.inc(v.tensor_single_scalar(mcol16[:, t:t + 1], mcol16[:, t:t + 1],
                                                   1.0 / D, ALU.mult))
            @blk.vector
            def _(v):
                vgap(v, 2)
                v.tensor_single_scalar(vcol16, vcol16, 1.0 / D, ALU.mult)
                v.tensor_mul(icol16, mcol16, mcol16)
                vgap(v, 2)
                v.tensor_sub(vcol16, vcol16, icol16)
                vgap(v, 2)
                s_v.inc(v.tensor_single_scalar(vcol16, vcol16, EPS, ALU.add))
            @blk.gpsimd
            def _(g):
                s_v.wait(g)
                emit_newton(g, vcol16, icol16, sqb[:, 0:16], sqb[:, 16:32])
                s_g.inc(g.tensor_mul(mcol16, mcol16, icol16))
            @blk.vector
            def _(v):
                s_g.wait(v)
                s_v.inc(v.tensor_single_scalar(mcol16, mcol16, -1.0, ALU.mult))
            if b == 0:
                tap("d_m16", mcol16)
                tap("d_v16", vcol16)
                tap("d_i16", icol16)
            for t in range(NT):
                xb = xrel if t % 2 == 0 else xrel2
                lb = lnta if t % 2 == 0 else lntb
                @blk.sync
                def _(sync, t=t, xb=xb):
                    s_s.wait(sync)
                    s_v.wait(sync)
                    s_ld.dinc(sync.dma_start(xb, xr[t]))
                @blk.scalar
                def _(s, xb=xb, lb=lb, t=t):
                    s_ld.wait(s)
                    s_s.inc(s.activation(lb, xb, AF.Identity, bias=mcol16[:, t:t + 1],
                                         scale=icol16[:, t:t + 1]))
                _tb0 = s_pe.n
                _vb0 = s_v.n
                @blk.tensor
                def _(te, lb=lb, _vb0=_vb0):
                    s_s.wait(te)
                    te.wait_ge(s_v.h, _vb0)
                    for dt in range(DT):
                        bank = ptb[dt % 2][:, 0:128]
                        if dt >= 2:
                            te.wait_ge(s_v.h, _vb0 + dt - 1)
                        s_pe.inc(te.transpose(bank, lb[:, dt * 128: dt * 128 + 128], idb))
                @blk.vector
                def _(v, t=t, _tb0=_tb0):
                    for dt in range(DT):
                        bank = ptb[dt % 2][:, 0:128]
                        v.wait_ge(s_pe.h, _tb0 + dt + 1)
                        s_v.inc(v.tensor_copy(
                            BIGA[:, dt * BIGW + 1 + t * 128: dt * BIGW + 1 + t * 128 + 128], bank))

            def load_weight(w_src, dst, pe_guard=None):
                for dt in range(DT):
                    @blk.sync
                    def _(sync, dt=dt, w_src=w_src, pe_guard=pe_guard):
                        s_v.wait(sync)
                        if dt == 0 and pe_guard is not None:
                            sync.wait_ge(s_pe.h, pe_guard)
                        elif dt == 0:
                            s_pe.wait(sync)
                        s_ld.dinc(sync.dma_start(wstage, w_src[dt * 128:(dt + 1) * 128, :]))
                    @blk.vector
                    def _(v, dt=dt, dst=dst):
                        s_ld.wait(v)
                        s_v.inc(v.tensor_copy(dst[:, dt * D:(dt + 1) * D], wstage))

            load_weight(Wv_h, Wbf)
            _vev = {}
            @blk.tensor
            def _(te):
                s_v.wait(te)
                for mt in range(NT):
                    for ncv in range(2):
                        bank = pbk[(mt * 2 + ncv) % 4][:, 0:512]
                        k = mt * 2 + ncv
                        if k >= 4:
                            te.wait_ge(s_s.h, _vev[k - 4])
                        for dt in range(DT):
                            te.matmul(bank, BIGA[:, dt * BIGW + 1 + mt * 128: dt * BIGW + 1 + mt * 128 + 128],
                                      Wbf[:, dt * D + ncv * 512: dt * D + ncv * 512 + 512],
                                      start=(dt == 0), stop=False, skip_group_check=True)
                        s_pe.inc(te.matmul(bank, onesrow_b[0:1, 0:128], bvrow[:, ncv * 512: ncv * 512 + 512],
                                           start=False, stop=True, skip_group_check=True))
                        _vev[k] = s_s.n + k + 1
            m_vproj = s_pe.n
            @blk.scalar
            def _(s):
                base = s_pe.n - NT * 2
                for mt in range(NT):
                    for ncv in range(2):
                        k = mt * 2 + ncv
                        bank = pbk[k % 4][:, 0:512]
                        s.wait_ge(s_pe.h, base + k + 1)
                        s_s.inc(s.copy(vtok[:, mt * D + ncv * 512: mt * D + ncv * 512 + 512], bank))

            load_weight(Wk_h, Wbf2)
            _kev = {}
            _kpe0 = s_pe.n
            @blk.tensor
            def _(te):
                s_v.wait(te)
                for mt in range(DT):
                    for ncv in range(4):
                        k = mt * 4 + ncv
                        bank = pbk[k % 4][:, 0:512]
                        if k >= 4:
                            te.wait_ge(s_s.h, _kev[k - 4])
                        ii = None
                        for dt in range(DT):
                            ii = te.matmul(bank, Wbf2[:, dt * D + mt * 128: dt * D + mt * 128 + 128],
                                           BIGA[:, dt * BIGW + 1 + ncv * 512: dt * BIGW + 1 + ncv * 512 + 512],
                                           start=(dt == 0), stop=(dt == DT - 1), skip_group_check=True)
                        s_pe.inc(ii)
                        _kev[k] = s_s.n + k + 1
            m_kproj = s_pe.n
            @blk.scalar
            def _(s, _kpe0=_kpe0):
                for mt in range(DT):
                    for ncv in range(4):
                        k = mt * 4 + ncv
                        bank = pbk[k % 4][:, 0:512]
                        s.wait_ge(s_pe.h, _kpe0 + k + 1)
                        s_s.inc(s.activation(kT[:, mt * S + ncv * 512: mt * S + ncv * 512 + 512], bank,
                                             AF.Identity, bias=bkcol[:, mt:mt + 1], scale=1.0))
            emit_sqrow(lambda dt, lo, hi: kT[:, dt * S + lo: dt * S + hi], ksqrow, m_kproj)

            load_weight(Wq_h, Wbf, pe_guard=m_vproj)
            for ncv in range(4):
                @blk.tensor
                def _(te, ncv=ncv):
                    if ncv == 0:
                        s_v.wait(te)
                    s_s.wait(te)
                    ii = None
                    for mt in range(DT):
                        for dt in range(DT):
                            ii = te.matmul(pbk[mt][:, 0:512],
                                           Wbf[:, dt * D + mt * 128: dt * D + mt * 128 + 128],
                                           BIGA[:, dt * BIGW + 1 + ncv * 512: dt * BIGW + 1 + ncv * 512 + 512],
                                           start=(dt == 0), stop=(dt == DT - 1), skip_group_check=True)
                    s_pe.inc(ii)
                @blk.scalar
                def _(s, ncv=ncv):
                    s_pe.wait(s)
                    ii = None
                    for mt in range(DT):
                        ii = s.activation(BIGA[:, mt * BIGW + 1 + ncv * 512: mt * BIGW + 1 + ncv * 512 + 512],
                                          pbk[mt][:, 0:512], AF.Identity, bias=bqcol[:, mt:mt + 1], scale=1.0)
                    s_s.inc(ii)
            m_qproj = s_pe.n
            emit_sqrow(lambda dt, lo, hi: BIGA[:, dt * BIGW + 1 + lo: dt * BIGW + 1 + hi],
                       qsqrow, m_qproj)

            load_weight(Wo_h, Wbf2, pe_guard=m_kproj)
            if b == 0:
                tap("d_kt", kT)
                tap("d_vt", vtok)
                tap("d_q", BIGA)
                tap("d_qsq", qsqrow)
                tap("d_ksq", ksqrow)
                tap("d_wbf", Wbf)

            m_qk = {}
            m_exp = {}
            m_ev = {}
            m_evac = {}
            m_ckb = {}
            for c in range(NCH):
                opsum = [pbk[2 + i // 2][:, (i % 2) * CH:(i % 2) * CH + CH] for i in range(DT)]
                dn = pbk[6][0:1, 0:CH]
                _n = s_pe.n
                for step in range(NT + 2):
                    if step < NT:
                        _n += 1
                        m_qk[(c, step)] = _n
                    if step >= 2:
                        _n += 1
                        m_ev[(c, step - 2)] = _n
                for jt in range(NT):
                    m_exp[(c, jt)] = s_s.n + jt + 1
                EBS = [eba, ebb, ebc]
                QKB = [0, 1, 7]
                @blk.tensor
                def _(te, c=c):
                    for step in range(NT + 2):
                        jt = step
                        if jt < NT:
                            qk = pbk[QKB[jt % 3]][:, 0:CH]
                            prev = (c, jt - 3) if jt >= 3 else (c - 1, NT + jt - 3)
                            if prev in m_exp and (prev[0] >= 0):
                                te.wait_ge(s_s.h, m_exp[prev])
                            if jt == 0 and c > 0:
                                te.wait_ge(s_v.h, m_evac[c - 1])
                                te.wait_ge(s_s.h, m_ckb[c - 1])
                            for dt in range(DT):
                                te.matmul(qk, kT[:, dt * S + jt * 128: dt * S + jt * 128 + 128],
                                          BIGA[:, dt * BIGW + 1 + c * CH: dt * BIGW + 1 + c * CH + CH],
                                          start=(dt == 0), stop=False, skip_group_check=True)
                            te.matmul(qk, onesrow_b[0:1, 0:128], qsqrow[0:1, c * CH: c * CH + CH],
                                      start=False, stop=False, skip_group_check=True)
                            s_pe.inc(te.matmul(qk, ksqrow[0:1, jt * 128: jt * 128 + 128],
                                               onesrow_b[0:1, 0:CH],
                                               start=False, stop=True, skip_group_check=True))
                            assert s_pe.n == m_qk[(c, jt)], (s_pe.n, m_qk[(c, jt)])
                        if step >= 2:
                            jt1 = step - 2
                            eb1 = EBS[jt1 % 3]
                            te.wait_ge(s_s.h, m_exp[(c, jt1)])
                            for i in range(DT):
                                te.matmul(opsum[i], vtok[:, jt1 * D + i * 128: jt1 * D + i * 128 + 128],
                                          eb1, start=(jt1 == 0 and i % 2 == 0), stop=(jt1 == NT - 1),
                                          skip_group_check=True)
                            s_pe.inc(te.matmul(dn, onescol_b, eb1, start=(jt1 == 0), stop=(jt1 == NT - 1),
                                               skip_group_check=True))
                            assert s_pe.n == m_ev[(c, jt1)], (s_pe.n, m_ev[(c, jt1)])
                @blk.scalar
                def _(s, c=c):
                    for jt in range(NT):
                        qk = pbk[QKB[jt % 3]][:, 0:CH]
                        eb_ = EBS[jt % 3]
                        s.wait_ge(s_pe.h, m_qk[(c, jt)])
                        prev = (c, jt - 3) if jt >= 3 else (c - 1, NT + jt - 3)
                        if prev in m_ev and prev[0] >= 0:
                            s.wait_ge(s_pe.h, m_ev[prev])
                        s.activation(sba, qk, AF.Exp, bias=0.0, scale=2.0)
                        s_s.inc(s.activation(eb_, sba, AF.Exp, bias=0.0, scale=1.0))
                        assert s_s.n == m_exp[(c, jt)]
                if b == 0 and c == 0:
                    tap("d_sb", sba)
                    tap("d_eb", ebb)
                @blk.vector
                def _(v):
                    s_pe.wait(v)
                    v.reciprocal(rowf[0:1, 0:CH], dn)
                    vgap(v)
                    vgap(v)
                    s_v.inc(v.tensor_copy(rowb[0:1, 0:CH], rowf[0:1, 0:CH]))
                @blk.tensor
                def _(te):
                    s_v.wait(te)
                    s_pe.inc(te.matmul(pbk[6][:, CH:CH + CH], onesrow_b[0:1, 0:128], rowb[0:1, 0:CH],
                                       start=True, stop=True, skip_group_check=True))
                @blk.vector
                def _(v, c=c):
                    s_pe.wait(v)
                    v.tensor_copy(rB, pbk[6][:, CH:CH + CH])
                    vgap(v)
                    vgap(v)
                    ii = None
                    for i in range(DT):
                        ii = v.tensor_mul(outn[:, i * CH:(i + 1) * CH], opsum[i], rB)
                    s_v.inc(ii)
                    m_evac[c] = s_v.mark()
                if b == 0 and c == 0:
                    tap("d_outn", outn)
                    tap("d_rb", rB)
                    tap("d_rowf", rowf)
                for imt in range(2):
                    aop = [pbk[2 + 2 * imt + ncv][:, 0:512] for ncv in range(2)]
                    @blk.tensor
                    def _(te, imt=imt, aop=aop):
                        s_v.wait(te)
                        ii = None
                        for ncv in range(2):
                            for i in range(DT):
                                te.matmul(aop[ncv], outn[:, i * CH + imt * 128: i * CH + imt * 128 + 128],
                                          Wbf2[:, i * D + ncv * 512: i * D + ncv * 512 + 512],
                                          start=(i == 0), stop=False, skip_group_check=True)
                            ii = te.matmul(aop[ncv], onesrow_b[0:1, 0:128], borow[:, ncv * 512: ncv * 512 + 512],
                                           start=False, stop=True, skip_group_check=True)
                        s_pe.inc(ii)
                for imt in range(2):
                    aop = [pbk[2 + 2 * imt + ncv][:, 0:512] for ncv in range(2)]
                    t_glob = c * 2 + imt
                    xb = xrel if t_glob % 2 == 0 else xrel2
                    lnt = lnta if t_glob % 2 == 0 else lntb
                    sb0 = s_s.n
                    vb0 = s_v.n
                    gb0 = s_g.n
                    ldm = s_ld.n + 16
                    @blk.sync
                    def _(sync, t_glob=t_glob, xb=xb):
                        sync.wait_ge(s_sp.h, s_sp.n)
                        sync.wait_ge(s_v.h, s_v.n)
                        s_ld.dinc(sync.dma_start(xb, xr[t_glob]))
                    @blk.scalar
                    def _(s, aop=aop, xb=xb, sb0=sb0, vb0=vb0):
                        s_pe.wait(s)
                        s.wait_ge(s_v.h, vb0)
                        s.copy(ckb[:, 0:512], aop[0])
                        s_s.inc(s.copy(ckb[:, 512:1024], aop[1]))
                        s_s.inc(s.activation(sqb, ckb, AF.Square))
                        s.wait_ge(s_v.h, vb0 + 2)
                        s_s.inc(s.activation(lnbuf, ckb, AF.Identity, bias=col(C_NMI), scale=col(C_Y)))
                        s.wait_ge(s_v.h, vb0 + 3)
                        s_s.inc(s.activation(sqb, xb, AF.Square))
                        s.wait_ge(s_v.h, vb0 + 5)
                        s_s.inc(s.activation(lnbuf, xb, AF.Identity, bias=col(C_NMI), scale=col(C_Y)))
                    m_ckb[c] = sb0 + 1
                    @blk.vector
                    def _(v, xb=xb, sb0=sb0, gb0=gb0, ldm=ldm):
                        v.wait_ge(s_s.h, sb0 + 2)
                        emit_var(v, ckb)
                        vgap(v, 2)
                        s_v.inc(v.tensor_single_scalar(col(C_VE), col(C_VAR), EPS, ALU.add))
                        v.wait_ge(s_g.h, gb0 + 1)
                        s_v.inc(v.tensor_single_scalar(col(C_NMI), col(C_NMI), -1.0, ALU.mult))
                        v.wait_ge(s_s.h, sb0 + 3)
                        v.wait_ge(s_ld.h, ldm)
                        s_v.inc(v.tensor_add(xb, xb, lnbuf))
                        v.wait_ge(s_s.h, sb0 + 4)
                        emit_var(v, xb)
                        vgap(v, 2)
                        s_v.inc(v.tensor_single_scalar(col(C_VE), col(C_VAR), EPS, ALU.add))
                        v.wait_ge(s_g.h, gb0 + 2)
                        s_v.inc(v.tensor_single_scalar(col(C_NMI), col(C_NMI), -1.0, ALU.mult))
                    @blk.gpsimd
                    def _(g, t_glob=t_glob, xb=xb, vb0=vb0):
                        g.wait_ge(s_v.h, vb0 + 1)
                        emit_newton(g, col(C_VE), col(C_Y))
                        s_g.inc(g.tensor_mul(col(C_NMI), col(C_MEAN), col(C_Y)))
                        g.wait_ge(s_v.h, vb0 + 3)
                        s_sp.dinc(g.dma_start(xsp_h.rearrange("(t p) d -> t p d", p=128)[t_glob], xb))
                        g.wait_ge(s_v.h, vb0 + 4)
                        emit_newton(g, col(C_VE), col(C_Y))
                        s_g.inc(g.tensor_mul(col(C_NMI), col(C_MEAN), col(C_Y)))
                    @blk.vector
                    def _(v, t_glob=t_glob, sb0=sb0):
                        v.wait_ge(s_s.h, sb0 + 5)
                        v.reduce_max(col(C_AMX), lnbuf, AX.X, apply_absolute_value=True)
                        vgap(v, 2)
                        v.tensor_single_scalar(col(C_AMX), col(C_AMX), EPS, ALU.max)
                        vgap(v, 2)
                        v.reciprocal(col(C_C2), col(C_AMX))
                        v.tensor_mul(c1col[:, t_glob:t_glob + 1], col(C_AMX), col(C_M127_1))
                        vgap(v)
                        v.tensor_single_scalar(col(C_R127), col(C_C2), 127.0, ALU.mult)
                        vgap(v, 2)
                        v.tensor_scalar(sqb, lnbuf, col(C_R127), 128.0, ALU.mult, ALU.add)
                        vgap(v)
                        v.tensor_copy(q_i32, sqb)
                        vgap(v)
                        v.tensor_copy(q1c, q_i32)
                        vgap(v)
                        v.scalar_tensor_tensor(ckb, sqb, 0.5, q1c, ALU.add, ALU.subtract)
                        vgap(v)
                        v.tensor_single_scalar(ckb, ckb, 1.0, ALU.is_ge)
                        vgap(v)
                        v.scalar_tensor_tensor(q1c, q1c, -128.0, ckb, ALU.add, ALU.add)
                        vgap(v)
                        s_v.inc(v.tensor_scalar(lnt, q1c, c1col[:, t_glob:t_glob + 1], None, ALU.mult))
                    @blk.gpsimd
                    def _(g, t_glob=t_glob, lnt=lnt):
                        s_v.wait(g)
                        s_sp.dinc(g.dma_start(xq_h.rearrange("(t p) d -> t p d", p=128)[t_glob], lnt))

            @blk.sync
            def _(sync):
                s_pe.wait(sync)
                s_sp.wait(sync)
                s_s.wait(sync)
                s_ld.dinc(sync.dma_start(w1qT, w1qT_h))
                s_ld.dinc(sync.dma_start(w2qT, w2qT_h))
            for t in range(NT):
                lb = lnta if t % 2 == 0 else lntb
                @blk.sync
                def _(sync, t=t, lb=lb):
                    s_v.wait(sync)
                    s_ld.dinc(sync.dma_start(lb, xq_h.rearrange("(t p) d -> t p d", p=128)[t]))
                _tb0 = s_pe.n
                _vb0 = s_v.n
                @blk.tensor
                def _(te, lb=lb, _vb0=_vb0):
                    s_ld.wait(te)
                    te.wait_ge(s_v.h, _vb0)
                    for dt in range(DT):
                        bank = ptb[dt % 2][:, 0:128]
                        if dt >= 2:
                            te.wait_ge(s_v.h, _vb0 + dt - 1)
                        s_pe.inc(te.transpose(bank, lb[:, dt * 128: dt * 128 + 128], idb))
                @blk.vector
                def _(v, t=t, _tb0=_tb0):
                    for dt in range(DT):
                        bank = ptb[dt % 2][:, 0:128]
                        v.wait_ge(s_pe.h, _tb0 + dt + 1)
                        s_v.inc(v.tensor_copy(
                            BIGA[:, dt * BIGW + 1 + t * 128: dt * BIGW + 1 + t * 128 + 128], bank))

            if b == 0:
                tap("d_xqt", BIGA)
                tap("d_w1", w1qT)
                tap("d_w2", w2qT)

            m_u2 = {}
            m_silu = {}
            m_hmm = {}
            for c in range(NFC):
                _vb = s_v.n
                for s_i in range(HT + 2):
                    if s_i - 2 >= 0:
                        m_u2[(c, s_i - 2)] = _vb + s_i + 1
                _sb = s_s.n
                for ht in range(HT):
                    m_silu[(c, ht)] = _sb + ht + 1
                @blk.tensor
                def _(te, c=c):
                    if c == 0:
                        te.wait_ge(s_ld.h, s_ld.n)
                        te.wait_ge(s_v.h, _vb)
                        te.wait_ge(s_s.h, _sb)
                    for ht in range(HT):
                        hb = pbk[ht % 4][:, 0:130]
                        prev = (c, ht - 4) if ht >= 4 else (c - 1, HT + ht - 4)
                        if prev in m_u2 and prev[0] >= 0:
                            te.wait_ge(s_v.h, m_u2[prev])
                        ii = None
                        for dt in range(DT):
                            ii = te.matmul(hb, w1qT[:, dt * H + ht * 128: dt * H + ht * 128 + 128],
                                           BIGA[:, dt * BIGW + c * FC: dt * BIGW + c * FC + 130],
                                           start=(dt == 0), stop=(dt == DT - 1), skip_group_check=True)
                        s_pe.inc(ii)
                        m_hmm[(c, ht)] = s_pe.n
                @blk.vector
                def _(v, c=c, _vb=_vb):
                    for s_i in range(HT + 2):
                        ht0, ht1, ht2 = s_i, s_i - 1, s_i - 2
                        did = None
                        if ht0 < HT:
                            v.wait_ge(s_pe.h, m_hmm[(c, ht0)])
                            hb = pbk[ht0 % 4][:, 0:130]
                            did = v.tensor_scalar(cva if ht0 % 2 == 0 else cvb, hb[:, 0:128],
                                                  cwcol[:, ht0 * 3: ht0 * 3 + 1], None, ALU.mult)
                        if 0 <= ht1 < HT:
                            hb = pbk[ht1 % 4][:, 0:130]
                            did = v.scalar_tensor_tensor(cvc if ht1 % 2 == 0 else cvd, hb[:, 1:129],
                                                         cwcol[:, ht1 * 3 + 1: ht1 * 3 + 2],
                                                         cva if ht1 % 2 == 0 else cvb,
                                                         ALU.mult, ALU.add)
                        if 0 <= ht2 < HT:
                            if (c, ht2 - 2) in m_silu:
                                v.wait_ge(s_s.h, m_silu[(c, ht2 - 2)])
                            hb = pbk[ht2 % 4][:, 0:130]
                            did = v.scalar_tensor_tensor(cve if ht2 % 2 == 0 else cvf, hb[:, 2:130],
                                                         cwcol[:, ht2 * 3 + 2: ht2 * 3 + 3],
                                                         cvc if ht2 % 2 == 0 else cvd,
                                                         ALU.mult, ALU.add)
                        s_v.inc(did)
                        assert s_v.n == _vb + s_i + 1
                @blk.scalar
                def _(s, c=c):
                    for ht in range(HT):
                        s.wait_ge(s_v.h, m_u2[(c, ht)])
                        s_s.inc(s.activation(YS[:, ht * FC:(ht + 1) * FC],
                                             cve if ht % 2 == 0 else cvf, AF.Silu,
                                             bias=cbcol[:, ht:ht + 1], scale=1.0))
                        assert s_s.n == m_silu[(c, ht)]
                @blk.vector
                def _(v):
                    s_s.wait(v)
                    s_v.inc(v.reduce_max(cva, YS.rearrange("p (ht i) -> p i ht", ht=HT), AX.X,
                                         apply_absolute_value=True))
                @blk.tensor
                def _(te):
                    s_v.wait(te)
                    s_pe.inc(te.transpose(pbk[6][:, 0:128], cva, idf))
                @blk.vector
                def _(v):
                    s_pe.wait(v)
                    v.reduce_max(col(C_AMX), pbk[6][:, 0:128], AX.X)
                    vgap(v)
                    vgap(v)
                    v.tensor_single_scalar(col(C_AMX), col(C_AMX), EPS, ALU.max)
                    vgap(v)
                    vgap(v)
                    v.reciprocal(col(C_R127), col(C_AMX))
                    v.tensor_mul(col(C_C2), col(C_AMX), col(C_M127_2))
                    vgap(v)
                    s_v.inc(v.tensor_single_scalar(col(C_R127), col(C_R127), 127.0, ALU.mult))
                @blk.tensor
                def _(te):
                    s_v.wait(te)
                    s_pe.inc(te.matmul(pbk[6][0:1, 128:256], col(C_R127), idf, start=True, stop=True,
                                       skip_group_check=True))
                @blk.scalar
                def _(s):
                    s_pe.wait(s)
                    s_s.inc(s.copy(rowf[0:1, 0:FC], pbk[6][0:1, 128:256]))
                @blk.tensor
                def _(te):
                    s_s.wait(te)
                    s_pe.inc(te.matmul(pbk[6][:, 256:384], onesrow_f, rowf[0:1, 0:FC],
                                       start=True, stop=True, skip_group_check=True))
                @blk.vector
                def _(v):
                    s_pe.wait(v)
                    s_v.inc(v.tensor_copy(r2B, pbk[6][:, 256:384]))
                for qq in range(4):
                    sl = slice(qq * 8 * FC, (qq + 1) * 8 * FC)
                    @blk.vector
                    def _(v, sl=sl, c=c, qq=qq):
                        s_v.wait(v)
                        if qq == 0:
                            s_s.wait(v)
                            v.wait_ge(s_out.h, s_out.n)
                            if c > 0:
                                s_pe.wait(v)
                        ysq = YS[:, sl]
                        vgap(v)
                        v.tensor_tensor(ysq.rearrange("p (a i) -> p a i", a=8),
                                        ysq.rearrange("p (a i) -> p a i", a=8),
                                        r2B.rearrange("p i -> p () i").broadcast_to([128, 8, FC]),
                                        ALU.mult)
                        vgap(v)
                        v.tensor_single_scalar(ysq, ysq, 128.0, ALU.add)
                        vgap(v)
                        v.tensor_copy(q_i32, ysq)
                        vgap(v)
                        v.tensor_copy(sqb, q_i32)
                        vgap(v)
                        v.scalar_tensor_tensor(ysq, ysq, 0.5, sqb, ALU.add, ALU.subtract)
                        vgap(v)
                        v.tensor_single_scalar(ysq, ysq, 1.0, ALU.is_ge)
                        vgap(v)
                        s_v.inc(v.scalar_tensor_tensor(hqb[:, sl], sqb, -128.0, ysq, ALU.add, ALU.add))
                for ncv in range(2):
                    bank = pbk[4 + ncv][:, 0:512]
                    @blk.tensor
                    def _(te, c=c, ncv=ncv, bank=bank):
                        s_v.wait(te)
                        s_s.wait(te)
                        ii = None
                        for ht in range(HT):
                            ii = te.matmul(bank, hqb[:, ht * FC: ht * FC + 128],
                                           w2qT[:, ht * D + ncv * 512: ht * D + ncv * 512 + 512],
                                           start=(ht == 0), stop=(ht == HT - 1), skip_group_check=True)
                        s_pe.inc(ii)
                m_w2 = s_pe.n
                @blk.sync
                def _(sync, c=c):
                    s_v.wait(sync)
                    s_out.wait(sync, s_out.n)
                    s_ld.dinc(sync.dma_start(xrel, xsp_h.rearrange("(t p) d -> t p d", p=128)[c]))
                _sb2 = s_s.n
                _vb2 = s_v.n
                _outb = s_out.n
                @blk.scalar
                def _(s, c=c, m_w2=m_w2, _vb2=_vb2, _outb=_outb):
                    s.wait_ge(s_pe.h, m_w2)
                    s_s.inc(s.activation(ckb[:, 0:512], pbk[4][:, 0:512], AF.Identity,
                                         bias=0.0, scale=col(C_C2)))
                    s_s.inc(s.activation(ckb[:, 512:1024], pbk[5][:, 0:512], AF.Identity,
                                         bias=0.0, scale=col(C_C2)))
                    s_s.inc(s.activation(sqb, ckb, AF.Square))
                    s.wait_ge(s_v.h, _vb2 + 2)
                    s.wait_ge(s_out.h, _outb)
                    s_s.inc(s.activation(lnbuf, ckb, AF.Identity, bias=col(C_NMI), scale=col(C_Y)))
                m_sq = _sb2 + 3
                m_lnfo = _sb2 + 4
                @blk.vector
                def _(v, m_sq=m_sq):
                    v.wait_ge(s_s.h, m_sq)
                    emit_var(v, ckb)
                    vgap(v, 2)
                    s_v.inc(v.tensor_single_scalar(col(C_VE), col(C_VAR), EPS, ALU.add))
                @blk.gpsimd
                def _(g):
                    s_v.wait(g)
                    emit_newton(g, col(C_VE), col(C_Y))
                    s_g.inc(g.tensor_mul(col(C_NMI), col(C_MEAN), col(C_Y)))
                @blk.vector
                def _(v):
                    s_g.wait(v)
                    s_v.inc(v.tensor_single_scalar(col(C_NMI), col(C_NMI), -1.0, ALU.mult))
                @blk.vector
                def _(v, m_lnfo=m_lnfo):
                    v.wait_ge(s_s.h, m_lnfo)
                    s_ld.wait(v)
                    s_v.inc(v.tensor_add(lnbuf, lnbuf, xrel))
                @blk.gpsimd
                def _(g, b=b, c=c):
                    s_v.wait(g)
                    s_out.dinc(g.dma_start(y_h[b].rearrange("(t p) d -> t p d", p=128)[c], lnbuf))

        for b in range(NB):
            batch(b)

        @blk.gpsimd
        def _(g):
            g.wait_ge(s_out.h, s_out.n)

    return nc


def kernel(L0_attn_x, L0_attn_Wq, L0_attn_bq, L0_attn_Wk, L0_attn_bk, L0_attn_Wv,
           L0_attn_bv, L0_attn_Wo, L0_attn_bo, L0_ffn_w1, L0_ffn_conv_w,
           L0_ffn_conv_b, L0_ffn_w2):
    x = np.ascontiguousarray(np.asarray(L0_attn_x, dtype=np.float32))
    assert x.shape == (32, S, D)
    base = dict(
        Wq=np.ascontiguousarray(L0_attn_Wq, dtype=np.float32),
        Wk=np.ascontiguousarray(L0_attn_Wk, dtype=np.float32),
        Wv=np.ascontiguousarray(L0_attn_Wv, dtype=np.float32),
        Wo=np.ascontiguousarray(L0_attn_Wo, dtype=np.float32),
        bq=np.ascontiguousarray(L0_attn_bq, dtype=np.float32),
        bk=np.ascontiguousarray(L0_attn_bk, dtype=np.float32),
        bv=np.ascontiguousarray(L0_attn_bv, dtype=np.float32),
        bo=np.ascontiguousarray(L0_attn_bo, dtype=np.float32),
        w1=np.ascontiguousarray(L0_ffn_w1, dtype=np.float32),
        cw=np.ascontiguousarray(np.asarray(L0_ffn_conv_w, dtype=np.float32).reshape(H, 3)),
        cb=np.ascontiguousarray(L0_ffn_conv_b, dtype=np.float32),
        w2=np.ascontiguousarray(L0_ffn_w2, dtype=np.float32),
        idf=np.eye(128, dtype=np.float32),
        idb=np.eye(128, dtype=ml_dtypes.bfloat16),
    )
    nc = build_nc()
    in_maps = [dict(base, x=np.ascontiguousarray(x[4 * c:4 * c + 4])) for c in range(8)]
    res = run_bass_kernel_spmd(nc, in_maps, core_ids=list(range(8)))
    return np.concatenate([r["y"] for r in res.results], axis=0)
